# version 11
# baseline (speedup 1.0000x reference)
import numpy as np
import ml_dtypes
import concourse.bass as bass
import concourse.mybir as mybir
from concourse.bass_utils import run_bass_kernel_spmd

F32 = mybir.dt.float32
BF16 = mybir.dt.bfloat16
AF = mybir.ActivationFunctionType
ALU = mybir.AluOpType
AX = mybir.AxisListType
bf = ml_dtypes.bfloat16

D = 2048
DFF = 8192
T = 1024
NCH = 16
PAIRS = 16
SCALE = float(1.0 / np.sqrt(32.0))
BIG = float(2.0 ** 63)
HALFBIG = float(0.5 * 2.0 ** 63)
NMAT = float(DFF * D)
GROUPS_KV = [[0, 1, 2, 3], [4, 5, 6, 7]]
GROUP_ALL = [[0, 1, 2, 3, 4, 5, 6, 7]]

DEBUG = False

SEM = {}
NSQ = 8
NGQ = 4
SQ = tuple(f"sq{i}" for i in range(NSQ))
GQ = tuple(f"gq{i}" for i in range(NGQ))
SEMNAMES = SQ + GQ + ("mm", "act", "vec", "gp",
                      "ccr", "cck", "ccv", "cw1", "cw2")


class Plan:

    ENGS = ("sync", "gpsimd", "tensor", "vector", "scalar")

    def __init__(self):
        self.items = {e: [] for e in self.ENGS}
        self.ctr = {s: 0 for s in SEMNAMES}
        self.rr = {"sync": 0, "gpsimd": 0}

    def add(self, eng, fn):
        self.items[eng].append(fn)

    def wait(self, eng, sem, val=None):
        if val is None:
            tok = sem
            if tok is None:
                return
            if isinstance(tok, list):
                best = {}
                for s, v in tok:
                    if s is not None and v > best.get(s, 0):
                        best[s] = v
                for s, v in best.items():
                    self.add(eng, lambda e, s=s, v=v: e.wait_ge(SEM[s], v))
                return
            sem, val = tok
        if sem is None or val is None or val <= 0:
            return
        self.add(eng, lambda e, s=sem, v=val: e.wait_ge(SEM[s], v))

    def dma(self, eng, out_ap, in_ap, **kw):
        pool = SQ if eng == "sync" else GQ
        s = pool[self.rr[eng] % len(pool)]
        self.rr[eng] += 1
        self.ctr[s] += 16
        v = self.ctr[s]
        self.add(eng, lambda e, o=out_ap, i=in_ap, kw=dict(kw), s=s: e.dma_start(
            out=o, in_=i, **kw).then_inc(SEM[s], 16))
        return (s, v)

    def op(self, eng, fn, inc=None):
        if inc is None:
            self.add(eng, fn)
            return None
        self.ctr[inc] += 1
        v = self.ctr[inc]
        self.add(eng, lambda e, f=fn, s=inc: f(e).then_inc(SEM[s], 1))
        return v

    def collective(self, sem, kind, op_, groups, in_ap, out_ap):
        self.ctr[sem] += 1
        v = self.ctr[sem]
        self.add("gpsimd", lambda e, k=kind, o=op_, g=groups, i=in_ap, oo=out_ap,
                 s=sem: e.collective_compute(
                     k, o, replica_groups=g, ins=[i.opt()],
                     outs=[oo.opt()]).then_inc(SEM[s], 1))
        return v


def build_program():
    nc = bass.Bass("TRN2", target_bir_lowering=False)

    x_d = nc.dram_tensor("x", [T, D], F32, kind="ExternalInput").ap()
    wq_d = nc.dram_tensor("wq", [D, D], F32, kind="ExternalInput").ap()
    wk_d = nc.dram_tensor("wk", [D, D], F32, kind="ExternalInput").ap()
    wv_d = nc.dram_tensor("wv", [D, D], F32, kind="ExternalInput").ap()
    wo_d = nc.dram_tensor("wo", [D, D], F32, kind="ExternalInput").ap()
    bq_d = nc.dram_tensor("bq", [128, 16], F32, kind="ExternalInput").ap()
    bk_d = nc.dram_tensor("bk", [128, 16], F32, kind="ExternalInput").ap()
    bo_d = nc.dram_tensor("bo", [128, 16], F32, kind="ExternalInput").ap()
    bv_row_d = nc.dram_tensor("bv_row", [1, D], F32, kind="ExternalInput").ap()
    bo_row_d = nc.dram_tensor("bo_row", [1, D], F32, kind="ExternalInput").ap()
    b1_d = nc.dram_tensor("b1", [128, 64], F32, kind="ExternalInput").ap()
    b2_d = nc.dram_tensor("b2", [128, 16], F32, kind="ExternalInput").ap()
    w1s_d = nc.dram_tensor("w1s", [DFF, 256], F32, kind="ExternalInput").ap()
    w2s_d = nc.dram_tensor("w2s", [D, 1024], F32, kind="ExternalInput").ap()
    out_d = nc.dram_tensor("out", [T, D], F32, kind="ExternalOutput").ap()
    if DEBUG:
        o_xn = nc.dram_tensor("o_xn", [128, NCH * T], BF16,
                              kind="ExternalOutput").ap()
        o_st = nc.dram_tensor("o_st", [D, T], BF16,
                              kind="ExternalOutput").ap()
        o_fe = nc.dram_tensor("o_fe", [128, NCH * T], BF16,
                              kind="ExternalOutput").ap()
        o_h = nc.dram_tensor("o_h", [128, 32768], BF16,
                             kind="ExternalOutput").ap()
        o_kg = nc.dram_tensor("o_kg", [4 * D, T], BF16,
                              kind="ExternalOutput").ap()
        o_vg = nc.dram_tensor("o_vg", [4 * T, 2080], BF16,
                              kind="ExternalOutput").ap()
        o_w1t = nc.dram_tensor("o_w1t", [D, DFF], BF16,
                               kind="ExternalOutput").ap()
        o_w2t = nc.dram_tensor("o_w2t", [DFF, D], BF16,
                               kind="ExternalOutput").ap()
        o_sum = nc.dram_tensor("o_sum", [1, 2], F32, kind="ExternalOutput").ap()
        o_q = nc.dram_tensor("o_q", [D, T], BF16,
                             kind="ExternalOutput").ap()

    d_k_in = nc.dram_tensor("d_k_in", [D, T], BF16).ap()
    d_k = nc.dram_tensor("d_k", [4 * D, T], BF16).ap()
    d_v_in = nc.dram_tensor("d_v_in", [T, 2080], BF16).ap()
    d_v = nc.dram_tensor("d_v", [4 * T, 2080], BF16).ap()
    d_w1t_in = nc.dram_tensor("d_w1t_in", [256, DFF], BF16).ap()
    d_w1t = nc.dram_tensor("d_w1t", [D, DFF], BF16).ap()
    d_w2t_in = nc.dram_tensor("d_w2t_in", [1024, D], BF16).ap()
    d_w2t = nc.dram_tensor("d_w2t", [DFF, D], BF16).ap()
    d_q = nc.dram_tensor("d_q", [D, T], BF16).ap()
    d_s = nc.dram_tensor("d_s", [D, T], BF16).ap()
    d_sum_in = nc.dram_tensor("d_sum_in", [1, 2], F32).ap()
    d_sum = nc.dram_tensor("d_sum", [1, 2], F32).ap()

    from contextlib import ExitStack
    ctx = ExitStack()
    with ctx:
        def sbt(name, shape, dt):
            return ctx.enter_context(nc.sbuf_tensor(name, shape, dt))

        def pst(name, shape, dt):
            return ctx.enter_context(nc.psum_tensor(name, shape, dt))

        xnbuf = sbt("xnbuf", [128, NCH * T], BF16)
        q2 = sbt("q2", [128, 2048], BF16)
        hbuf = sbt("hbuf", [128, 60480], BF16)
        wbufA = sbt("wbufA", [128, 6144], BF16)
        wbufB = sbt("wbufB", [128, 6144], BF16)
        fa = sbt("fa", [128, 2048], F32)
        fb = sbt("fb", [128, 2048], F32)


        s_bq = sbt("s_bq", [128, 16], F32)
        s_bk = sbt("s_bk", [128, 16], F32)
        s_bo = sbt("s_bo", [128, 16], F32)
        s_b1 = sbt("s_b1", [128, 64], F32)
        s_b2 = sbt("s_b2", [128, 16], F32)
        ident = sbt("ident", [128, 128], BF16)
        bvrow = sbt("bvrow", [1, 2048], BF16)
        ones_t = sbt("ones_t", [1, 128], BF16)
        ones1 = sbt("ones1", [1, 128], F32)
        ones128 = sbt("ones128", [128, 1], F32)
        ones64f = sbt("ones64f", [1, 64], F32)
        stat = sbt("stat", [128, 16], F32)
        epst = sbt("epst", [128, 1], F32)
        acc = sbt("acc", [128, 2], F32)
        acct = sbt("acct", [128, 1], F32)
        ssmall = sbt("ssmall", [1, 8], F32)
        inv_bc = sbt("inv_bc", [128, 4], F32)
        recv = wbufB[:, 4096:6144].bitcast(F32)
        recA = recv[0:1, 0:512]
        recB = recv[0:1, 512:1024]
        bcbufA = fa[0:64, 0:512]
        bcbufB = fa[0:64, 512:1024]

        H_XN = 0
        H_KT = 0
        H_VST = 2048
        H_TU, H_TV, H_TT = 0, 2048, 4096
        H_WST = 6144
        H_H = 0
        H_KB = 32768
        H_VB = 36864
        H_EA = 41024
        H_FE = 44096
        H_PBF = 44096

        P_a = pst("P_a", [128, 1024], F32)
        P_b = pst("P_b", [128, 1024], F32)
        P_c = pst("P_c", [128, 512], F32)
        P_d = pst("P_d", [128, 512], F32)
        P_f = pst("P_f", [128, 1024], BF16)
        P_h = pst("P_h", [128, 512], F32)

        SEM.clear()
        for s in SEMNAMES:
            SEM[s] = ctx.enter_context(nc.semaphore(s))

        P = Plan()
        C = P.ctr

        P.op("gpsimd", lambda e: e.memset(ident[:, :], 0.0))
        v_id = P.op("gpsimd", lambda e: e.affine_select(
            out=ident[:, :], in_=ident[:, :], compare_op=ALU.not_equal,
            fill=1.0, base=0, pattern=[[-1, 128]], channel_multiplier=1),
            inc="gp")
        P.op("gpsimd", lambda e: e.memset(ones1[:, :], 1.0))
        P.op("gpsimd", lambda e: e.memset(ones_t[:, :], 1.0))
        v_bvrow = P.dma("gpsimd", bvrow[:, :], bv_row_d[:, :])
        P.op("gpsimd", lambda e: e.memset(ones128[:, :], 1.0))
        P.op("gpsimd", lambda e: e.memset(ones64f[:, :], 1.0))
        v_ones = P.op("gpsimd", lambda e: e.memset(epst[:, :], 1e-5), inc="gp")

        toks_bias = [P.dma("sync", s_bq[:, :], bq_d[:, :]),
                     P.dma("sync", s_bk[:, :], bk_d[:, :]),
                     P.dma("sync", s_bo[:, :], bo_d[:, :]),
                     P.dma("sync", s_b1[:, :], b1_d[:, :]),
                     P.dma("sync", s_b2[:, :], b2_d[:, :])]

        v_acc0 = P.op("vector", lambda e: e.memset(acc[:, :], 0.0), inc="vec")
        fa_free = ("vec", 0)
        fb_free = ("vec", 0)
        for which, src_ap, ntile, ng, tw in ((0, w1s_d, 8, 8, 256),
                                             (1, w2s_d, 8, 2, 1024)):
            rows = ng * 128
            for k in range(ntile):
                buf, free = (fa, fa_free) if k % 2 == 0 else (fb, fb_free)
                P.wait("sync", free[0], free[1])
                vl = P.dma("sync", buf[:, :].rearrange("p (g c) -> p g c", c=tw),
                           src_ap[k * rows:(k + 1) * rows, :].rearrange(
                               "(g p) c -> p g c", p=128))
                P.wait("vector", vl)
                P.op("vector", lambda e, buf=buf: e.reduce_sum(
                    acct[:, 0:1], buf[:, :], axis=AX.X,
                    apply_absolute_value=True), inc="vec")
                if k % 2 == 0:
                    fa_free = ("vec", C["vec"])
                else:
                    fb_free = ("vec", C["vec"])
                P.wait("vector", "vec", C["vec"])
                P.op("vector", lambda e, w=which: e.tensor_add(
                    acc[:, w:w + 1], acc[:, w:w + 1], acct[:, 0:1]), inc="vec")
        v_abs_done = C["vec"]
        P.wait("tensor", "vec", v_abs_done)
        P.wait("tensor", "gp", v_ones)
        v_sum_mm = P.op("tensor", lambda e: e.matmul(
            P_c[0:1, 0:2], ones128[:, 0:1], acc[:, :], start=True, stop=True),
            inc="mm")
        P.wait("vector", "mm", v_sum_mm)
        v_sum_cp = P.op("vector", lambda e: e.tensor_copy(
            ssmall[0:1, 0:2], P_c[0:1, 0:2]), inc="vec")
        P.wait("sync", "vec", v_sum_cp)
        v_sum_st = P.dma("sync", d_sum_in[:, :], ssmall[0:1, 0:2])
        P.wait("gpsimd", v_sum_st)
        v_ar = P.collective("ccr", "AllReduce", ALU.add, GROUP_ALL, d_sum_in, d_sum)

        xnbf = hbuf[:, H_XN:H_XN + 2048]
        xnbf_free = ("mm", 0)
        pf_evac = {0: 0, 1: 0}
        v_tph = 0
        xin_free = {0: fa_free}
        stat_free = {}
        for i in range(8):
            xin = fa[:, :]
            sb0 = (i % 2) * 8
            P.wait("sync", xin_free[0][0], xin_free[0][1])
            vl = P.dma("sync", xin, x_d[i * 128:(i + 1) * 128, :])
            P.wait("vector", vl)
            P.wait("vector", "act", stat_free.get(i % 2, 0))
            P.op("vector", lambda e, xin=xin, sb0=sb0: e.reduce_sum(
                stat[:, sb0:sb0 + 1], xin, axis=AX.X, negate=True), inc="vec")
            P.wait("vector", "vec", C["vec"])
            v_nm = P.op("vector", lambda e, sb0=sb0: e.tensor_scalar_mul(
                stat[:, sb0 + 1:sb0 + 2], stat[:, sb0:sb0 + 1],
                1.0 / 2048.0), inc="vec")
            P.wait("scalar", "vec", v_nm)
            P.op("scalar", lambda e, xin=xin, sb0=sb0: e.activation(
                out=fb[:, :], in_=xin, func=AF.Identity,
                bias=stat[:, sb0 + 1:sb0 + 2], scale=1.0), inc="act")
            P.wait("scalar", "act", C["act"])
            v_sqr = P.op("scalar", lambda e, xin=xin, sb0=sb0: e.activation(
                out=xin, in_=fb[:, :], func=AF.Square,
                accum_out=stat[:, sb0 + 2:sb0 + 3]), inc="act")
            xin_free[0] = ("act", v_sqr)
            P.wait("scalar", "act", C["act"])
            if i == 0:
                P.wait("scalar", "gp", v_ones)
            v_std = P.op("scalar", lambda e, sb0=sb0: e.activation(
                out=stat[:, sb0 + 3:sb0 + 4], in_=stat[:, sb0 + 2:sb0 + 3],
                func=AF.Sqrt, bias=epst[:, 0:1], scale=1.0 / 2048.0), inc="act")
            stat_free[i % 2] = v_std
            P.wait("vector", "act", v_std)
            v_rstd = P.op("vector", lambda e, sb0=sb0: e.reciprocal(
                stat[:, sb0 + 4:sb0 + 5], stat[:, sb0 + 3:sb0 + 4]), inc="vec")
            P.wait("vector", "vec", C["vec"])
            v_rstd2 = P.op("vector", lambda e, sb0=sb0: e.tensor_copy(
                stat[:, sb0 + 5:sb0 + 6], stat[:, sb0 + 4:sb0 + 5]), inc="vec")
            P.wait("scalar", "vec", v_rstd2)
            P.wait("scalar", xnbf_free[0], xnbf_free[1])
            v_xn = P.op("scalar", lambda e, sb0=sb0: e.activation(
                out=xnbf[:, :], in_=fb[:, :], func=AF.Identity,
                scale=stat[:, sb0 + 5:sb0 + 6]), inc="act")
            v_xn_last = v_xn
            P.wait("tensor", "act", v_xn)
            if i == 0:
                P.wait("tensor", "gp", v_id)
            for b4 in range(4):
                half = b4 % 2
                P.wait("tensor", "vec", pf_evac[half])
                for k in range(4):
                    kk = b4 * 4 + k
                    fn = lambda e, kk=kk, k=k, half=half: e.transpose(
                        P_f[:, half * 512 + k * 128: half * 512 + k * 128 + 128],
                        xnbf[:, kk * 128:kk * 128 + 128], ident[:, :])
                    if k == 3:
                        v_tph = P.op("tensor", fn, inc="mm")
                    else:
                        P.op("tensor", fn)
                P.wait("vector", "mm", v_tph)
                out3 = xnbuf[:, b4 * 4 * T: (b4 * 4 + 4) * T].rearrange(
                    "p (c t) -> p c t", t=T)[:, :, i * 128:i * 128 + 128]
                in3 = P_f[:, half * 512: half * 512 + 512].rearrange(
                    "p (c t) -> p c t", t=128)
                vev = P.op(
                    "vector", lambda e, o=out3, ii=in3: e.tensor_copy(o, ii),
                    inc="vec")
                pf_evac[0] = pf_evac[1] = vev
            xnbf_free = ("mm", v_tph)
        fa_free = xin_free[0]
        v_xnT_done = max(pf_evac.values())

        def band_ap(w_ap, jd):
            return w_ap[:, jd * 128:(jd + 1) * 128].rearrange(
                "(i p) d -> p i d", p=128)

        def chunked(sb_ap, inner):
            return sb_ap.rearrange("p (i d) -> p i d", d=inner)

        band_free = {0: ("gp", 0), 1: ("gp", 0)}
        psum_free = {0: ("act", 0), 1: ("act", 0)}
        kst_free = {0: None, 1: None}
        toks_kst = []
        v_kmm_last = 0
        for jd in range(16):
            p = jd % 2
            wb = (wbufA if p == 0 else wbufB)[:, 0:2048]
            P.wait("gpsimd", band_free[p][0], band_free[p][1])
            vw = P.dma("gpsimd", chunked(wb, 128), band_ap(wk_d, jd))
            P.wait("tensor", vw)
            P.wait("tensor", psum_free[p][0], psum_free[p][1])
            if jd == 0:
                P.wait("tensor", "vec", v_xnT_done)
            P_x = P_a if p == 0 else P_b
            for hh in range(2):
                for i in range(16):
                    fn = lambda e, wb=wb, i=i, hh=hh, P_x=P_x: e.matmul(
                        P_x[:, hh * 512:(hh + 1) * 512],
                        wb[:, i * 128:(i + 1) * 128],
                        xnbuf[:, i * T + hh * 512: i * T + (hh + 1) * 512],
                        start=(i == 0), stop=(i == 15))
                    if hh == 1 and i == 15:
                        v_kmm = P.op("tensor", fn, inc="mm")
                    else:
                        P.op("tensor", fn)
            band_free[p] = ("mm", v_kmm)
            v_kmm_last = v_kmm
            P.wait("scalar", "mm", v_kmm)
            P.wait("scalar", kst_free[p])
            if jd == 0:
                P.wait("scalar", toks_bias)
            if jd < 2:
                P.wait("scalar", "mm", v_tph)
            kts = hbuf[:, H_KT + p * 1024: H_KT + (p + 1) * 1024]
            v_kev = P.op("scalar", lambda e, P_x=P_x, jd=jd, kts=kts: e.activation(
                out=kts, in_=P_x[:, :], func=AF.Identity,
                bias=s_bk[:, jd:jd + 1], scale=1.0), inc="act")
            psum_free[p] = ("act", v_kev)
            P.wait("sync", "act", v_kev)
            v_kst = P.dma("sync", d_k_in[jd * 128:(jd + 1) * 128, :], kts)
            toks_kst.append(v_kst)
            kst_free[p] = v_kst
        P.wait("gpsimd", toks_kst)
        v_agk = P.collective("cck", "AllGather", ALU.bypass, GROUPS_KV, d_k_in, d_k)

        P.wait("tensor", v_bvrow)
        P.wait("tensor", "gp", v_ones)

        v_vst_ms = 0
        for tt in range(8):
            v_vst_ms = P.op("gpsimd", lambda e, tt=tt: e.memset(
                hbuf[:, H_VST + tt * 2080: H_VST + (tt + 1) * 2080], 1.0),
                inc="gp")
        v_vev = {}
        H_WVB = 18688
        toks_wv = []
        for q in range(4):
            toks_wv.append(P.dma(
                "gpsimd",
                chunked(hbuf[:, H_WVB + q * 8192: H_WVB + (q + 1) * 8192], 512),
                wv_d[:, q * 512:(q + 1) * 512].rearrange(
                    "(i p) d -> p i d", p=128)))
        vslots = [P_c[:, :], P_d[:, :], P_a[:, 0:512], P_a[:, 512:1024]]
        vpsum_free = {q: psum_free[0] if q >= 2 else ("vec", 0) for q in range(4)}
        vpsum_free[0] = ("vec", v_sum_cp)
        v_vmm = 0
        for tt in range(8):
            P.wait("tensor", toks_wv)
            for q in range(4):
                P.wait("tensor", vpsum_free[q][0], vpsum_free[q][1])
            for i in range(16):
                for q in range(4):
                    fn = (lambda e, i=i, q=q, tt=tt: e.matmul(
                        vslots[q],
                        xnbuf[:, i * T + tt * 128: i * T + tt * 128 + 128],
                        hbuf[:, H_WVB + q * 8192 + i * 512:
                             H_WVB + q * 8192 + (i + 1) * 512],
                        start=(i == 0), stop=False))
                    P.op("tensor", fn)
            for q in range(4):
                fn = (lambda e, q=q: e.matmul(
                    vslots[q], ones_t[0:1, :],
                    bvrow[0:1, q * 512:(q + 1) * 512],
                    start=False, stop=True))
                if q == 3:
                    v_vmm = P.op("tensor", fn, inc="mm")
                else:
                    P.op("tensor", fn)
            P.wait("vector", "mm", v_vmm)
            if tt == 0:
                P.wait("vector", "gp", v_vst_ms)
            vv = 0
            for q in range(4):
                base = H_VST + tt * 2080 + q * 520
                o3 = hbuf[:, base: base + 520].rearrange(
                    "p (h e) -> p h e", e=65)[:, :, 0:64]
                i3 = vslots[q].rearrange("p (h e) -> p h e", e=64)
                vv = P.op("vector", lambda e, o3=o3, i3=i3: e.tensor_copy(
                    o3, i3), inc="vec")
                vpsum_free[q] = ("vec", vv)
            v_vev[(tt, 3)] = vv
        toks_vst = []
        for tt in range(8):
            P.wait("sync", "vec", v_vev[(tt, 3)])
            toks_vst.append(P.dma(
                "sync", d_v_in[tt * 128:(tt + 1) * 128, :],
                hbuf[:, H_VST + tt * 2080: H_VST + (tt + 1) * 2080]))
        P.wait("gpsimd", toks_vst)
        v_agv = P.collective("ccv", "AllGather", ALU.bypass, GROUPS_KV, d_v_in, d_v)

        qband_free = {0: band_free[0], 1: band_free[1]}
        qpsum_free = {0: ("mm", v_vmm), 1: ("mm", v_vmm)}
        qst_free = {0: kst_free[0], 1: kst_free[1]}
        toks_qst = []
        v_qev = 0
        for jd in range(16):
            p = jd % 2
            wb = (wbufA if p == 0 else wbufB)[:, 0:2048]
            P.wait("gpsimd", qband_free[p][0], qband_free[p][1])
            vw = P.dma("gpsimd", chunked(wb, 128), band_ap(wq_d, jd))
            P.wait("tensor", vw)
            P.wait("tensor", qpsum_free[p][0], qpsum_free[p][1])
            P_x = P_a if p == 0 else P_b
            for hh in range(2):
                for i in range(16):
                    fn = lambda e, wb=wb, i=i, hh=hh, P_x=P_x: e.matmul(
                        P_x[:, hh * 512:(hh + 1) * 512],
                        wb[:, i * 128:(i + 1) * 128],
                        xnbuf[:, i * T + hh * 512: i * T + (hh + 1) * 512],
                        start=(i == 0), stop=(i == 15))
                    if hh == 1 and i == 15:
                        v_qmm = P.op("tensor", fn, inc="mm")
                    else:
                        P.op("tensor", fn)
            qband_free[p] = ("mm", v_qmm)
            P.wait("scalar", "mm", v_qmm)
            P.wait("scalar", qst_free[p])
            qts = hbuf[:, H_KT + p * 1024: H_KT + (p + 1) * 1024]
            v_qev = P.op("scalar", lambda e, P_x=P_x, jd=jd, qts=qts: e.activation(
                out=qts, in_=P_x[:, :],
                func=AF.Identity, bias=s_bq[:, jd:jd + 1], scale=1.0), inc="act")
            qpsum_free[p] = ("act", v_qev)
            P.wait("sync", "act", v_qev)
            v_qst = P.dma("sync", d_q[jd * 128:(jd + 1) * 128, :], qts)
            toks_qst.append(v_qst)
            qst_free[p] = v_qst

        P.wait("sync", "ccr", v_ar)
        v_sumld = P.dma("sync", ssmall[0:1, 2:4], d_sum[:, :])
        P.wait("vector", v_sumld)
        P.op("vector", lambda e: e.tensor_scalar(
            out=ssmall[0:1, 4:6], in0=ssmall[0:1, 2:4], scalar1=1.0 / NMAT,
            scalar2=1e-8, op0=ALU.mult, op1=ALU.max), inc="vec")
        P.wait("vector", "vec", C["vec"])
        v_inv = P.op("vector", lambda e: e.reciprocal(
            ssmall[0:1, 6:8], ssmall[0:1, 4:6]), inc="vec")
        P.wait("vector", "vec", C["vec"])
        P.wait("tensor", "vec", v_inv)
        P.wait("tensor", vpsum_free[1][0], vpsum_free[1][1])
        v_invb = P.op("tensor", lambda e: e.matmul(
            P_d[:, 0:2], ones1[0:1, :], ssmall[0:1, 6:8],
            start=True, stop=True), inc="mm")
        P.wait("vector", "mm", v_invb)
        P.op("vector", lambda e: e.tensor_scalar_mul(
            inv_bc[:, 0:2], P_d[:, 0:2], BIG), inc="vec")
        v_invbc = P.op("vector", lambda e: e.tensor_scalar_mul(
            inv_bc[:, 2:4], P_d[:, 0:2], -BIG), inc="vec")

        v_wst = {}
        hp2 = [0]
        pf_free = {0: ("vec", pf_evac[0]), 1: ("vec", pf_evac[1])}
        tern_free = ("mm", 0)
        v_w1std = v_w2std = 0
        for which, src_ap, ntile, ng, tw in ((0, w1s_d, 8, 8, 256),
                                             (1, w2s_d, 8, 2, 1024)):
            rows = ng * 128
            wide = ng * tw
            ntp = tw // 128
            v_tp = 0
            for k in range(ntile):
                P.wait("sync", fa_free[0], fa_free[1])
                if which == 0 and k == 0:
                    P.wait("sync", "vec", v_vev[(7, 3)])
                vl = P.dma("sync", fa[:, :].rearrange("p (g c) -> p g c", c=tw),
                           src_ap[k * rows:(k + 1) * rows, :].rearrange(
                               "(g p) c -> p g c", p=128))
                P.wait("vector", vl)
                P.wait("vector", "vec", v_invbc)
                if which == 0 and k == 0:
                    P.wait("vector", toks_vst)
                    P.wait("vector", toks_kst)
                    P.wait("vector", "act", v_xn_last)
                P.wait("vector", tern_free[0], tern_free[1])
                m = which
                P.op("vector", lambda e, m=m: e.tensor_scalar(
                    out=fb[:, :], in0=fa[:, :], scalar1=inv_bc[:, m:m + 1],
                    scalar2=-HALFBIG, op0=ALU.mult, op1=ALU.add), inc="vec")
                P.wait("vector", "vec", C["vec"])
                P.op("vector", lambda e: e.tensor_scalar(
                    out=hbuf[:, H_TU:H_TU + 2048], in0=fb[:, :], scalar1=0.0,
                    scalar2=1.0, op0=ALU.max, op1=ALU.min), inc="vec")
                P.wait("vector", "vec", C["vec"])
                v_u2 = P.op("vector", lambda e, m=m: e.tensor_scalar(
                    out=fb[:, :], in0=fa[:, :],
                    scalar1=inv_bc[:, 2 + m:3 + m],
                    scalar2=-HALFBIG, op0=ALU.mult, op1=ALU.add), inc="vec")
                fa_free = ("vec", v_u2)
                P.wait("vector", "vec", C["vec"])
                P.op("vector", lambda e: e.tensor_scalar(
                    out=hbuf[:, H_TV:H_TV + 2048], in0=fb[:, :], scalar1=0.0,
                    scalar2=1.0, op0=ALU.max, op1=ALU.min), inc="vec")
                P.wait("vector", "vec", C["vec"])
                v_tt = P.op("vector", lambda e: e.tensor_tensor(
                    out=hbuf[:, H_TT:H_TT + 2048], in0=hbuf[:, H_TU:H_TU + 2048],
                    in1=hbuf[:, H_TV:H_TV + 2048], op=ALU.subtract), inc="vec")
                P.wait("tensor", "vec", v_tt)
                cw = 8192 if which == 0 else 2048
                for g in range(ng):
                    kk = k * ng + g
                    nsub = (ntp + 3) // 4
                    for sub in range(nsub):
                        j0 = sub * 4
                        nj = min(4, ntp - j0)
                        half = hp2[0] % 2
                        hp2[0] += 1
                        P.wait("tensor", pf_free[half][0], pf_free[half][1])
                        for jj in range(nj):
                            j = j0 + jj
                            fn = (lambda e, g=g, j=j, jj=jj, half=half, tw=tw:
                                  e.transpose(
                                      P_f[:, half * 512 + jj * 128:
                                          half * 512 + (jj + 1) * 128],
                                      hbuf[:, H_TT + g * tw + j * 128:
                                           H_TT + g * tw + (j + 1) * 128],
                                      ident[:, :]))
                            if jj == nj - 1:
                                v_tp = P.op("tensor", fn, inc="mm")
                            else:
                                P.op("tensor", fn)
                        P.wait("vector", "mm", v_tp)
                        if which == 1 and kk == 0 and sub == 0:
                            P.wait("vector", toks_w1st)
                        o3 = hbuf[:, H_WST:H_WST + ntp * cw].rearrange(
                            "p (j c) -> p j c", c=cw)[
                                :, j0:j0 + nj, kk * 128:kk * 128 + 128]
                        i3 = P_f[:, half * 512: half * 512 + nj * 128].rearrange(
                            "p (j c) -> p j c", c=128)
                        vev = ("vec", P.op(
                            "vector", lambda e, o3=o3, i3=i3: e.tensor_copy(
                                o3, i3), inc="vec"))
                        pf_free[0] = pf_free[1] = vev
                tern_free = ("mm", v_tp)
            nrow = ntp
            cw = 8192 if which == 0 else 2048
            dst = d_w1t_in if which == 0 else d_w2t_in
            toks_std = []
            for j in range(nrow):
                P.wait("sync", "vec", pf_free[0][1])
                P.wait("sync", "vec", pf_free[1][1])
                toks_std.append(P.dma(
                    "sync", dst[j * 128:(j + 1) * 128, :],
                    hbuf[:, H_WST + j * cw:H_WST + (j + 1) * cw]))
            if which == 0:
                toks_w1st = toks_std
                P.wait("gpsimd", toks_w1st)
                v_agw1 = P.collective("cw1", "AllGather", ALU.bypass, GROUP_ALL,
                                      d_w1t_in, d_w1t)
            else:
                toks_w2st = toks_std
                P.wait("gpsimd", toks_w2st)
                v_agw2 = P.collective("cw2", "AllGather", ALU.bypass, GROUP_ALL,
                                      d_w2t_in, d_w2t)

        kvld = {}
        feats_evac = {}
        feats_mm = {}
        exp_ms2 = {}
        bc_copy = {}
        v_feats_evac_last = 0
        v_fb_last0 = C["vec"]

        F1_START = 2 * 64
        f1 = {"emitted": 0, "slot": 0, "v_hmm": {}, "v_hev": {},
              "w1_free": {0: qband_free[0], 1: qband_free[1]}, "toks": {}}

        def f1_load(sb_):
            par = sb_ % 2
            wb = (wbufA if par == 0 else wbufB)[:, 0:4096]
            P.wait("gpsimd", f1["w1_free"][par][0], f1["w1_free"][par][1])
            if sb_ < 2:
                P.wait("gpsimd", "cw1", 1)
            f1["toks"][sb_] = P.dma(
                "gpsimd", wb.rearrange("p (i c) -> p i c", c=256),
                d_w1t[:, sb_ * 256:(sb_ + 1) * 256].rearrange(
                    "(i p) c -> p i c", p=128))

        def f1_emit_unit():
            u = f1["emitted"]
            if u >= 1024:
                return
            f1["emitted"] += 1
            b, i = u // 16, u % 16
            sb_ = b // 2
            l = b % 2
            par = sb_ % 2
            wb = (wbufA if par == 0 else wbufB)[:, 0:4096]
            if i == 0:
                P.wait("tensor", f1["toks"][sb_])
                if b > 0:
                    P.wait("tensor", "vec", f1["v_hev"][b - 1])
            fn = (lambda e, wb=wb, i=i, l=l: e.matmul(
                P_h[:, :], wb[:, i * 256 + l * 128: i * 256 + (l + 1) * 128],
                xnbuf[:, i * T: i * T + 512],
                start=(i == 0), stop=(i == 15)))
            if i < 15:
                P.op("tensor", fn)
            else:
                f1["v_hmm"][b] = P.op("tensor", fn, inc="mm")
                f1["w1_free"][par] = ("mm", f1["v_hmm"][b])
                if l == 1 and sb_ + 2 < 32:
                    f1_load(sb_ + 2)
                P.wait("vector", "mm", f1["v_hmm"][b])
                if b == 0:
                    P.wait("vector", toks_w1st)
                    P.wait("vector", toks_w2st)
                    P.wait("vector", toks_vst)
                    P.wait("vector", toks_kst)
                    P.wait("vector", toks_qst)
                    P.wait("vector", "mm", tern_free[1])
                f1["v_hev"][b] = P.op(
                    "vector", lambda e, b=b: e.tensor_scalar_add(
                        hbuf[:, H_H + b * 512:H_H + (b + 1) * 512],
                        P_h[:, :], s_b1[:, b:b + 1]), inc="vec")

        def f1_slot():
            s = f1["slot"]
            f1["slot"] += 1
            if s < F1_START:
                return
            target = min(1024, ((s - F1_START + 1) * 1024 + 767) // 768)
            while f1["emitted"] < target:
                f1_emit_unit()

        f1_load(0)
        f1_load(1)

        for j in range(PAIRS):
            kb = hbuf[:, H_KB: H_KB + 4096]
            vb = hbuf[:, H_VB: H_VB + 4160]
            qp = j % 2
            qsl = q2[:, qp * 1024:(qp + 1) * 1024]
            if j >= 1:
                P.wait("sync", "mm", feats_mm[(j - 1, 1)])
            if j < 2:
                P.wait("sync", "cck", v_agk)
                P.wait("sync", "ccv", v_agv)
                P.wait("sync", toks_w2st)
                P.wait("sync", "mm", tern_free[1])
                P.wait("sync", toks_qst)
            toks_kv = [P.dma("sync", qsl, d_q[j * 128:(j + 1) * 128, :])]
            for r in range(4):
                toks_kv.append(P.dma(
                    "sync", kb[:, r * 1024:(r + 1) * 1024],
                    d_k[r * 2048 + j * 128: r * 2048 + (j + 1) * 128, :]))
            for r in range(4):
                toks_kv.append(P.dma(
                    "sync", chunked(vb[:, r * 1040:(r + 1) * 1040], 130),
                    d_v[r * 1024:(r + 1) * 1024,
                        130 * j:130 * (j + 1)].rearrange(
                            "(c p) m -> p c m", p=128)))
            kvld[j] = toks_kv

            for qt in range(2):
                exp_ms = {}
                mmg = {}
                mm2g = {}
                q0 = qt * 512
                expb = [hbuf[:, H_EA + par * 1024: H_EA + (par + 1) * 1024]
                        for par in range(3)]

                def mm2_pair(kc):
                    par = kc % 3
                    for (head, Pfe) in ((0, P_c), (1, P_d)):
                        cbase = kc * 130 + head * 65
                        fn = (lambda e, vb=vb, cbase=cbase, Pfe=Pfe,
                              par=par, head=head, kc=kc: e.matmul(
                                  Pfe[0:65, :],
                                  vb[:, cbase:cbase + 65],
                                  expb[par][:, head * 512:(head + 1) * 512],
                                  start=(kc == 0), stop=(kc == 31)))
                        yield fn

                for kc in range(32):
                    par = kc % 2
                    Psc = P_a if par == 0 else P_b
                    if kc == 0:
                        if qt == 0:
                            P.wait("tensor", kvld[j])
                            if j == 0:
                                P.wait("tensor", "act", v_qev)
                        if (j, qt) != (0, 0):
                            prev = (j, 0) if qt == 1 else (j - 1, 1)
                            P.wait("tensor", "act", exp_ms2.get(prev, 0))
                    else:
                        P.wait("tensor", "act", exp_ms.get(kc - 2, 0))
                    v_g = 0
                    for head, rl, rh in ((0, 0, 64), (1, 64, 128)):
                        fn = (lambda e, kb=kb, kc=kc, Psc=Psc, rl=rl, rh=rh,
                              qsl=qsl, q0=q0, head=head: e.matmul(
                                  Psc[:, head * 512:(head + 1) * 512],
                                  kb[rl:rh, kc * 128:(kc + 1) * 128],
                                  qsl[rl:rh, q0: q0 + 512],
                                  start=True, stop=True,
                                  tile_position=(rl, 0)))
                        if head == 1:
                            v_g = P.op("tensor", fn, inc="mm")
                        else:
                            P.op("tensor", fn)
                    mmg[kc] = v_g
                    if kc >= 2:
                        if kc == 2:
                            prev = (j, 0) if qt == 1 else (j - 1, 1)
                            if (j, qt) != (0, 0):
                                P.wait("tensor", "vec", feats_evac.get(prev, 0))
                        fns = list(mm2_pair(kc - 2))
                        for idx, fn in enumerate(fns):
                            if idx == len(fns) - 1:
                                mm2g[kc - 2] = P.op("tensor", fn, inc="mm")
                            else:
                                P.op("tensor", fn)
                    f1_slot()
                    P.wait("scalar", "mm", mmg[kc])
                    if kc >= 3:
                        P.wait("scalar", "mm", mm2g.get(kc - 3, 0))
                    exp_ms[kc] = P.op("scalar", lambda e, par3=kc % 3, Psc=Psc:
                                      e.activation(out=expb[par3][:, :],
                                                   in_=Psc[:, :], func=AF.Exp,
                                                   scale=SCALE), inc="act")
                exp_ms2[(j, qt)] = exp_ms[31]
                P.wait("tensor", "act", exp_ms[31])
                v_feats = 0
                fns = list(mm2_pair(30)) + list(mm2_pair(31))
                for idx, fn in enumerate(fns):
                    if idx == len(fns) - 1:
                        v_feats = P.op("tensor", fn, inc="mm")
                    else:
                        P.op("tensor", fn)
                P.wait("vector", "mm", v_feats)
                P.op("vector", lambda e: e.reciprocal(
                    recA, P_c[64:65, :]), inc="vec")
                v_rec = P.op("vector", lambda e: e.reciprocal(
                    recB, P_d[64:65, :]), inc="vec")
                P.wait("tensor", "vec", v_rec)
                P.op("tensor", lambda e: e.matmul(
                    P_c[64:128, :], ones64f[0:1, :], recA,
                    start=True, stop=True, tile_position=(0, 64)))
                v_bc = P.op("tensor", lambda e: e.matmul(
                    P_d[64:128, :], ones64f[0:1, :], recB,
                    start=True, stop=True, tile_position=(0, 64)), inc="mm")
                v_bc_last = v_bc
                P.wait("vector", "mm", v_bc)
                if j == 0 and qt == 0:
                    P.wait("vector", toks_w2st)
                fcol = j * T + q0
                P.op("vector", lambda e: e.tensor_copy(
                    bcbufA, P_c[64:128, :]), inc="vec")
                v_bcc = P.op("vector", lambda e: e.tensor_copy(
                    bcbufB, P_d[64:128, :]), inc="vec")
                P.op("vector", lambda e, fcol=fcol: e.tensor_tensor(
                    out=hbuf[0:64, H_FE + fcol:H_FE + fcol + 512],
                    in0=P_c[0:64, :], in1=bcbufA, op=ALU.mult), inc="vec")
                vfe = P.op("vector", lambda e, fcol=fcol: e.tensor_tensor(
                    out=hbuf[64:128, H_FE + fcol:H_FE + fcol + 512],
                    in0=P_d[0:64, :], in1=bcbufB, op=ALU.mult), inc="vec")
                bc_copy[(j, qt)] = v_bcc
                feats_evac[(j, qt)] = vfe
                feats_mm[(j, qt)] = v_feats
                v_feats_evac_last = vfe
        while f1["emitted"] < 1024:
            f1_emit_unit()
        P.wait("scalar", "vec", f1["v_hev"][63])
        v_h_done_th0 = 0
        for cgel in range(4):
            v_h_done_th0 = P.op("scalar", lambda e, cgel=cgel: e.activation(
                out=hbuf[:, cgel * 8192:(cgel + 1) * 8192],
                in_=hbuf[:, cgel * 8192:(cgel + 1) * 8192],
                func=AF.Gelu, scale=1.0), inc="act")

        v_dbg_fe = None
        if DEBUG:
            P.wait("sync", "vec", v_feats_evac_last)
            v_dbg_fe = [P.dma("sync", o_fe[:, :], hbuf[:, H_FE:H_FE + NCH * T])]

        wo_band_free = {0: f1["w1_free"][0], 1: f1["w1_free"][1]}
        wo_psum_free = {0: ("vec", v_feats_evac_last), 1: ("vec", v_feats_evac_last)}
        sst_free = {0: None, 1: None}
        toks_sst = []
        v_woev = {}
        v_womm_last = 0
        for jd in range(16):
            p = jd % 2
            wb = (wbufA if p == 0 else wbufB)[:, 0:2048]
            P.wait("gpsimd", wo_band_free[p][0], wo_band_free[p][1])
            vw = P.dma("gpsimd", chunked(wb, 128), band_ap(wo_d, jd))
            P.wait("tensor", vw)
            P.wait("tensor", wo_psum_free[p][0], wo_psum_free[p][1])
            if jd == 0:
                P.wait("tensor", "vec", v_feats_evac_last)
                if DEBUG:
                    P.wait("tensor", v_dbg_fe)
            P_x = P_a if p == 0 else P_b
            for hh in range(2):
                for i in range(16):
                    fe = H_FE + i * T + hh * 512
                    fn = lambda e, wb=wb, i=i, hh=hh, P_x=P_x, fe=fe: e.matmul(
                        P_x[:, hh * 512:(hh + 1) * 512],
                        wb[:, i * 128:(i + 1) * 128],
                        hbuf[:, fe:fe + 512],
                        start=(i == 0), stop=(i == 15))
                    if hh == 1 and i == 15:
                        v_womm = P.op("tensor", fn, inc="mm")
                    else:
                        P.op("tensor", fn)
            wo_band_free[p] = ("mm", v_womm)
            v_womm_last = v_womm
            P.wait("scalar", "mm", v_womm)
            P.wait("scalar", sst_free[p])
            if jd < 2:
                P.wait("scalar", "mm", feats_mm[(PAIRS - 1, 1)])
            sts = q2[:, p * 1024:(p + 1) * 1024]
            vv = P.op("scalar", lambda e, P_x=P_x, jd=jd, sts=sts: e.activation(
                out=sts, in_=P_x[:, :],
                func=AF.Identity, bias=s_bo[:, jd:jd + 1], scale=1.0), inc="act")
            v_woev[jd] = vv
            wo_psum_free[p] = ("act", vv)
            P.wait("sync", "act", vv)
            v_sst = P.dma("sync", d_s[jd * 128:(jd + 1) * 128, :], sts)
            toks_sst.append(v_sst)
            sst_free[p] = v_sst
        v_woev_last = v_woev[15]

        w1_free = {0: ("mm", 0), 1: ("mm", 0)}
        h_psum_free = {0: ("act", 0), 1: ("act", 0)}
        f2_psum_free = ("act", 0)
        fa_slot_free = {sl: ("vec", v_feats_evac_last) for sl in range(4)}
        s_tile_free = {sl: None for sl in range(4)}
        fb_slot_free = {sl: ("vec", v_feats_evac_last) for sl in range(4)}
        pbf_free = {0: ("mm", 0), 1: ("mm", 0)}
        pf2_free = {0: pf_free[0], 1: pf_free[1]}
        fout_free = {0: None, 1: None}
        w2_free = {0: ("mm", v_bc_last), 1: ("mm", v_bc_last)}
        v_out_dma = 0
        fout = hbuf[:, H_KB:H_KB + 2048].bitcast(F32)
        w1_free = dict(f1["w1_free"])
        v_f2_last_th0 = 0
        for th in range(2):
            t0 = th * 512
            if th == 0:
                v_h_done = v_h_done_th0
            else:
                v_hmm = {}
                v_gelu = {}
                for sb_ in range(32):
                    p = sb_ % 2
                    wb = (wbufA if p == 0 else wbufB)[:, 0:4096]
                    qeng = "sync" if sb_ % 2 == 0 else "gpsimd"
                    P.wait(qeng, w1_free[p][0], w1_free[p][1])
                    if sb_ < 2:
                        P.wait(qeng, "mm", v_womm_last)
                    toks_w1l = [P.dma(
                        qeng,
                        wb.rearrange("p (i c) -> p i c", c=256),
                        d_w1t[:, sb_ * 256:(sb_ + 1) * 256].rearrange(
                            "(i p) c -> p i c", p=128))]
                    for l in range(2):
                        bb = sb_ * 2 + l
                        P.wait("tensor", toks_w1l)
                        if bb == 0:
                            P.wait("tensor", "vec", f1["v_hev"][63])
                            P.wait("tensor", "mm", v_f2_last_th0)
                        else:
                            P.wait("tensor", "act", v_gelu[bb - 1])
                        for i in range(16):
                            fn = (lambda e, wb=wb, i=i, l=l, t0=t0: e.matmul(
                                P_h[:, :],
                                wb[:, i * 256 + l * 128:
                                   i * 256 + (l + 1) * 128],
                                xnbuf[:, i * T + t0: i * T + t0 + 512],
                                start=(i == 0), stop=(i == 15)))
                            if i == 15:
                                v_hmm[bb] = P.op("tensor", fn, inc="mm")
                            else:
                                P.op("tensor", fn)
                        P.wait("scalar", "mm", v_hmm[bb])
                        if DEBUG and bb == 0:
                            P.wait("scalar", v_dbg_h)
                        if bb == 0:
                            P.wait("scalar", "mm", v_f2_last_th0)
                        v_gelu[bb] = P.op(
                            "scalar", lambda e, bb=bb: e.activation(
                                out=hbuf[:, H_H + bb * 512:
                                         H_H + (bb + 1) * 512],
                                in_=P_h[:, :], func=AF.Gelu,
                                bias=s_b1[:, bb:bb + 1], scale=1.0), inc="act")
                    w1_free[p] = ("mm", v_hmm[sb_ * 2 + 1])
                v_h_done = v_gelu[63]
            if DEBUG and th == 0:
                P.wait("sync", "act", v_h_done)
                v_dbg_h = P.dma("sync", o_h[:, :], hbuf[:, H_H:H_H + 32768])
            f2slots = [P_a[:, 0:512], P_a[:, 512:1024],
                       P_b[:, 0:512], P_b[:, 512:1024]]
            for g in range(4):
                v_f2 = {}
                for fg in range(16):
                    p = fg % 2
                    w2b = (wbufA if p == 0 else wbufB)[:, 4096:6144]
                    P.wait("sync" if p == 0 else "gpsimd",
                           w2_free[p][0], w2_free[p][1])
                    if th == 0 and g == 0 and fg < 2:
                        P.wait("sync" if p == 0 else "gpsimd", "cw2", 1)
                    vld = P.dma(
                        "sync" if p == 0 else "gpsimd",
                        w2b.rearrange("p (f c) -> p f c", c=512),
                        d_w2t[fg * 512:(fg + 1) * 512,
                              g * 512:(g + 1) * 512].rearrange(
                                  "(f p) c -> p f c", p=128))
                    P.wait("tensor", vld)
                    if fg == 0:
                        P.wait("tensor", "act", v_h_done)
                        P.wait("tensor", f2_psum_free[0], f2_psum_free[1])
                        if th == 0 and g == 0:
                            P.wait("tensor", "act", v_woev_last)
                    for fl in range(4):
                        fbk = fg * 4 + fl
                        for sl in range(4):
                            fn = (lambda e, w2b=w2b, sl=sl, fl=fl, fbk=fbk:
                                  e.matmul(
                                      f2slots[sl],
                                      w2b[:, fl * 512 + sl * 128:
                                          fl * 512 + (sl + 1) * 128],
                                      hbuf[:, H_H + fbk * 512:
                                           H_H + (fbk + 1) * 512],
                                      start=(fbk == 0), stop=(fbk == 63)))
                            if sl == 3 and fl == 3:
                                v_f2[fg] = P.op("tensor", fn, inc="mm")
                            else:
                                P.op("tensor", fn)
                    w2_free[p] = ("mm", v_f2[fg])
                    if th == 0:
                        v_f2_last_th0 = v_f2[fg]
                v_last_evac = 0
                for sl in range(4):
                    jd = g * 4 + sl
                    ppar = jd % 2
                    P.wait("scalar", "mm", v_f2[15])
                    P.wait("scalar", fa_slot_free[sl][0], fa_slot_free[sl][1])
                    v_silu = P.op("scalar", lambda e, sl=sl, jd=jd: e.activation(
                        out=fa[:, sl * 512:(sl + 1) * 512], in_=f2slots[sl],
                        func=AF.Silu, bias=s_b2[:, jd:jd + 1], scale=1.0),
                        inc="act")
                    v_last_evac = v_silu
                    squ = q2[:, sl * 512:(sl + 1) * 512]
                    P.wait("sync", s_tile_free[sl])
                    if th == 0 and g == 0 and sl < 2:
                        P.wait("sync", toks_sst)
                    v_sld = P.dma("sync", squ,
                                  d_s[jd * 128:(jd + 1) * 128, t0:t0 + 512])
                    P.wait("vector", "act", v_silu)
                    P.wait("vector", v_sld)
                    P.wait("vector", pbf_free[ppar][0], pbf_free[ppar][1])
                    v_p = P.op("vector", lambda e, sl=sl, ppar=ppar, squ=squ:
                               e.tensor_tensor(
                                   out=hbuf[:, H_PBF + ppar * 512:
                                            H_PBF + (ppar + 1) * 512],
                                   in0=fa[:, sl * 512:(sl + 1) * 512],
                                   in1=squ,
                                   op=ALU.mult), inc="vec")
                    fa_slot_free[sl] = ("vec", v_p)
                    s_tile_free[sl] = ("vec", v_p)
                    P.wait("tensor", "vec", v_p)
                    P.wait("tensor", pf2_free[ppar][0], pf2_free[ppar][1])
                    v_tp5 = 0
                    for s4 in range(4):
                        fn = (lambda e, s4=s4, ppar=ppar: e.transpose(
                            P_f[:, ppar * 512 + s4 * 128:
                                ppar * 512 + (s4 + 1) * 128],
                            hbuf[:, H_PBF + ppar * 512 + s4 * 128:
                                 H_PBF + ppar * 512 + (s4 + 1) * 128],
                            ident[:, :]))
                        if s4 == 3:
                            v_tp5 = P.op("tensor", fn, inc="mm")
                        else:
                            P.op("tensor", fn)
                    pbf_free[ppar] = ("mm", v_tp5)
                    P.wait("sync", fb_slot_free[sl][0], fb_slot_free[sl][1])
                    vxl = P.dma("sync",
                                chunked(fb[:, sl * 512:(sl + 1) * 512], 128),
                                x_d[th * 512:(th + 1) * 512,
                                    jd * 128:(jd + 1) * 128].rearrange(
                                        "(s p) d -> p s d", p=128))
                    P.wait("vector", "mm", v_tp5)
                    P.wait("vector", vxl)
                    P.wait("vector", fout_free[ppar])
                    v_ot = P.op("vector", lambda e, sl=sl, ppar=ppar:
                                e.tensor_tensor(
                                    out=fout[:, ppar * 512:(ppar + 1) * 512],
                                    in0=P_f[:, ppar * 512:ppar * 512 + 512],
                                    in1=fb[:, sl * 512:(sl + 1) * 512],
                                    op=ALU.add), inc="vec")
                    pf2_free[0] = pf2_free[1] = ("vec", v_ot)
                    fb_slot_free[sl] = ("vec", v_ot)
                    P.wait("sync", "vec", v_ot)
                    v_out_dma = P.dma(
                        "sync",
                        out_d[th * 512:(th + 1) * 512,
                              jd * 128:(jd + 1) * 128].rearrange(
                                  "(s p) d -> p s d", p=128),
                        chunked(fout[:, ppar * 512:(ppar + 1) * 512], 128))
                    fout_free[ppar] = v_out_dma
                f2_psum_free = ("act", v_last_evac)

        if DEBUG:
            P.wait("sync", "vec", C["vec"])
            P.wait("sync", "act", C["act"])
            P.dma("sync", o_xn[:, :], xnbuf[:, :])
            for r in range(16):
                P.dma("sync", o_st[r * 128:(r + 1) * 128, :],
                      d_s[r * 128:(r + 1) * 128, :])
                P.dma("sync", o_q[r * 128:(r + 1) * 128, :],
                      d_q[r * 128:(r + 1) * 128, :])
            P.dma("sync", o_sum[:, :], ssmall[0:1, 2:4])
            P.dma("sync", o_kg[:, :], d_k[:, :])
            P.dma("sync", o_vg[:, :], d_v[:, :])
            for r in range(16):
                P.dma("sync", o_w1t[r * 128:(r + 1) * 128, :],
                      d_w1t[r * 128:(r + 1) * 128, :])
            for r in range(64):
                P.dma("sync", o_w2t[r * 128:(r + 1) * 128, :],
                      d_w2t[r * 128:(r + 1) * 128, :])

        for s in SQ:
            P.wait("sync", s, C[s])

        with nc.Block() as block:
            @block.sync
            def _(e):
                for fn in P.items["sync"]:
                    fn(e)

            @block.gpsimd
            def _(e):
                for fn in P.items["gpsimd"]:
                    fn(e)

            @block.tensor
            def _(e):
                for fn in P.items["tensor"]:
                    fn(e)

            @block.vector
            def _(e):
                for fn in P.items["vector"]:
                    fn(e)

            @block.scalar
            def _(e):
                for fn in P.items["scalar"]:
                    fn(e)

    return nc


_NC_CACHE = None


def kernel(L0_attn_x, L0_attn_Wq, L0_attn_bq, L0_attn_Wk, L0_attn_bk,
           L0_attn_Wv, L0_attn_bv, L0_attn_Wo, L0_attn_bo,
           L0_ffn_w1, L0_ffn_b1, L0_ffn_w2, L0_ffn_b2):
    global _NC_CACHE
    x = np.ascontiguousarray(np.asarray(L0_attn_x, dtype=np.float32))
    wq = np.ascontiguousarray(np.asarray(L0_attn_Wq, dtype=np.float32))
    wk = np.ascontiguousarray(np.asarray(L0_attn_Wk, dtype=np.float32))
    wv = np.ascontiguousarray(np.asarray(L0_attn_Wv, dtype=np.float32))
    wo = np.ascontiguousarray(np.asarray(L0_attn_Wo, dtype=np.float32))
    w1 = np.ascontiguousarray(np.asarray(L0_ffn_w1, dtype=np.float32))
    w2 = np.ascontiguousarray(np.asarray(L0_ffn_w2, dtype=np.float32))

    def fold(b, n):
        return np.ascontiguousarray(
            np.asarray(b, dtype=np.float32).reshape(n, 128).T)

    bq2 = fold(L0_attn_bq, 16)
    bk2 = fold(L0_attn_bk, 16)
    bo2 = fold(L0_attn_bo, 16)
    b12 = fold(L0_ffn_b1, 64)
    b22 = fold(L0_ffn_b2, 16)
    bvr = np.ascontiguousarray(
        np.asarray(L0_attn_bv, dtype=np.float32).reshape(1, D))

    if _NC_CACHE is None:
        _NC_CACHE = build_program()
    nc = _NC_CACHE

    in_maps = []
    for c in range(8):
        b, sl = c // 4, c % 4
        in_maps.append({
            "x": np.ascontiguousarray(x[b, sl * T:(sl + 1) * T, :]),
            "wq": wq, "wk": wk, "wv": wv, "wo": wo,
            "bq": bq2, "bk": bk2, "bo": bo2, "bv_row": bvr,
            "bo_row": np.ascontiguousarray(
                np.asarray(L0_attn_bo, dtype=np.float32).reshape(1, D)),
            "b1": b12, "b2": b22,
            "w1s": np.ascontiguousarray(w1[:, c * 256:(c + 1) * 256]),
            "w2s": np.ascontiguousarray(w2[:, c * 1024:(c + 1) * 1024]),
        })
    res = run_bass_kernel_spmd(nc, in_maps, core_ids=list(range(8)))
    out = np.empty((2, 4096, D), dtype=np.float32)
    for c in range(8):
        b, sl = c // 4, c % 4
        out[b, sl * T:(sl + 1) * T, :] = res.results[c]["out"]
    return out


# version 12
# speedup vs baseline: 1.1018x; 1.1018x over previous
import numpy as np
import ml_dtypes
import concourse.bass as bass
import concourse.mybir as mybir
from concourse.bass_utils import run_bass_kernel_spmd

F32 = mybir.dt.float32
BF16 = mybir.dt.bfloat16
AF = mybir.ActivationFunctionType
ALU = mybir.AluOpType
AX = mybir.AxisListType
bf = ml_dtypes.bfloat16

D = 2048
DFF = 8192
T = 1024
NCH = 16
PAIRS = 16
SCALE = float(1.0 / np.sqrt(32.0))
BIG = float(2.0 ** 63)
HALFBIG = float(0.5 * 2.0 ** 63)
NMAT = float(DFF * D)
GROUPS_KV = [[0, 1, 2, 3], [4, 5, 6, 7]]
GROUP_ALL = [[0, 1, 2, 3, 4, 5, 6, 7]]

DEBUG = False

SEM = {}
NSQ = 8
NGQ = 4
SQ = tuple(f"sq{i}" for i in range(NSQ))
GQ = tuple(f"gq{i}" for i in range(NGQ))
SEMNAMES = SQ + GQ + ("mm", "act", "vec", "gp",
                      "ccr", "cck", "ccv", "cw1", "cw2")


class Plan:

    ENGS = ("sync", "gpsimd", "tensor", "vector", "scalar")

    def __init__(self):
        self.items = {e: [] for e in self.ENGS}
        self.ctr = {s: 0 for s in SEMNAMES}
        self.rr = {"sync": 0, "gpsimd": 0}

    def add(self, eng, fn):
        self.items[eng].append(fn)

    def wait(self, eng, sem, val=None):
        if val is None:
            tok = sem
            if tok is None:
                return
            if isinstance(tok, list):
                best = {}
                for s, v in tok:
                    if s is not None and v > best.get(s, 0):
                        best[s] = v
                for s, v in best.items():
                    self.add(eng, lambda e, s=s, v=v: e.wait_ge(SEM[s], v))
                return
            sem, val = tok
        if sem is None or val is None or val <= 0:
            return
        self.add(eng, lambda e, s=sem, v=val: e.wait_ge(SEM[s], v))

    def dma(self, eng, out_ap, in_ap, **kw):
        pool = SQ if eng == "sync" else GQ
        s = pool[self.rr[eng] % len(pool)]
        self.rr[eng] += 1
        self.ctr[s] += 16
        v = self.ctr[s]
        self.add(eng, lambda e, o=out_ap, i=in_ap, kw=dict(kw), s=s: e.dma_start(
            out=o, in_=i, **kw).then_inc(SEM[s], 16))
        return (s, v)

    def op(self, eng, fn, inc=None):
        if inc is None:
            self.add(eng, fn)
            return None
        self.ctr[inc] += 1
        v = self.ctr[inc]
        self.add(eng, lambda e, f=fn, s=inc: f(e).then_inc(SEM[s], 1))
        return v

    def collective(self, sem, kind, op_, groups, in_ap, out_ap):
        self.ctr[sem] += 1
        v = self.ctr[sem]
        self.add("gpsimd", lambda e, k=kind, o=op_, g=groups, i=in_ap, oo=out_ap,
                 s=sem: e.collective_compute(
                     k, o, replica_groups=g, ins=[i.opt()],
                     outs=[oo.opt()]).then_inc(SEM[s], 1))
        return v


def build_program():
    nc = bass.Bass("TRN2", target_bir_lowering=False)

    x_d = nc.dram_tensor("x", [T, D], F32, kind="ExternalInput").ap()
    wq_d = nc.dram_tensor("wq", [D, D], F32, kind="ExternalInput").ap()
    wk_d = nc.dram_tensor("wk", [D, D], F32, kind="ExternalInput").ap()
    wv_d = nc.dram_tensor("wv", [D, D], F32, kind="ExternalInput").ap()
    wo_d = nc.dram_tensor("wo", [D, D], F32, kind="ExternalInput").ap()
    bq_d = nc.dram_tensor("bq", [128, 16], F32, kind="ExternalInput").ap()
    bk_d = nc.dram_tensor("bk", [128, 16], F32, kind="ExternalInput").ap()
    bo_d = nc.dram_tensor("bo", [128, 16], F32, kind="ExternalInput").ap()
    bv_row_d = nc.dram_tensor("bv_row", [1, D], F32, kind="ExternalInput").ap()
    bo_row_d = nc.dram_tensor("bo_row", [1, D], F32, kind="ExternalInput").ap()
    b1_d = nc.dram_tensor("b1", [128, 64], F32, kind="ExternalInput").ap()
    b2_d = nc.dram_tensor("b2", [128, 16], F32, kind="ExternalInput").ap()
    w1s_d = nc.dram_tensor("w1s", [DFF, 256], F32, kind="ExternalInput").ap()
    w2s_d = nc.dram_tensor("w2s", [D, 1024], F32, kind="ExternalInput").ap()
    out_d = nc.dram_tensor("out", [T, D], F32, kind="ExternalOutput").ap()
    if DEBUG:
        o_xn = nc.dram_tensor("o_xn", [128, NCH * T], BF16,
                              kind="ExternalOutput").ap()
        o_st = nc.dram_tensor("o_st", [D, T], BF16,
                              kind="ExternalOutput").ap()
        o_fe = nc.dram_tensor("o_fe", [128, NCH * T], BF16,
                              kind="ExternalOutput").ap()
        o_h = nc.dram_tensor("o_h", [128, 32768], BF16,
                             kind="ExternalOutput").ap()
        o_kg = nc.dram_tensor("o_kg", [4 * D, T], BF16,
                              kind="ExternalOutput").ap()
        o_vg = nc.dram_tensor("o_vg", [4 * T, 2080], BF16,
                              kind="ExternalOutput").ap()
        o_w1t = nc.dram_tensor("o_w1t", [D, DFF], BF16,
                               kind="ExternalOutput").ap()
        o_w2t = nc.dram_tensor("o_w2t", [DFF, D], BF16,
                               kind="ExternalOutput").ap()
        o_sum = nc.dram_tensor("o_sum", [1, 2], F32, kind="ExternalOutput").ap()
        o_q = nc.dram_tensor("o_q", [D, T], BF16,
                             kind="ExternalOutput").ap()

    d_k_in = nc.dram_tensor("d_k_in", [D, T], BF16).ap()
    d_k = nc.dram_tensor("d_k", [4 * D, T], BF16).ap()
    d_v_in = nc.dram_tensor("d_v_in", [T, 2080], BF16).ap()
    d_v = nc.dram_tensor("d_v", [4 * T, 2080], BF16).ap()
    d_w1t_in = nc.dram_tensor("d_w1t_in", [256, DFF], BF16).ap()
    d_w1t = nc.dram_tensor("d_w1t", [D, DFF], BF16).ap()
    d_w2t_in = nc.dram_tensor("d_w2t_in", [1024, D], BF16).ap()
    d_w2t = nc.dram_tensor("d_w2t", [DFF, D], BF16).ap()
    d_q = nc.dram_tensor("d_q", [D, T], BF16).ap()
    d_s = nc.dram_tensor("d_s", [D, T], BF16).ap()
    d_sum_in = nc.dram_tensor("d_sum_in", [1, 2], F32).ap()
    d_sum = nc.dram_tensor("d_sum", [1, 2], F32).ap()

    from contextlib import ExitStack
    ctx = ExitStack()
    with ctx:
        def sbt(name, shape, dt):
            return ctx.enter_context(nc.sbuf_tensor(name, shape, dt))

        def pst(name, shape, dt):
            return ctx.enter_context(nc.psum_tensor(name, shape, dt))

        xnbuf = sbt("xnbuf", [128, NCH * T], BF16)
        q2 = sbt("q2", [128, 2048], BF16)
        hbuf = sbt("hbuf", [128, 60480], BF16)
        wbufA = sbt("wbufA", [128, 6144], BF16)
        wbufB = sbt("wbufB", [128, 6144], BF16)
        fa = sbt("fa", [128, 2048], F32)
        fb = sbt("fb", [128, 2048], F32)


        s_bq = sbt("s_bq", [128, 16], F32)
        s_bk = sbt("s_bk", [128, 16], F32)
        s_bo = sbt("s_bo", [128, 16], F32)
        s_b1 = sbt("s_b1", [128, 64], F32)
        s_b2 = sbt("s_b2", [128, 16], F32)
        ident = sbt("ident", [128, 128], BF16)
        bvrow = sbt("bvrow", [1, 2048], BF16)
        ones_t = sbt("ones_t", [1, 128], BF16)
        ones1 = sbt("ones1", [1, 128], F32)
        ones128 = sbt("ones128", [128, 1], F32)
        ones64f = sbt("ones64f", [1, 64], F32)
        stat = sbt("stat", [128, 16], F32)
        epst = sbt("epst", [128, 1], F32)
        acc = sbt("acc", [128, 2], F32)
        acct = sbt("acct", [128, 1], F32)
        ssmall = sbt("ssmall", [1, 8], F32)
        inv_bc = sbt("inv_bc", [128, 4], F32)
        recv = wbufB[:, 4096:6144].bitcast(F32)
        recA = recv[0:1, 0:512]
        recB = recv[0:1, 512:1024]
        bcbufA = fa[0:64, 0:512]
        bcbufB = fa[0:64, 512:1024]

        H_XN = 0
        H_KT = 0
        H_VST = 2048
        H_TU, H_TV, H_TT = 0, 2048, 4096
        H_WST = 6144
        H_H = 0
        H_KB = 32768
        H_VB = 36864
        H_EA = 41024
        H_FE = 44096
        H_PBF = 44096

        P_a = pst("P_a", [128, 1024], F32)
        P_b = pst("P_b", [128, 1024], F32)
        P_c = pst("P_c", [128, 512], F32)
        P_d = pst("P_d", [128, 512], F32)
        P_f = pst("P_f", [128, 1024], BF16)
        P_h = pst("P_h", [128, 512], F32)

        SEM.clear()
        for s in SEMNAMES:
            SEM[s] = ctx.enter_context(nc.semaphore(s))

        P = Plan()
        C = P.ctr

        P.op("gpsimd", lambda e: e.memset(ident[:, :], 0.0))
        v_id = P.op("gpsimd", lambda e: e.affine_select(
            out=ident[:, :], in_=ident[:, :], compare_op=ALU.not_equal,
            fill=1.0, base=0, pattern=[[-1, 128]], channel_multiplier=1),
            inc="gp")
        P.op("gpsimd", lambda e: e.memset(ones1[:, :], 1.0))
        P.op("gpsimd", lambda e: e.memset(ones_t[:, :], 1.0))
        v_bvrow = P.dma("gpsimd", bvrow[:, :], bv_row_d[:, :])
        P.op("gpsimd", lambda e: e.memset(ones128[:, :], 1.0))
        P.op("gpsimd", lambda e: e.memset(ones64f[:, :], 1.0))
        v_ones = P.op("gpsimd", lambda e: e.memset(epst[:, :], 1e-5), inc="gp")

        toks_bias = [P.dma("sync", s_bq[:, :], bq_d[:, :]),
                     P.dma("sync", s_bk[:, :], bk_d[:, :]),
                     P.dma("sync", s_bo[:, :], bo_d[:, :]),
                     P.dma("sync", s_b1[:, :], b1_d[:, :]),
                     P.dma("sync", s_b2[:, :], b2_d[:, :])]

        xnbf = hbuf[:, H_XN:H_XN + 2048]
        xnbf_free = ("mm", 0)
        pf_evac = {0: 0, 1: 0}
        v_tph = 0
        xin_free = {0: ("vec", 0)}
        stat_free = {}
        for i in range(8):
            xin = fa[:, :]
            sb0 = (i % 2) * 8
            P.wait("sync", xin_free[0][0], xin_free[0][1])
            vl = P.dma("sync", xin, x_d[i * 128:(i + 1) * 128, :])
            P.wait("vector", vl)
            P.wait("vector", "act", stat_free.get(i % 2, 0))
            P.op("vector", lambda e, xin=xin, sb0=sb0: e.reduce_sum(
                stat[:, sb0:sb0 + 1], xin, axis=AX.X, negate=True), inc="vec")
            P.wait("vector", "vec", C["vec"])
            v_nm = P.op("vector", lambda e, sb0=sb0: e.tensor_scalar_mul(
                stat[:, sb0 + 1:sb0 + 2], stat[:, sb0:sb0 + 1],
                1.0 / 2048.0), inc="vec")
            P.wait("scalar", "vec", v_nm)
            P.op("scalar", lambda e, xin=xin, sb0=sb0: e.activation(
                out=fb[:, :], in_=xin, func=AF.Identity,
                bias=stat[:, sb0 + 1:sb0 + 2], scale=1.0), inc="act")
            P.wait("scalar", "act", C["act"])
            v_sqr = P.op("scalar", lambda e, xin=xin, sb0=sb0: e.activation(
                out=xin, in_=fb[:, :], func=AF.Square,
                accum_out=stat[:, sb0 + 2:sb0 + 3]), inc="act")
            xin_free[0] = ("act", v_sqr)
            P.wait("scalar", "act", C["act"])
            if i == 0:
                P.wait("scalar", "gp", v_ones)
            v_std = P.op("scalar", lambda e, sb0=sb0: e.activation(
                out=stat[:, sb0 + 3:sb0 + 4], in_=stat[:, sb0 + 2:sb0 + 3],
                func=AF.Sqrt, bias=epst[:, 0:1], scale=1.0 / 2048.0), inc="act")
            stat_free[i % 2] = v_std
            P.wait("vector", "act", v_std)
            v_rstd = P.op("vector", lambda e, sb0=sb0: e.reciprocal(
                stat[:, sb0 + 4:sb0 + 5], stat[:, sb0 + 3:sb0 + 4]), inc="vec")
            P.wait("vector", "vec", C["vec"])
            v_rstd2 = P.op("vector", lambda e, sb0=sb0: e.tensor_copy(
                stat[:, sb0 + 5:sb0 + 6], stat[:, sb0 + 4:sb0 + 5]), inc="vec")
            P.wait("scalar", "vec", v_rstd2)
            P.wait("scalar", xnbf_free[0], xnbf_free[1])
            v_xn = P.op("scalar", lambda e, sb0=sb0: e.activation(
                out=xnbf[:, :], in_=fb[:, :], func=AF.Identity,
                scale=stat[:, sb0 + 5:sb0 + 6]), inc="act")
            v_xn_last = v_xn
            P.wait("tensor", "act", v_xn)
            if i == 0:
                P.wait("tensor", "gp", v_id)
            for b4 in range(4):
                half = b4 % 2
                P.wait("tensor", "vec", pf_evac[half])
                for k in range(4):
                    kk = b4 * 4 + k
                    fn = lambda e, kk=kk, k=k, half=half: e.transpose(
                        P_f[:, half * 512 + k * 128: half * 512 + k * 128 + 128],
                        xnbf[:, kk * 128:kk * 128 + 128], ident[:, :])
                    if k == 3:
                        v_tph = P.op("tensor", fn, inc="mm")
                    else:
                        P.op("tensor", fn)
                P.wait("vector", "mm", v_tph)
                out3 = xnbuf[:, b4 * 4 * T: (b4 * 4 + 4) * T].rearrange(
                    "p (c t) -> p c t", t=T)[:, :, i * 128:i * 128 + 128]
                in3 = P_f[:, half * 512: half * 512 + 512].rearrange(
                    "p (c t) -> p c t", t=128)
                vev = P.op(
                    "vector", lambda e, o=out3, ii=in3: e.tensor_copy(o, ii),
                    inc="vec")
                pf_evac[0] = pf_evac[1] = vev
            xnbf_free = ("mm", v_tph)
        fa_free = xin_free[0]
        v_xnT_done = max(pf_evac.values())

        def band_ap(w_ap, jd):
            return w_ap[:, jd * 128:(jd + 1) * 128].rearrange(
                "(i p) d -> p i d", p=128)

        def chunked(sb_ap, inner):
            return sb_ap.rearrange("p (i d) -> p i d", d=inner)

        v_vst_ms = 0
        for tt in range(8):
            v_vst_ms = P.op("gpsimd", lambda e, tt=tt: e.memset(
                hbuf[:, H_VST + tt * 2080: H_VST + (tt + 1) * 2080], 1.0),
                inc="gp")
        v_vev = {}
        H_WVB = 18688
        toks_wv = []
        for q in range(4):
            toks_wv.append(P.dma(
                "gpsimd",
                chunked(hbuf[:, H_WVB + q * 8192: H_WVB + (q + 1) * 8192], 512),
                wv_d[:, q * 512:(q + 1) * 512].rearrange(
                    "(i p) d -> p i d", p=128)))
        vslots = [P_c[:, :], P_d[:, :], P_a[:, 0:512], P_a[:, 512:1024]]
        vpsum_free = {q: ("vec", 0) for q in range(4)}
        v_vmm = 0
        for tt in range(8):
            P.wait("tensor", toks_wv)
            if tt == 0:
                P.wait("tensor", "vec", v_xnT_done)
                P.wait("tensor", v_bvrow)
                P.wait("tensor", "gp", v_ones)
            for q in range(4):
                P.wait("tensor", vpsum_free[q][0], vpsum_free[q][1])
            for i in range(16):
                for q in range(4):
                    fn = (lambda e, i=i, q=q, tt=tt: e.matmul(
                        vslots[q],
                        xnbuf[:, i * T + tt * 128: i * T + tt * 128 + 128],
                        hbuf[:, H_WVB + q * 8192 + i * 512:
                             H_WVB + q * 8192 + (i + 1) * 512],
                        start=(i == 0), stop=False))
                    P.op("tensor", fn)
            for q in range(4):
                fn = (lambda e, q=q: e.matmul(
                    vslots[q], ones_t[0:1, :],
                    bvrow[0:1, q * 512:(q + 1) * 512],
                    start=False, stop=True))
                if q == 3:
                    v_vmm = P.op("tensor", fn, inc="mm")
                else:
                    P.op("tensor", fn)
            P.wait("vector", "mm", v_vmm)
            if tt == 0:
                P.wait("vector", "gp", v_vst_ms)
            vv = 0
            for q in range(4):
                base = H_VST + tt * 2080 + q * 520
                o3 = hbuf[:, base: base + 520].rearrange(
                    "p (h e) -> p h e", e=65)[:, :, 0:64]
                i3 = vslots[q].rearrange("p (h e) -> p h e", e=64)
                vv = P.op("vector", lambda e, o3=o3, i3=i3: e.tensor_copy(
                    o3, i3), inc="vec")
                vpsum_free[q] = ("vec", vv)
            v_vev[(tt, 3)] = vv
        toks_vst = []
        for tt in range(8):
            P.wait("sync", "vec", v_vev[(tt, 3)])
            toks_vst.append(P.dma(
                "sync", d_v_in[tt * 128:(tt + 1) * 128, :],
                hbuf[:, H_VST + tt * 2080: H_VST + (tt + 1) * 2080]))
        P.wait("gpsimd", toks_vst)
        v_agv = P.collective("ccv", "AllGather", ALU.bypass, GROUPS_KV, d_v_in, d_v)

        v_vevac_last = max(vpsum_free[q][1] for q in range(4))
        band_free = {0: ("gp", 0), 1: ("gp", 0)}
        psum_free = {0: ("vec", v_vevac_last), 1: ("vec", v_vevac_last)}
        kst_free = {0: None, 1: None}
        toks_kst = []
        v_kmm_last = 0
        for jd in range(16):
            p = jd % 2
            wb = (wbufA if p == 0 else wbufB)[:, 0:2048]
            P.wait("gpsimd", band_free[p][0], band_free[p][1])
            vw = P.dma("gpsimd", chunked(wb, 128), band_ap(wk_d, jd))
            P.wait("tensor", vw)
            P.wait("tensor", psum_free[p][0], psum_free[p][1])
            P_x = P_a if p == 0 else P_b
            for hh in range(2):
                for i in range(16):
                    fn = lambda e, wb=wb, i=i, hh=hh, P_x=P_x: e.matmul(
                        P_x[:, hh * 512:(hh + 1) * 512],
                        wb[:, i * 128:(i + 1) * 128],
                        xnbuf[:, i * T + hh * 512: i * T + (hh + 1) * 512],
                        start=(i == 0), stop=(i == 15))
                    if hh == 1 and i == 15:
                        v_kmm = P.op("tensor", fn, inc="mm")
                    else:
                        P.op("tensor", fn)
            band_free[p] = ("mm", v_kmm)
            v_kmm_last = v_kmm
            P.wait("scalar", "mm", v_kmm)
            P.wait("scalar", kst_free[p])
            if jd == 0:
                P.wait("scalar", toks_bias)
            if jd < 2:
                P.wait("scalar", "mm", v_tph)
            kts = hbuf[:, H_KT + p * 1024: H_KT + (p + 1) * 1024]
            v_kev = P.op("scalar", lambda e, P_x=P_x, jd=jd, kts=kts: e.activation(
                out=kts, in_=P_x[:, :], func=AF.Identity,
                bias=s_bk[:, jd:jd + 1], scale=1.0), inc="act")
            psum_free[p] = ("act", v_kev)
            P.wait("sync", "act", v_kev)
            v_kst = P.dma("sync", d_k_in[jd * 128:(jd + 1) * 128, :], kts)
            toks_kst.append(v_kst)
            kst_free[p] = v_kst
        P.wait("gpsimd", toks_kst)
        v_agk = P.collective("cck", "AllGather", ALU.bypass, GROUPS_KV, d_k_in, d_k)


        qband_free = {0: band_free[0], 1: band_free[1]}
        qpsum_free = {0: psum_free[0], 1: psum_free[1]}
        qst_free = {0: kst_free[0], 1: kst_free[1]}
        toks_qst = []
        v_qev = 0
        for jd in range(16):
            p = jd % 2
            wb = (wbufA if p == 0 else wbufB)[:, 0:2048]
            P.wait("gpsimd", qband_free[p][0], qband_free[p][1])
            vw = P.dma("gpsimd", chunked(wb, 128), band_ap(wq_d, jd))
            P.wait("tensor", vw)
            P.wait("tensor", qpsum_free[p][0], qpsum_free[p][1])
            P_x = P_a if p == 0 else P_b
            for hh in range(2):
                for i in range(16):
                    fn = lambda e, wb=wb, i=i, hh=hh, P_x=P_x: e.matmul(
                        P_x[:, hh * 512:(hh + 1) * 512],
                        wb[:, i * 128:(i + 1) * 128],
                        xnbuf[:, i * T + hh * 512: i * T + (hh + 1) * 512],
                        start=(i == 0), stop=(i == 15))
                    if hh == 1 and i == 15:
                        v_qmm = P.op("tensor", fn, inc="mm")
                    else:
                        P.op("tensor", fn)
            qband_free[p] = ("mm", v_qmm)
            P.wait("scalar", "mm", v_qmm)
            P.wait("scalar", qst_free[p])
            qts = hbuf[:, H_KT + p * 1024: H_KT + (p + 1) * 1024]
            v_qev = P.op("scalar", lambda e, P_x=P_x, jd=jd, qts=qts: e.activation(
                out=qts, in_=P_x[:, :],
                func=AF.Identity, bias=s_bq[:, jd:jd + 1], scale=1.0), inc="act")
            qpsum_free[p] = ("act", v_qev)
            P.wait("sync", "act", v_qev)
            v_qst = P.dma("sync", d_q[jd * 128:(jd + 1) * 128, :], qts)
            toks_qst.append(v_qst)
            qst_free[p] = v_qst

        v_acc0 = P.op("vector", lambda e: e.memset(acc[:, :], 0.0), inc="vec")
        fa_free = xin_free[0]
        fb_free = ("act", v_xn_last)
        for which, src_ap, ntile, ng, tw in ((0, w1s_d, 8, 8, 256),
                                             (1, w2s_d, 8, 2, 1024)):
            rows = ng * 128
            for k in range(ntile):
                buf, free = (fa, fa_free) if k % 2 == 0 else (fb, fb_free)
                P.wait("sync", free[0], free[1])
                vl = P.dma("sync", buf[:, :].rearrange("p (g c) -> p g c", c=tw),
                           src_ap[k * rows:(k + 1) * rows, :].rearrange(
                               "(g p) c -> p g c", p=128))
                P.wait("vector", vl)
                P.op("vector", lambda e, buf=buf: e.reduce_sum(
                    acct[:, 0:1], buf[:, :], axis=AX.X,
                    apply_absolute_value=True), inc="vec")
                if k % 2 == 0:
                    fa_free = ("vec", C["vec"])
                else:
                    fb_free = ("vec", C["vec"])
                P.wait("vector", "vec", C["vec"])
                P.op("vector", lambda e, w=which: e.tensor_add(
                    acc[:, w:w + 1], acc[:, w:w + 1], acct[:, 0:1]), inc="vec")
        v_abs_done = C["vec"]
        P.wait("tensor", "vec", v_abs_done)
        P.wait("tensor", "vec", vpsum_free[0][1])
        v_sum_mm = P.op("tensor", lambda e: e.matmul(
            P_c[0:1, 0:2], ones128[:, 0:1], acc[:, :], start=True, stop=True),
            inc="mm")
        P.wait("vector", "mm", v_sum_mm)
        v_sum_cp = P.op("vector", lambda e: e.tensor_copy(
            ssmall[0:1, 0:2], P_c[0:1, 0:2]), inc="vec")
        P.wait("sync", "vec", v_sum_cp)
        v_sum_st = P.dma("sync", d_sum_in[:, :], ssmall[0:1, 0:2])
        P.wait("gpsimd", v_sum_st)
        v_ar = P.collective("ccr", "AllReduce", ALU.add, GROUP_ALL, d_sum_in, d_sum)

        P.wait("sync", "ccr", v_ar)
        v_sumld = P.dma("sync", ssmall[0:1, 2:4], d_sum[:, :])
        P.wait("vector", v_sumld)
        P.op("vector", lambda e: e.tensor_scalar(
            out=ssmall[0:1, 4:6], in0=ssmall[0:1, 2:4], scalar1=1.0 / NMAT,
            scalar2=1e-8, op0=ALU.mult, op1=ALU.max), inc="vec")
        P.wait("vector", "vec", C["vec"])
        v_inv = P.op("vector", lambda e: e.reciprocal(
            ssmall[0:1, 6:8], ssmall[0:1, 4:6]), inc="vec")
        P.wait("vector", "vec", C["vec"])
        P.wait("tensor", "vec", v_inv)
        P.wait("tensor", vpsum_free[1][0], vpsum_free[1][1])
        v_invb = P.op("tensor", lambda e: e.matmul(
            P_d[:, 0:2], ones1[0:1, :], ssmall[0:1, 6:8],
            start=True, stop=True), inc="mm")
        P.wait("vector", "mm", v_invb)
        P.op("vector", lambda e: e.tensor_scalar_mul(
            inv_bc[:, 0:2], P_d[:, 0:2], BIG), inc="vec")
        v_invbc = P.op("vector", lambda e: e.tensor_scalar_mul(
            inv_bc[:, 2:4], P_d[:, 0:2], -BIG), inc="vec")

        v_wst = {}
        hp2 = [0]
        pf_free = {0: ("vec", pf_evac[0]), 1: ("vec", pf_evac[1])}
        tern_free = ("mm", 0)
        v_w1std = v_w2std = 0
        for which, src_ap, ntile, ng, tw in ((0, w1s_d, 8, 8, 256),
                                             (1, w2s_d, 8, 2, 1024)):
            rows = ng * 128
            wide = ng * tw
            ntp = tw // 128
            v_tp = 0
            for k in range(ntile):
                P.wait("sync", fa_free[0], fa_free[1])
                if which == 0 and k == 0:
                    P.wait("sync", "vec", v_vev[(7, 3)])
                vl = P.dma("sync", fa[:, :].rearrange("p (g c) -> p g c", c=tw),
                           src_ap[k * rows:(k + 1) * rows, :].rearrange(
                               "(g p) c -> p g c", p=128))
                P.wait("vector", vl)
                P.wait("vector", "vec", v_invbc)
                if which == 0 and k == 0:
                    P.wait("vector", toks_vst)
                    P.wait("vector", toks_kst)
                    P.wait("vector", "act", v_xn_last)
                P.wait("vector", tern_free[0], tern_free[1])
                m = which
                P.op("vector", lambda e, m=m: e.tensor_scalar(
                    out=fb[:, :], in0=fa[:, :], scalar1=inv_bc[:, m:m + 1],
                    scalar2=-HALFBIG, op0=ALU.mult, op1=ALU.add), inc="vec")
                P.wait("vector", "vec", C["vec"])
                P.op("vector", lambda e: e.tensor_scalar(
                    out=hbuf[:, H_TU:H_TU + 2048], in0=fb[:, :], scalar1=0.0,
                    scalar2=1.0, op0=ALU.max, op1=ALU.min), inc="vec")
                P.wait("vector", "vec", C["vec"])
                v_u2 = P.op("vector", lambda e, m=m: e.tensor_scalar(
                    out=fb[:, :], in0=fa[:, :],
                    scalar1=inv_bc[:, 2 + m:3 + m],
                    scalar2=-HALFBIG, op0=ALU.mult, op1=ALU.add), inc="vec")
                fa_free = ("vec", v_u2)
                P.wait("vector", "vec", C["vec"])
                P.op("vector", lambda e: e.tensor_scalar(
                    out=hbuf[:, H_TV:H_TV + 2048], in0=fb[:, :], scalar1=0.0,
                    scalar2=1.0, op0=ALU.max, op1=ALU.min), inc="vec")
                P.wait("vector", "vec", C["vec"])
                v_tt = P.op("vector", lambda e: e.tensor_tensor(
                    out=hbuf[:, H_TT:H_TT + 2048], in0=hbuf[:, H_TU:H_TU + 2048],
                    in1=hbuf[:, H_TV:H_TV + 2048], op=ALU.subtract), inc="vec")
                P.wait("tensor", "vec", v_tt)
                cw = 8192 if which == 0 else 2048
                for g in range(ng):
                    kk = k * ng + g
                    nsub = (ntp + 3) // 4
                    for sub in range(nsub):
                        j0 = sub * 4
                        nj = min(4, ntp - j0)
                        half = hp2[0] % 2
                        hp2[0] += 1
                        P.wait("tensor", pf_free[half][0], pf_free[half][1])
                        for jj in range(nj):
                            j = j0 + jj
                            fn = (lambda e, g=g, j=j, jj=jj, half=half, tw=tw:
                                  e.transpose(
                                      P_f[:, half * 512 + jj * 128:
                                          half * 512 + (jj + 1) * 128],
                                      hbuf[:, H_TT + g * tw + j * 128:
                                           H_TT + g * tw + (j + 1) * 128],
                                      ident[:, :]))
                            if jj == nj - 1:
                                v_tp = P.op("tensor", fn, inc="mm")
                            else:
                                P.op("tensor", fn)
                        P.wait("vector", "mm", v_tp)
                        if which == 1 and kk == 0 and sub == 0:
                            P.wait("vector", toks_w1st)
                        o3 = hbuf[:, H_WST:H_WST + ntp * cw].rearrange(
                            "p (j c) -> p j c", c=cw)[
                                :, j0:j0 + nj, kk * 128:kk * 128 + 128]
                        i3 = P_f[:, half * 512: half * 512 + nj * 128].rearrange(
                            "p (j c) -> p j c", c=128)
                        vev = ("vec", P.op(
                            "vector", lambda e, o3=o3, i3=i3: e.tensor_copy(
                                o3, i3), inc="vec"))
                        pf_free[0] = pf_free[1] = vev
                tern_free = ("mm", v_tp)
            nrow = ntp
            cw = 8192 if which == 0 else 2048
            dst = d_w1t_in if which == 0 else d_w2t_in
            toks_std = []
            for j in range(nrow):
                P.wait("sync", "vec", pf_free[0][1])
                P.wait("sync", "vec", pf_free[1][1])
                toks_std.append(P.dma(
                    "sync", dst[j * 128:(j + 1) * 128, :],
                    hbuf[:, H_WST + j * cw:H_WST + (j + 1) * cw]))
            if which == 0:
                toks_w1st = toks_std
                P.wait("gpsimd", toks_w1st)
                v_agw1 = P.collective("cw1", "AllGather", ALU.bypass, GROUP_ALL,
                                      d_w1t_in, d_w1t)
            else:
                toks_w2st = toks_std
                P.wait("gpsimd", toks_w2st)
                v_agw2 = P.collective("cw2", "AllGather", ALU.bypass, GROUP_ALL,
                                      d_w2t_in, d_w2t)

        kvld = {}
        feats_evac = {}
        feats_mm = {}
        exp_ms2 = {}
        bc_copy = {}
        v_feats_evac_last = 0
        v_fb_last0 = C["vec"]

        F1_START = 2 * 64
        f1 = {"emitted": 0, "slot": 0, "v_hmm": {}, "v_hev": {},
              "w1_free": {0: qband_free[0], 1: qband_free[1]}, "toks": {}}

        def f1_load(sb_):
            par = sb_ % 2
            wb = (wbufA if par == 0 else wbufB)[:, 0:4096]
            P.wait("gpsimd", f1["w1_free"][par][0], f1["w1_free"][par][1])
            if sb_ < 2:
                P.wait("gpsimd", "cw1", 1)
            f1["toks"][sb_] = P.dma(
                "gpsimd", wb.rearrange("p (i c) -> p i c", c=256),
                d_w1t[:, sb_ * 256:(sb_ + 1) * 256].rearrange(
                    "(i p) c -> p i c", p=128))

        def f1_emit_unit():
            u = f1["emitted"]
            if u >= 1024:
                return
            f1["emitted"] += 1
            b, i = u // 16, u % 16
            sb_ = b // 2
            l = b % 2
            par = sb_ % 2
            wb = (wbufA if par == 0 else wbufB)[:, 0:4096]
            if i == 0:
                P.wait("tensor", f1["toks"][sb_])
                if b > 0:
                    P.wait("tensor", "vec", f1["v_hev"][b - 1])
            fn = (lambda e, wb=wb, i=i, l=l: e.matmul(
                P_h[:, :], wb[:, i * 256 + l * 128: i * 256 + (l + 1) * 128],
                xnbuf[:, i * T: i * T + 512],
                start=(i == 0), stop=(i == 15)))
            if i < 15:
                P.op("tensor", fn)
            else:
                f1["v_hmm"][b] = P.op("tensor", fn, inc="mm")
                f1["w1_free"][par] = ("mm", f1["v_hmm"][b])
                if l == 1 and sb_ + 2 < 32:
                    f1_load(sb_ + 2)
                P.wait("vector", "mm", f1["v_hmm"][b])
                if b == 0:
                    P.wait("vector", toks_w1st)
                    P.wait("vector", toks_w2st)
                    P.wait("vector", toks_vst)
                    P.wait("vector", toks_kst)
                    P.wait("vector", toks_qst)
                    P.wait("vector", "mm", tern_free[1])
                f1["v_hev"][b] = P.op(
                    "vector", lambda e, b=b: e.tensor_scalar_add(
                        hbuf[:, H_H + b * 512:H_H + (b + 1) * 512],
                        P_h[:, :], s_b1[:, b:b + 1]), inc="vec")

        def f1_slot():
            s = f1["slot"]
            f1["slot"] += 1
            if s < F1_START:
                return
            target = min(1024, ((s - F1_START + 1) * 1024 + 767) // 768)
            while f1["emitted"] < target:
                f1_emit_unit()

        f1_load(0)
        f1_load(1)

        for j in range(PAIRS):
            kb = hbuf[:, H_KB: H_KB + 4096]
            vb = hbuf[:, H_VB: H_VB + 4160]
            qp = j % 2
            qsl = q2[:, qp * 1024:(qp + 1) * 1024]
            if j >= 1:
                P.wait("sync", "mm", feats_mm[(j - 1, 1)])
            if j < 2:
                P.wait("sync", "cck", v_agk)
                P.wait("sync", "ccv", v_agv)
                P.wait("sync", toks_qst)
            toks_kv = [P.dma("sync", qsl, d_q[j * 128:(j + 1) * 128, :])]
            for r in range(4):
                toks_kv.append(P.dma(
                    "sync", kb[:, r * 1024:(r + 1) * 1024],
                    d_k[r * 2048 + j * 128: r * 2048 + (j + 1) * 128, :]))
            for r in range(4):
                toks_kv.append(P.dma(
                    "sync", chunked(vb[:, r * 1040:(r + 1) * 1040], 130),
                    d_v[r * 1024:(r + 1) * 1024,
                        130 * j:130 * (j + 1)].rearrange(
                            "(c p) m -> p c m", p=128)))
            kvld[j] = toks_kv

            for qt in range(2):
                exp_ms = {}
                mmg = {}
                mm2g = {}
                q0 = qt * 512
                expb = [hbuf[:, H_EA + par * 1024: H_EA + (par + 1) * 1024]
                        for par in range(3)]

                def mm2_pair(kc):
                    par = kc % 3
                    for (head, Pfe) in ((0, P_c), (1, P_d)):
                        cbase = kc * 130 + head * 65
                        fn = (lambda e, vb=vb, cbase=cbase, Pfe=Pfe,
                              par=par, head=head, kc=kc: e.matmul(
                                  Pfe[0:65, :],
                                  vb[:, cbase:cbase + 65],
                                  expb[par][:, head * 512:(head + 1) * 512],
                                  start=(kc == 0), stop=(kc == 31)))
                        yield fn

                for kc in range(32):
                    par = kc % 2
                    Psc = P_a if par == 0 else P_b
                    if kc == 0:
                        if qt == 0:
                            P.wait("tensor", kvld[j])
                            if j == 0:
                                P.wait("tensor", "act", v_qev)
                        if (j, qt) != (0, 0):
                            prev = (j, 0) if qt == 1 else (j - 1, 1)
                            P.wait("tensor", "act", exp_ms2.get(prev, 0))
                    else:
                        P.wait("tensor", "act", exp_ms.get(kc - 2, 0))
                    v_g = 0
                    for head, rl, rh in ((0, 0, 64), (1, 64, 128)):
                        fn = (lambda e, kb=kb, kc=kc, Psc=Psc, rl=rl, rh=rh,
                              qsl=qsl, q0=q0, head=head: e.matmul(
                                  Psc[:, head * 512:(head + 1) * 512],
                                  kb[rl:rh, kc * 128:(kc + 1) * 128],
                                  qsl[rl:rh, q0: q0 + 512],
                                  start=True, stop=True,
                                  tile_position=(rl, 0)))
                        if head == 1:
                            v_g = P.op("tensor", fn, inc="mm")
                        else:
                            P.op("tensor", fn)
                    mmg[kc] = v_g
                    if kc >= 2:
                        if kc == 2:
                            prev = (j, 0) if qt == 1 else (j - 1, 1)
                            if (j, qt) != (0, 0):
                                P.wait("tensor", "vec", feats_evac.get(prev, 0))
                        fns = list(mm2_pair(kc - 2))
                        for idx, fn in enumerate(fns):
                            if idx == len(fns) - 1:
                                mm2g[kc - 2] = P.op("tensor", fn, inc="mm")
                            else:
                                P.op("tensor", fn)
                    f1_slot()
                    P.wait("scalar", "mm", mmg[kc])
                    if kc >= 3:
                        P.wait("scalar", "mm", mm2g.get(kc - 3, 0))
                    exp_ms[kc] = P.op("scalar", lambda e, par3=kc % 3, Psc=Psc:
                                      e.activation(out=expb[par3][:, :],
                                                   in_=Psc[:, :], func=AF.Exp,
                                                   scale=SCALE), inc="act")
                exp_ms2[(j, qt)] = exp_ms[31]
                P.wait("tensor", "act", exp_ms[31])
                v_feats = 0
                fns = list(mm2_pair(30)) + list(mm2_pair(31))
                for idx, fn in enumerate(fns):
                    if idx == len(fns) - 1:
                        v_feats = P.op("tensor", fn, inc="mm")
                    else:
                        P.op("tensor", fn)
                P.wait("vector", "mm", v_feats)
                P.op("vector", lambda e: e.reciprocal(
                    recA, P_c[64:65, :]), inc="vec")
                v_rec = P.op("vector", lambda e: e.reciprocal(
                    recB, P_d[64:65, :]), inc="vec")
                P.wait("tensor", "vec", v_rec)
                P.op("tensor", lambda e: e.matmul(
                    P_c[64:128, :], ones64f[0:1, :], recA,
                    start=True, stop=True, tile_position=(0, 64)))
                v_bc = P.op("tensor", lambda e: e.matmul(
                    P_d[64:128, :], ones64f[0:1, :], recB,
                    start=True, stop=True, tile_position=(0, 64)), inc="mm")
                v_bc_last = v_bc
                P.wait("vector", "mm", v_bc)
                if j == 0 and qt == 0:
                    P.wait("vector", toks_w2st)
                fcol = j * T + q0
                P.op("vector", lambda e: e.tensor_copy(
                    bcbufA, P_c[64:128, :]), inc="vec")
                v_bcc = P.op("vector", lambda e: e.tensor_copy(
                    bcbufB, P_d[64:128, :]), inc="vec")
                P.op("vector", lambda e, fcol=fcol: e.tensor_tensor(
                    out=hbuf[0:64, H_FE + fcol:H_FE + fcol + 512],
                    in0=P_c[0:64, :], in1=bcbufA, op=ALU.mult), inc="vec")
                vfe = P.op("vector", lambda e, fcol=fcol: e.tensor_tensor(
                    out=hbuf[64:128, H_FE + fcol:H_FE + fcol + 512],
                    in0=P_d[0:64, :], in1=bcbufB, op=ALU.mult), inc="vec")
                bc_copy[(j, qt)] = v_bcc
                feats_evac[(j, qt)] = vfe
                feats_mm[(j, qt)] = v_feats
                v_feats_evac_last = vfe
        while f1["emitted"] < 1024:
            f1_emit_unit()
        P.wait("scalar", "vec", f1["v_hev"][63])
        v_h_done_th0 = 0
        for cgel in range(4):
            v_h_done_th0 = P.op("scalar", lambda e, cgel=cgel: e.activation(
                out=hbuf[:, cgel * 8192:(cgel + 1) * 8192],
                in_=hbuf[:, cgel * 8192:(cgel + 1) * 8192],
                func=AF.Gelu, scale=1.0), inc="act")

        v_dbg_fe = None
        if DEBUG:
            P.wait("sync", "vec", v_feats_evac_last)
            v_dbg_fe = [P.dma("sync", o_fe[:, :], hbuf[:, H_FE:H_FE + NCH * T])]

        wo_band_free = {0: f1["w1_free"][0], 1: f1["w1_free"][1]}
        wo_psum_free = {0: ("vec", v_feats_evac_last), 1: ("vec", v_feats_evac_last)}
        sst_free = {0: None, 1: None}
        toks_sst = []
        v_woev = {}
        v_womm_last = 0
        for jd in range(16):
            p = jd % 2
            wb = (wbufA if p == 0 else wbufB)[:, 0:2048]
            P.wait("gpsimd", wo_band_free[p][0], wo_band_free[p][1])
            vw = P.dma("gpsimd", chunked(wb, 128), band_ap(wo_d, jd))
            P.wait("tensor", vw)
            P.wait("tensor", wo_psum_free[p][0], wo_psum_free[p][1])
            if jd == 0:
                P.wait("tensor", "vec", v_feats_evac_last)
                if DEBUG:
                    P.wait("tensor", v_dbg_fe)
            P_x = P_a if p == 0 else P_b
            for hh in range(2):
                for i in range(16):
                    fe = H_FE + i * T + hh * 512
                    fn = lambda e, wb=wb, i=i, hh=hh, P_x=P_x, fe=fe: e.matmul(
                        P_x[:, hh * 512:(hh + 1) * 512],
                        wb[:, i * 128:(i + 1) * 128],
                        hbuf[:, fe:fe + 512],
                        start=(i == 0), stop=(i == 15))
                    if hh == 1 and i == 15:
                        v_womm = P.op("tensor", fn, inc="mm")
                    else:
                        P.op("tensor", fn)
            wo_band_free[p] = ("mm", v_womm)
            v_womm_last = v_womm
            P.wait("scalar", "mm", v_womm)
            P.wait("scalar", sst_free[p])
            if jd < 2:
                P.wait("scalar", "mm", feats_mm[(PAIRS - 1, 1)])
            sts = q2[:, p * 1024:(p + 1) * 1024]
            vv = P.op("scalar", lambda e, P_x=P_x, jd=jd, sts=sts: e.activation(
                out=sts, in_=P_x[:, :],
                func=AF.Identity, bias=s_bo[:, jd:jd + 1], scale=1.0), inc="act")
            v_woev[jd] = vv
            wo_psum_free[p] = ("act", vv)
            P.wait("sync", "act", vv)
            v_sst = P.dma("sync", d_s[jd * 128:(jd + 1) * 128, :], sts)
            toks_sst.append(v_sst)
            sst_free[p] = v_sst
        v_woev_last = v_woev[15]

        w1_free = {0: ("mm", 0), 1: ("mm", 0)}
        h_psum_free = {0: ("act", 0), 1: ("act", 0)}
        f2_psum_free = ("act", 0)
        fa_slot_free = {sl: ("vec", v_feats_evac_last) for sl in range(4)}
        s_tile_free = {sl: None for sl in range(4)}
        fb_slot_free = {sl: ("vec", v_feats_evac_last) for sl in range(4)}
        pbf_free = {0: ("mm", 0), 1: ("mm", 0)}
        pf2_free = {0: pf_free[0], 1: pf_free[1]}
        fout_free = {0: None, 1: None}
        w2_free = {0: ("mm", v_bc_last), 1: ("mm", v_bc_last)}
        v_out_dma = 0
        fout = hbuf[:, H_KB:H_KB + 2048].bitcast(F32)
        w1_free = dict(f1["w1_free"])
        v_f2_last_th0 = 0
        for th in range(2):
            t0 = th * 512
            if th == 0:
                v_h_done = v_h_done_th0
            else:
                v_hmm = {}
                v_gelu = {}
                for sb_ in range(32):
                    p = sb_ % 2
                    wb = (wbufA if p == 0 else wbufB)[:, 0:4096]
                    qeng = "sync" if sb_ % 2 == 0 else "gpsimd"
                    P.wait(qeng, w1_free[p][0], w1_free[p][1])
                    if sb_ < 2:
                        P.wait(qeng, "mm", v_womm_last)
                    toks_w1l = [P.dma(
                        qeng,
                        wb.rearrange("p (i c) -> p i c", c=256),
                        d_w1t[:, sb_ * 256:(sb_ + 1) * 256].rearrange(
                            "(i p) c -> p i c", p=128))]
                    for l in range(2):
                        bb = sb_ * 2 + l
                        P.wait("tensor", toks_w1l)
                        if bb == 0:
                            P.wait("tensor", "vec", f1["v_hev"][63])
                            P.wait("tensor", "mm", v_f2_last_th0)
                        else:
                            P.wait("tensor", "act", v_gelu[bb - 1])
                        for i in range(16):
                            fn = (lambda e, wb=wb, i=i, l=l, t0=t0: e.matmul(
                                P_h[:, :],
                                wb[:, i * 256 + l * 128:
                                   i * 256 + (l + 1) * 128],
                                xnbuf[:, i * T + t0: i * T + t0 + 512],
                                start=(i == 0), stop=(i == 15)))
                            if i == 15:
                                v_hmm[bb] = P.op("tensor", fn, inc="mm")
                            else:
                                P.op("tensor", fn)
                        P.wait("scalar", "mm", v_hmm[bb])
                        if DEBUG and bb == 0:
                            P.wait("scalar", v_dbg_h)
                        if bb == 0:
                            P.wait("scalar", "mm", v_f2_last_th0)
                        v_gelu[bb] = P.op(
                            "scalar", lambda e, bb=bb: e.activation(
                                out=hbuf[:, H_H + bb * 512:
                                         H_H + (bb + 1) * 512],
                                in_=P_h[:, :], func=AF.Gelu,
                                bias=s_b1[:, bb:bb + 1], scale=1.0), inc="act")
                    w1_free[p] = ("mm", v_hmm[sb_ * 2 + 1])
                v_h_done = v_gelu[63]
            if DEBUG and th == 0:
                P.wait("sync", "act", v_h_done)
                v_dbg_h = P.dma("sync", o_h[:, :], hbuf[:, H_H:H_H + 32768])
            f2slots = [P_a[:, 0:512], P_a[:, 512:1024],
                       P_b[:, 0:512], P_b[:, 512:1024]]
            for g in range(4):
                v_f2 = {}
                for fg in range(16):
                    p = fg % 2
                    w2b = (wbufA if p == 0 else wbufB)[:, 4096:6144]
                    P.wait("sync" if p == 0 else "gpsimd",
                           w2_free[p][0], w2_free[p][1])
                    if th == 0 and g == 0 and fg < 2:
                        P.wait("sync" if p == 0 else "gpsimd", "cw2", 1)
                    vld = P.dma(
                        "sync" if p == 0 else "gpsimd",
                        w2b.rearrange("p (f c) -> p f c", c=512),
                        d_w2t[fg * 512:(fg + 1) * 512,
                              g * 512:(g + 1) * 512].rearrange(
                                  "(f p) c -> p f c", p=128))
                    P.wait("tensor", vld)
                    if fg == 0:
                        P.wait("tensor", "act", v_h_done)
                        P.wait("tensor", f2_psum_free[0], f2_psum_free[1])
                        if th == 0 and g == 0:
                            P.wait("tensor", "act", v_woev_last)
                    for fl in range(4):
                        fbk = fg * 4 + fl
                        for sl in range(4):
                            fn = (lambda e, w2b=w2b, sl=sl, fl=fl, fbk=fbk:
                                  e.matmul(
                                      f2slots[sl],
                                      w2b[:, fl * 512 + sl * 128:
                                          fl * 512 + (sl + 1) * 128],
                                      hbuf[:, H_H + fbk * 512:
                                           H_H + (fbk + 1) * 512],
                                      start=(fbk == 0), stop=(fbk == 63)))
                            if sl == 3 and fl == 3:
                                v_f2[fg] = P.op("tensor", fn, inc="mm")
                            else:
                                P.op("tensor", fn)
                    w2_free[p] = ("mm", v_f2[fg])
                    if th == 0:
                        v_f2_last_th0 = v_f2[fg]
                v_last_evac = 0
                for sl in range(4):
                    jd = g * 4 + sl
                    ppar = jd % 2
                    P.wait("scalar", "mm", v_f2[15])
                    P.wait("scalar", fa_slot_free[sl][0], fa_slot_free[sl][1])
                    v_silu = P.op("scalar", lambda e, sl=sl, jd=jd: e.activation(
                        out=fa[:, sl * 512:(sl + 1) * 512], in_=f2slots[sl],
                        func=AF.Silu, bias=s_b2[:, jd:jd + 1], scale=1.0),
                        inc="act")
                    v_last_evac = v_silu
                    squ = q2[:, sl * 512:(sl + 1) * 512]
                    P.wait("sync", s_tile_free[sl])
                    if th == 0 and g == 0 and sl < 2:
                        P.wait("sync", toks_sst)
                    v_sld = P.dma("sync", squ,
                                  d_s[jd * 128:(jd + 1) * 128, t0:t0 + 512])
                    P.wait("vector", "act", v_silu)
                    P.wait("vector", v_sld)
                    P.wait("vector", pbf_free[ppar][0], pbf_free[ppar][1])
                    v_p = P.op("vector", lambda e, sl=sl, ppar=ppar, squ=squ:
                               e.tensor_tensor(
                                   out=hbuf[:, H_PBF + ppar * 512:
                                            H_PBF + (ppar + 1) * 512],
                                   in0=fa[:, sl * 512:(sl + 1) * 512],
                                   in1=squ,
                                   op=ALU.mult), inc="vec")
                    fa_slot_free[sl] = ("vec", v_p)
                    s_tile_free[sl] = ("vec", v_p)
                    P.wait("tensor", "vec", v_p)
                    P.wait("tensor", pf2_free[ppar][0], pf2_free[ppar][1])
                    v_tp5 = 0
                    for s4 in range(4):
                        fn = (lambda e, s4=s4, ppar=ppar: e.transpose(
                            P_f[:, ppar * 512 + s4 * 128:
                                ppar * 512 + (s4 + 1) * 128],
                            hbuf[:, H_PBF + ppar * 512 + s4 * 128:
                                 H_PBF + ppar * 512 + (s4 + 1) * 128],
                            ident[:, :]))
                        if s4 == 3:
                            v_tp5 = P.op("tensor", fn, inc="mm")
                        else:
                            P.op("tensor", fn)
                    pbf_free[ppar] = ("mm", v_tp5)
                    P.wait("sync", fb_slot_free[sl][0], fb_slot_free[sl][1])
                    vxl = P.dma("sync",
                                chunked(fb[:, sl * 512:(sl + 1) * 512], 128),
                                x_d[th * 512:(th + 1) * 512,
                                    jd * 128:(jd + 1) * 128].rearrange(
                                        "(s p) d -> p s d", p=128))
                    P.wait("vector", "mm", v_tp5)
                    P.wait("vector", vxl)
                    P.wait("vector", fout_free[ppar])
                    v_ot = P.op("vector", lambda e, sl=sl, ppar=ppar:
                                e.tensor_tensor(
                                    out=fout[:, ppar * 512:(ppar + 1) * 512],
                                    in0=P_f[:, ppar * 512:ppar * 512 + 512],
                                    in1=fb[:, sl * 512:(sl + 1) * 512],
                                    op=ALU.add), inc="vec")
                    pf2_free[0] = pf2_free[1] = ("vec", v_ot)
                    fb_slot_free[sl] = ("vec", v_ot)
                    P.wait("sync", "vec", v_ot)
                    v_out_dma = P.dma(
                        "sync",
                        out_d[th * 512:(th + 1) * 512,
                              jd * 128:(jd + 1) * 128].rearrange(
                                  "(s p) d -> p s d", p=128),
                        chunked(fout[:, ppar * 512:(ppar + 1) * 512], 128))
                    fout_free[ppar] = v_out_dma
                f2_psum_free = ("act", v_last_evac)

        if DEBUG:
            P.wait("sync", "vec", C["vec"])
            P.wait("sync", "act", C["act"])
            P.dma("sync", o_xn[:, :], xnbuf[:, :])
            for r in range(16):
                P.dma("sync", o_st[r * 128:(r + 1) * 128, :],
                      d_s[r * 128:(r + 1) * 128, :])
                P.dma("sync", o_q[r * 128:(r + 1) * 128, :],
                      d_q[r * 128:(r + 1) * 128, :])
            P.dma("sync", o_sum[:, :], ssmall[0:1, 2:4])
            P.dma("sync", o_kg[:, :], d_k[:, :])
            P.dma("sync", o_vg[:, :], d_v[:, :])
            for r in range(16):
                P.dma("sync", o_w1t[r * 128:(r + 1) * 128, :],
                      d_w1t[r * 128:(r + 1) * 128, :])
            for r in range(64):
                P.dma("sync", o_w2t[r * 128:(r + 1) * 128, :],
                      d_w2t[r * 128:(r + 1) * 128, :])

        for s in SQ:
            P.wait("sync", s, C[s])

        with nc.Block() as block:
            @block.sync
            def _(e):
                for fn in P.items["sync"]:
                    fn(e)

            @block.gpsimd
            def _(e):
                for fn in P.items["gpsimd"]:
                    fn(e)

            @block.tensor
            def _(e):
                for fn in P.items["tensor"]:
                    fn(e)

            @block.vector
            def _(e):
                for fn in P.items["vector"]:
                    fn(e)

            @block.scalar
            def _(e):
                for fn in P.items["scalar"]:
                    fn(e)

    return nc


_NC_CACHE = None


def kernel(L0_attn_x, L0_attn_Wq, L0_attn_bq, L0_attn_Wk, L0_attn_bk,
           L0_attn_Wv, L0_attn_bv, L0_attn_Wo, L0_attn_bo,
           L0_ffn_w1, L0_ffn_b1, L0_ffn_w2, L0_ffn_b2):
    global _NC_CACHE
    x = np.ascontiguousarray(np.asarray(L0_attn_x, dtype=np.float32))
    wq = np.ascontiguousarray(np.asarray(L0_attn_Wq, dtype=np.float32))
    wk = np.ascontiguousarray(np.asarray(L0_attn_Wk, dtype=np.float32))
    wv = np.ascontiguousarray(np.asarray(L0_attn_Wv, dtype=np.float32))
    wo = np.ascontiguousarray(np.asarray(L0_attn_Wo, dtype=np.float32))
    w1 = np.ascontiguousarray(np.asarray(L0_ffn_w1, dtype=np.float32))
    w2 = np.ascontiguousarray(np.asarray(L0_ffn_w2, dtype=np.float32))

    def fold(b, n):
        return np.ascontiguousarray(
            np.asarray(b, dtype=np.float32).reshape(n, 128).T)

    bq2 = fold(L0_attn_bq, 16)
    bk2 = fold(L0_attn_bk, 16)
    bo2 = fold(L0_attn_bo, 16)
    b12 = fold(L0_ffn_b1, 64)
    b22 = fold(L0_ffn_b2, 16)
    bvr = np.ascontiguousarray(
        np.asarray(L0_attn_bv, dtype=np.float32).reshape(1, D))

    if _NC_CACHE is None:
        _NC_CACHE = build_program()
    nc = _NC_CACHE

    in_maps = []
    for c in range(8):
        b, sl = c // 4, c % 4
        in_maps.append({
            "x": np.ascontiguousarray(x[b, sl * T:(sl + 1) * T, :]),
            "wq": wq, "wk": wk, "wv": wv, "wo": wo,
            "bq": bq2, "bk": bk2, "bo": bo2, "bv_row": bvr,
            "bo_row": np.ascontiguousarray(
                np.asarray(L0_attn_bo, dtype=np.float32).reshape(1, D)),
            "b1": b12, "b2": b22,
            "w1s": np.ascontiguousarray(w1[:, c * 256:(c + 1) * 256]),
            "w2s": np.ascontiguousarray(w2[:, c * 1024:(c + 1) * 1024]),
        })
    res = run_bass_kernel_spmd(nc, in_maps, core_ids=list(range(8)))
    out = np.empty((2, 4096, D), dtype=np.float32)
    for c in range(8):
        b, sl = c // 4, c % 4
        out[b, sl * T:(sl + 1) * T, :] = res.results[c]["out"]
    return out


# version 13
# speedup vs baseline: 1.1059x; 1.0037x over previous
import numpy as np
import ml_dtypes
import concourse.bass as bass
import concourse.mybir as mybir
from concourse.bass_utils import run_bass_kernel_spmd

F32 = mybir.dt.float32
BF16 = mybir.dt.bfloat16
AF = mybir.ActivationFunctionType
ALU = mybir.AluOpType
AX = mybir.AxisListType
bf = ml_dtypes.bfloat16

D = 2048
DFF = 8192
T = 1024
NCH = 16
PAIRS = 16
SCALE = float(1.0 / np.sqrt(32.0))
BIG = float(2.0 ** 63)
HALFBIG = float(0.5 * 2.0 ** 63)
NMAT = float(DFF * D)
GROUPS_KV = [[0, 1, 2, 3], [4, 5, 6, 7]]
GROUP_ALL = [[0, 1, 2, 3, 4, 5, 6, 7]]

DEBUG = False

SEM = {}
NSQ = 8
NGQ = 4
SQ = tuple(f"sq{i}" for i in range(NSQ))
GQ = tuple(f"gq{i}" for i in range(NGQ))
SC = tuple(f"sc{i}" for i in range(4))
SEMNAMES = SQ + GQ + SC + ("mm", "act", "vec", "gp",
                           "ccr", "cck", "ccv", "cw1", "cw2")


class Plan:

    ENGS = ("sync", "gpsimd", "tensor", "vector", "scalar")

    def __init__(self):
        self.items = {e: [] for e in self.ENGS}
        self.ctr = {s: 0 for s in SEMNAMES}
        self.rr = {"sync": 0, "gpsimd": 0, "scalar": 0}

    def add(self, eng, fn):
        self.items[eng].append(fn)

    def wait(self, eng, sem, val=None):
        if val is None:
            tok = sem
            if tok is None:
                return
            if isinstance(tok, list):
                best = {}
                for s, v in tok:
                    if s is not None and v > best.get(s, 0):
                        best[s] = v
                for s, v in best.items():
                    self.add(eng, lambda e, s=s, v=v: e.wait_ge(SEM[s], v))
                return
            sem, val = tok
        if sem is None or val is None or val <= 0:
            return
        self.add(eng, lambda e, s=sem, v=val: e.wait_ge(SEM[s], v))

    def dma(self, eng, out_ap, in_ap, **kw):
        pool = {"sync": SQ, "gpsimd": GQ, "scalar": SC}[eng]
        s = pool[self.rr[eng] % len(pool)]
        self.rr[eng] += 1
        self.ctr[s] += 16
        v = self.ctr[s]
        self.add(eng, lambda e, o=out_ap, i=in_ap, kw=dict(kw), s=s: e.dma_start(
            out=o, in_=i, **kw).then_inc(SEM[s], 16))
        return (s, v)

    def op(self, eng, fn, inc=None):
        if inc is None:
            self.add(eng, fn)
            return None
        self.ctr[inc] += 1
        v = self.ctr[inc]
        self.add(eng, lambda e, f=fn, s=inc: f(e).then_inc(SEM[s], 1))
        return v

    def collective(self, sem, kind, op_, groups, in_ap, out_ap):
        self.ctr[sem] += 1
        v = self.ctr[sem]
        self.add("gpsimd", lambda e, k=kind, o=op_, g=groups, i=in_ap, oo=out_ap,
                 s=sem: e.collective_compute(
                     k, o, replica_groups=g, ins=[i.opt()],
                     outs=[oo.opt()]).then_inc(SEM[s], 1))
        return v


def build_program():
    nc = bass.Bass("TRN2", target_bir_lowering=False)

    x_d = nc.dram_tensor("x", [T, D], F32, kind="ExternalInput").ap()
    wq_d = nc.dram_tensor("wq", [D, D], F32, kind="ExternalInput").ap()
    wk_d = nc.dram_tensor("wk", [D, D], F32, kind="ExternalInput").ap()
    wv_d = nc.dram_tensor("wv", [D, D], F32, kind="ExternalInput").ap()
    wo_d = nc.dram_tensor("wo", [D, D], F32, kind="ExternalInput").ap()
    bq_d = nc.dram_tensor("bq", [128, 16], F32, kind="ExternalInput").ap()
    bk_d = nc.dram_tensor("bk", [128, 16], F32, kind="ExternalInput").ap()
    bo_d = nc.dram_tensor("bo", [128, 16], F32, kind="ExternalInput").ap()
    bv_row_d = nc.dram_tensor("bv_row", [1, D], F32, kind="ExternalInput").ap()
    bo_row_d = nc.dram_tensor("bo_row", [1, D], F32, kind="ExternalInput").ap()
    b1_d = nc.dram_tensor("b1", [128, 64], F32, kind="ExternalInput").ap()
    b2_d = nc.dram_tensor("b2", [128, 16], F32, kind="ExternalInput").ap()
    w1s_d = nc.dram_tensor("w1s", [DFF, 256], F32, kind="ExternalInput").ap()
    w2s_d = nc.dram_tensor("w2s", [D, 1024], F32, kind="ExternalInput").ap()
    out_d = nc.dram_tensor("out", [T, D], F32, kind="ExternalOutput").ap()
    if DEBUG:
        o_xn = nc.dram_tensor("o_xn", [128, NCH * T], BF16,
                              kind="ExternalOutput").ap()
        o_st = nc.dram_tensor("o_st", [D, T], BF16,
                              kind="ExternalOutput").ap()
        o_fe = nc.dram_tensor("o_fe", [128, NCH * T], BF16,
                              kind="ExternalOutput").ap()
        o_h = nc.dram_tensor("o_h", [128, 32768], BF16,
                             kind="ExternalOutput").ap()
        o_kg = nc.dram_tensor("o_kg", [4 * D, T], BF16,
                              kind="ExternalOutput").ap()
        o_vg = nc.dram_tensor("o_vg", [4 * T, 2080], BF16,
                              kind="ExternalOutput").ap()
        o_w1t = nc.dram_tensor("o_w1t", [D, DFF], BF16,
                               kind="ExternalOutput").ap()
        o_w2t = nc.dram_tensor("o_w2t", [DFF, D], BF16,
                               kind="ExternalOutput").ap()
        o_sum = nc.dram_tensor("o_sum", [1, 2], F32, kind="ExternalOutput").ap()
        o_q = nc.dram_tensor("o_q", [D, T], BF16,
                             kind="ExternalOutput").ap()

    d_k_in = nc.dram_tensor("d_k_in", [D, T], BF16).ap()
    d_k = nc.dram_tensor("d_k", [4 * D, T], BF16).ap()
    d_v_in = nc.dram_tensor("d_v_in", [T, 2080], BF16).ap()
    d_v = nc.dram_tensor("d_v", [4 * T, 2080], BF16).ap()
    d_w1t_in = nc.dram_tensor("d_w1t_in", [256, DFF], BF16).ap()
    d_w1t = nc.dram_tensor("d_w1t", [D, DFF], BF16).ap()
    d_w2t_in = nc.dram_tensor("d_w2t_in", [1024, D], BF16).ap()
    d_w2t = nc.dram_tensor("d_w2t", [DFF, D], BF16).ap()
    d_q = nc.dram_tensor("d_q", [D, T], BF16).ap()
    d_s = nc.dram_tensor("d_s", [D, T], BF16).ap()
    d_sum_in = nc.dram_tensor("d_sum_in", [1, 2], F32).ap()
    d_sum = nc.dram_tensor("d_sum", [1, 2], F32).ap()

    from contextlib import ExitStack
    ctx = ExitStack()
    with ctx:
        def sbt(name, shape, dt):
            return ctx.enter_context(nc.sbuf_tensor(name, shape, dt))

        def pst(name, shape, dt):
            return ctx.enter_context(nc.psum_tensor(name, shape, dt))

        xnbuf = sbt("xnbuf", [128, NCH * T], BF16)
        q2 = sbt("q2", [128, 2048], BF16)
        hbuf = sbt("hbuf", [128, 60480], BF16)
        wbufA = sbt("wbufA", [128, 6144], BF16)
        wbufB = sbt("wbufB", [128, 6144], BF16)
        fa = sbt("fa", [128, 2048], F32)
        fb = sbt("fb", [128, 2048], F32)


        s_bq = sbt("s_bq", [128, 16], F32)
        s_bk = sbt("s_bk", [128, 16], F32)
        s_bo = sbt("s_bo", [128, 16], F32)
        s_b1 = sbt("s_b1", [128, 64], F32)
        s_b2 = sbt("s_b2", [128, 16], F32)
        ident = sbt("ident", [128, 128], BF16)
        bvrow = sbt("bvrow", [1, 2048], BF16)
        ones_t = sbt("ones_t", [1, 128], BF16)
        ones1 = sbt("ones1", [1, 128], F32)
        ones128 = sbt("ones128", [128, 1], F32)
        ones64f = sbt("ones64f", [1, 64], F32)
        stat = sbt("stat", [128, 16], F32)
        epst = sbt("epst", [128, 1], F32)
        acc = sbt("acc", [128, 2], F32)
        acct = sbt("acct", [128, 1], F32)
        ssmall = sbt("ssmall", [1, 8], F32)
        inv_bc = sbt("inv_bc", [128, 4], F32)
        recv = wbufB[:, 4096:6144].bitcast(F32)
        recA = recv[0:1, 0:512]
        recB = recv[0:1, 512:1024]
        bcbufA = fa[0:64, 0:512]
        bcbufB = fa[0:64, 512:1024]

        H_XN = 0
        H_KT = 0
        H_VST = 2048
        H_TU, H_TV, H_TT = 0, 2048, 4096
        H_WST = 6144
        H_H = 0
        H_KB = 32768
        H_VB = 36864
        H_EA = 41024
        H_FE = 44096
        H_PBF = 44096

        P_a = pst("P_a", [128, 1024], F32)
        P_b = pst("P_b", [128, 1024], F32)
        P_c = pst("P_c", [128, 512], F32)
        P_d = pst("P_d", [128, 512], F32)
        P_f = pst("P_f", [128, 1024], BF16)
        P_h = pst("P_h", [128, 512], F32)

        SEM.clear()
        for s in SEMNAMES:
            SEM[s] = ctx.enter_context(nc.semaphore(s))

        P = Plan()
        C = P.ctr

        P.op("gpsimd", lambda e: e.memset(ident[:, :], 0.0))
        v_id = P.op("gpsimd", lambda e: e.affine_select(
            out=ident[:, :], in_=ident[:, :], compare_op=ALU.not_equal,
            fill=1.0, base=0, pattern=[[-1, 128]], channel_multiplier=1),
            inc="gp")
        P.op("gpsimd", lambda e: e.memset(ones1[:, :], 1.0))
        P.op("gpsimd", lambda e: e.memset(ones_t[:, :], 1.0))
        v_bvrow = P.dma("gpsimd", bvrow[:, :], bv_row_d[:, :])
        P.op("gpsimd", lambda e: e.memset(ones128[:, :], 1.0))
        P.op("gpsimd", lambda e: e.memset(ones64f[:, :], 1.0))
        v_ones = P.op("gpsimd", lambda e: e.memset(epst[:, :], 1e-5), inc="gp")

        toks_bias = [P.dma("sync", s_bq[:, :], bq_d[:, :]),
                     P.dma("sync", s_bk[:, :], bk_d[:, :]),
                     P.dma("sync", s_bo[:, :], bo_d[:, :]),
                     P.dma("sync", s_b1[:, :], b1_d[:, :]),
                     P.dma("sync", s_b2[:, :], b2_d[:, :])]

        xnbf = hbuf[:, H_XN:H_XN + 2048]
        xnbf_free = ("mm", 0)
        pf_evac = {0: 0, 1: 0}
        v_tph = 0
        xin_free = {0: ("vec", 0)}
        stat_free = {}
        for i in range(8):
            xin = fa[:, :]
            sb0 = (i % 2) * 8
            P.wait("sync", xin_free[0][0], xin_free[0][1])
            vl = P.dma("sync", xin, x_d[i * 128:(i + 1) * 128, :])
            P.wait("vector", vl)
            P.wait("vector", "act", stat_free.get(i % 2, 0))
            P.op("vector", lambda e, xin=xin, sb0=sb0: e.reduce_sum(
                stat[:, sb0:sb0 + 1], xin, axis=AX.X, negate=True), inc="vec")
            P.wait("vector", "vec", C["vec"])
            v_nm = P.op("vector", lambda e, sb0=sb0: e.tensor_scalar_mul(
                stat[:, sb0 + 1:sb0 + 2], stat[:, sb0:sb0 + 1],
                1.0 / 2048.0), inc="vec")
            P.wait("scalar", "vec", v_nm)
            P.op("scalar", lambda e, xin=xin, sb0=sb0: e.activation(
                out=fb[:, :], in_=xin, func=AF.Identity,
                bias=stat[:, sb0 + 1:sb0 + 2], scale=1.0), inc="act")
            P.wait("scalar", "act", C["act"])
            v_sqr = P.op("scalar", lambda e, xin=xin, sb0=sb0: e.activation(
                out=xin, in_=fb[:, :], func=AF.Square,
                accum_out=stat[:, sb0 + 2:sb0 + 3]), inc="act")
            xin_free[0] = ("act", v_sqr)
            P.wait("scalar", "act", C["act"])
            if i == 0:
                P.wait("scalar", "gp", v_ones)
            v_std = P.op("scalar", lambda e, sb0=sb0: e.activation(
                out=stat[:, sb0 + 3:sb0 + 4], in_=stat[:, sb0 + 2:sb0 + 3],
                func=AF.Sqrt, bias=epst[:, 0:1], scale=1.0 / 2048.0), inc="act")
            stat_free[i % 2] = v_std
            P.wait("vector", "act", v_std)
            v_rstd = P.op("vector", lambda e, sb0=sb0: e.reciprocal(
                stat[:, sb0 + 4:sb0 + 5], stat[:, sb0 + 3:sb0 + 4]), inc="vec")
            P.wait("vector", "vec", C["vec"])
            v_rstd2 = P.op("vector", lambda e, sb0=sb0: e.tensor_copy(
                stat[:, sb0 + 5:sb0 + 6], stat[:, sb0 + 4:sb0 + 5]), inc="vec")
            P.wait("scalar", "vec", v_rstd2)
            P.wait("scalar", xnbf_free[0], xnbf_free[1])
            v_xn = P.op("scalar", lambda e, sb0=sb0: e.activation(
                out=xnbf[:, :], in_=fb[:, :], func=AF.Identity,
                scale=stat[:, sb0 + 5:sb0 + 6]), inc="act")
            v_xn_last = v_xn
            P.wait("tensor", "act", v_xn)
            if i == 0:
                P.wait("tensor", "gp", v_id)
            for b4 in range(4):
                half = b4 % 2
                P.wait("tensor", "vec", pf_evac[half])
                for k in range(4):
                    kk = b4 * 4 + k
                    fn = lambda e, kk=kk, k=k, half=half: e.transpose(
                        P_f[:, half * 512 + k * 128: half * 512 + k * 128 + 128],
                        xnbf[:, kk * 128:kk * 128 + 128], ident[:, :])
                    if k == 3:
                        v_tph = P.op("tensor", fn, inc="mm")
                    else:
                        P.op("tensor", fn)
                P.wait("vector", "mm", v_tph)
                out3 = xnbuf[:, b4 * 4 * T: (b4 * 4 + 4) * T].rearrange(
                    "p (c t) -> p c t", t=T)[:, :, i * 128:i * 128 + 128]
                in3 = P_f[:, half * 512: half * 512 + 512].rearrange(
                    "p (c t) -> p c t", t=128)
                vev = P.op(
                    "vector", lambda e, o=out3, ii=in3: e.tensor_copy(o, ii),
                    inc="vec")
                pf_evac[0] = pf_evac[1] = vev
            xnbf_free = ("mm", v_tph)
        fa_free = xin_free[0]
        v_xnT_done = max(pf_evac.values())

        def band_ap(w_ap, jd):
            return w_ap[:, jd * 128:(jd + 1) * 128].rearrange(
                "(i p) d -> p i d", p=128)

        def chunked(sb_ap, inner):
            return sb_ap.rearrange("p (i d) -> p i d", d=inner)

        v_vst_ms = 0
        for tt in range(8):
            v_vst_ms = P.op("gpsimd", lambda e, tt=tt: e.memset(
                hbuf[:, H_VST + tt * 2080: H_VST + (tt + 1) * 2080], 1.0),
                inc="gp")
        v_vev = {}
        H_WVB = 18688
        toks_wv = []
        for q in range(4):
            toks_wv.append(P.dma(
                "gpsimd",
                chunked(hbuf[:, H_WVB + q * 8192: H_WVB + (q + 1) * 8192], 512),
                wv_d[:, q * 512:(q + 1) * 512].rearrange(
                    "(i p) d -> p i d", p=128)))
        vslots = [P_c[:, :], P_d[:, :], P_a[:, 0:512], P_a[:, 512:1024]]
        vpsum_free = {q: ("vec", 0) for q in range(4)}
        v_vmm = 0
        for tt in range(8):
            P.wait("tensor", toks_wv)
            if tt == 0:
                P.wait("tensor", "vec", v_xnT_done)
                P.wait("tensor", v_bvrow)
                P.wait("tensor", "gp", v_ones)
            for q in range(4):
                P.wait("tensor", vpsum_free[q][0], vpsum_free[q][1])
            for i in range(16):
                for q in range(4):
                    fn = (lambda e, i=i, q=q, tt=tt: e.matmul(
                        vslots[q],
                        xnbuf[:, i * T + tt * 128: i * T + tt * 128 + 128],
                        hbuf[:, H_WVB + q * 8192 + i * 512:
                             H_WVB + q * 8192 + (i + 1) * 512],
                        start=(i == 0), stop=False))
                    P.op("tensor", fn)
            for q in range(4):
                fn = (lambda e, q=q: e.matmul(
                    vslots[q], ones_t[0:1, :],
                    bvrow[0:1, q * 512:(q + 1) * 512],
                    start=False, stop=True))
                if q == 3:
                    v_vmm = P.op("tensor", fn, inc="mm")
                else:
                    P.op("tensor", fn)
            P.wait("vector", "mm", v_vmm)
            if tt == 0:
                P.wait("vector", "gp", v_vst_ms)
            vv = 0
            for q in range(4):
                base = H_VST + tt * 2080 + q * 520
                o3 = hbuf[:, base: base + 520].rearrange(
                    "p (h e) -> p h e", e=65)[:, :, 0:64]
                i3 = vslots[q].rearrange("p (h e) -> p h e", e=64)
                vv = P.op("vector", lambda e, o3=o3, i3=i3: e.tensor_copy(
                    o3, i3), inc="vec")
                vpsum_free[q] = ("vec", vv)
            v_vev[(tt, 3)] = vv
        toks_vst = []
        for tt in range(8):
            P.wait("sync", "vec", v_vev[(tt, 3)])
            toks_vst.append(P.dma(
                "sync", d_v_in[tt * 128:(tt + 1) * 128, :],
                hbuf[:, H_VST + tt * 2080: H_VST + (tt + 1) * 2080]))
        P.wait("gpsimd", toks_vst)
        v_agv = P.collective("ccv", "AllGather", ALU.bypass, GROUPS_KV, d_v_in, d_v)

        v_vevac_last = max(vpsum_free[q][1] for q in range(4))
        band_free = {0: ("gp", 0), 1: ("gp", 0)}
        psum_free = {0: ("vec", v_vevac_last), 1: ("vec", v_vevac_last)}
        kst_free = {0: None, 1: None}
        toks_kst = []
        v_kmm_last = 0
        for jd in range(16):
            p = jd % 2
            wb = (wbufA if p == 0 else wbufB)[:, 0:2048]
            P.wait("gpsimd", band_free[p][0], band_free[p][1])
            vw = P.dma("gpsimd", chunked(wb, 128), band_ap(wk_d, jd))
            P.wait("tensor", vw)
            P.wait("tensor", psum_free[p][0], psum_free[p][1])
            P_x = P_a if p == 0 else P_b
            for hh in range(2):
                for i in range(16):
                    fn = lambda e, wb=wb, i=i, hh=hh, P_x=P_x: e.matmul(
                        P_x[:, hh * 512:(hh + 1) * 512],
                        wb[:, i * 128:(i + 1) * 128],
                        xnbuf[:, i * T + hh * 512: i * T + (hh + 1) * 512],
                        start=(i == 0), stop=(i == 15))
                    if hh == 1 and i == 15:
                        v_kmm = P.op("tensor", fn, inc="mm")
                    else:
                        P.op("tensor", fn)
            band_free[p] = ("mm", v_kmm)
            v_kmm_last = v_kmm
            P.wait("scalar", "mm", v_kmm)
            P.wait("scalar", kst_free[p])
            if jd == 0:
                P.wait("scalar", toks_bias)
            if jd < 2:
                P.wait("scalar", "mm", v_tph)
            kts = hbuf[:, H_KT + p * 1024: H_KT + (p + 1) * 1024]
            v_kev = P.op("scalar", lambda e, P_x=P_x, jd=jd, kts=kts: e.activation(
                out=kts, in_=P_x[:, :], func=AF.Identity,
                bias=s_bk[:, jd:jd + 1], scale=1.0), inc="act")
            psum_free[p] = ("act", v_kev)
            P.wait("sync", "act", v_kev)
            v_kst = P.dma("sync", d_k_in[jd * 128:(jd + 1) * 128, :], kts)
            toks_kst.append(v_kst)
            kst_free[p] = v_kst
        P.wait("gpsimd", toks_kst)
        v_agk = P.collective("cck", "AllGather", ALU.bypass, GROUPS_KV, d_k_in, d_k)


        qband_free = {0: band_free[0], 1: band_free[1]}
        qpsum_free = {0: psum_free[0], 1: psum_free[1]}
        qst_free = {0: kst_free[0], 1: kst_free[1]}
        toks_qst = []
        v_qev = 0
        for jd in range(16):
            p = jd % 2
            wb = (wbufA if p == 0 else wbufB)[:, 0:2048]
            P.wait("gpsimd", qband_free[p][0], qband_free[p][1])
            vw = P.dma("gpsimd", chunked(wb, 128), band_ap(wq_d, jd))
            P.wait("tensor", vw)
            P.wait("tensor", qpsum_free[p][0], qpsum_free[p][1])
            P_x = P_a if p == 0 else P_b
            for hh in range(2):
                for i in range(16):
                    fn = lambda e, wb=wb, i=i, hh=hh, P_x=P_x: e.matmul(
                        P_x[:, hh * 512:(hh + 1) * 512],
                        wb[:, i * 128:(i + 1) * 128],
                        xnbuf[:, i * T + hh * 512: i * T + (hh + 1) * 512],
                        start=(i == 0), stop=(i == 15))
                    if hh == 1 and i == 15:
                        v_qmm = P.op("tensor", fn, inc="mm")
                    else:
                        P.op("tensor", fn)
            qband_free[p] = ("mm", v_qmm)
            P.wait("scalar", "mm", v_qmm)
            P.wait("scalar", qst_free[p])
            qts = hbuf[:, H_KT + p * 1024: H_KT + (p + 1) * 1024]
            v_qev = P.op("scalar", lambda e, P_x=P_x, jd=jd, qts=qts: e.activation(
                out=qts, in_=P_x[:, :],
                func=AF.Identity, bias=s_bq[:, jd:jd + 1], scale=1.0), inc="act")
            qpsum_free[p] = ("act", v_qev)
            P.wait("sync", "act", v_qev)
            v_qst = P.dma("sync", d_q[jd * 128:(jd + 1) * 128, :], qts)
            toks_qst.append(v_qst)
            qst_free[p] = v_qst

        v_acc0 = P.op("vector", lambda e: e.memset(acc[:, :], 0.0), inc="vec")
        fa_free = xin_free[0]
        fb_free = ("act", v_xn_last)
        for which, src_ap, ntile, ng, tw in ((0, w1s_d, 8, 8, 256),
                                             (1, w2s_d, 8, 2, 1024)):
            rows = ng * 128
            for k in range(ntile):
                buf, free = (fa, fa_free) if k % 2 == 0 else (fb, fb_free)
                P.wait("scalar", free[0], free[1])
                vl = P.dma("scalar", buf[:, :].rearrange("p (g c) -> p g c", c=tw),
                           src_ap[k * rows:(k + 1) * rows, :].rearrange(
                               "(g p) c -> p g c", p=128))
                P.wait("vector", vl)
                P.op("vector", lambda e, buf=buf: e.reduce_sum(
                    acct[:, 0:1], buf[:, :], axis=AX.X,
                    apply_absolute_value=True), inc="vec")
                if k % 2 == 0:
                    fa_free = ("vec", C["vec"])
                else:
                    fb_free = ("vec", C["vec"])
                P.wait("vector", "vec", C["vec"])
                P.op("vector", lambda e, w=which: e.tensor_add(
                    acc[:, w:w + 1], acc[:, w:w + 1], acct[:, 0:1]), inc="vec")
        v_abs_done = C["vec"]
        P.wait("tensor", "vec", v_abs_done)
        P.wait("tensor", "vec", vpsum_free[0][1])
        v_sum_mm = P.op("tensor", lambda e: e.matmul(
            P_c[0:1, 0:2], ones128[:, 0:1], acc[:, :], start=True, stop=True),
            inc="mm")
        P.wait("vector", "mm", v_sum_mm)
        v_sum_cp = P.op("vector", lambda e: e.tensor_copy(
            ssmall[0:1, 0:2], P_c[0:1, 0:2]), inc="vec")
        P.wait("sync", "vec", v_sum_cp)
        v_sum_st = P.dma("sync", d_sum_in[:, :], ssmall[0:1, 0:2])
        P.wait("gpsimd", v_sum_st)
        v_ar = P.collective("ccr", "AllReduce", ALU.add, GROUP_ALL, d_sum_in, d_sum)

        P.wait("sync", "ccr", v_ar)
        v_sumld = P.dma("sync", ssmall[0:1, 2:4], d_sum[:, :])
        P.wait("vector", v_sumld)
        P.op("vector", lambda e: e.tensor_scalar(
            out=ssmall[0:1, 4:6], in0=ssmall[0:1, 2:4], scalar1=1.0 / NMAT,
            scalar2=1e-8, op0=ALU.mult, op1=ALU.max), inc="vec")
        P.wait("vector", "vec", C["vec"])
        v_inv = P.op("vector", lambda e: e.reciprocal(
            ssmall[0:1, 6:8], ssmall[0:1, 4:6]), inc="vec")
        P.wait("vector", "vec", C["vec"])
        P.wait("tensor", "vec", v_inv)
        P.wait("tensor", vpsum_free[1][0], vpsum_free[1][1])
        v_invb = P.op("tensor", lambda e: e.matmul(
            P_d[:, 0:2], ones1[0:1, :], ssmall[0:1, 6:8],
            start=True, stop=True), inc="mm")
        P.wait("vector", "mm", v_invb)
        P.op("vector", lambda e: e.tensor_scalar_mul(
            inv_bc[:, 0:2], P_d[:, 0:2], BIG), inc="vec")
        v_invbc = P.op("vector", lambda e: e.tensor_scalar_mul(
            inv_bc[:, 2:4], P_d[:, 0:2], -BIG), inc="vec")

        v_wst = {}
        hp2 = [0]
        pf_free = {0: ("vec", pf_evac[0]), 1: ("vec", pf_evac[1])}
        tern_free = ("mm", 0)
        v_w1std = v_w2std = 0
        for which, src_ap, ntile, ng, tw in ((0, w1s_d, 8, 8, 256),
                                             (1, w2s_d, 8, 2, 1024)):
            rows = ng * 128
            wide = ng * tw
            ntp = tw // 128
            v_tp = 0
            for k in range(ntile):
                P.wait("scalar", fa_free[0], fa_free[1])
                vl = P.dma("scalar", fa[:, :].rearrange("p (g c) -> p g c", c=tw),
                           src_ap[k * rows:(k + 1) * rows, :].rearrange(
                               "(g p) c -> p g c", p=128))
                P.wait("vector", vl)
                P.wait("vector", "vec", v_invbc)
                if which == 0 and k == 0:
                    P.wait("vector", toks_vst)
                    P.wait("vector", toks_kst)
                    P.wait("vector", "act", v_xn_last)
                P.wait("vector", tern_free[0], tern_free[1])
                m = which
                P.op("vector", lambda e, m=m: e.tensor_scalar(
                    out=fb[:, :], in0=fa[:, :], scalar1=inv_bc[:, m:m + 1],
                    scalar2=-HALFBIG, op0=ALU.mult, op1=ALU.add), inc="vec")
                P.wait("vector", "vec", C["vec"])
                P.op("vector", lambda e: e.tensor_scalar(
                    out=hbuf[:, H_TU:H_TU + 2048], in0=fb[:, :], scalar1=0.0,
                    scalar2=1.0, op0=ALU.max, op1=ALU.min), inc="vec")
                P.wait("vector", "vec", C["vec"])
                v_u2 = P.op("vector", lambda e, m=m: e.tensor_scalar(
                    out=fb[:, :], in0=fa[:, :],
                    scalar1=inv_bc[:, 2 + m:3 + m],
                    scalar2=-HALFBIG, op0=ALU.mult, op1=ALU.add), inc="vec")
                fa_free = ("vec", v_u2)
                P.wait("vector", "vec", C["vec"])
                P.op("vector", lambda e: e.tensor_scalar(
                    out=hbuf[:, H_TV:H_TV + 2048], in0=fb[:, :], scalar1=0.0,
                    scalar2=1.0, op0=ALU.max, op1=ALU.min), inc="vec")
                P.wait("vector", "vec", C["vec"])
                v_tt = P.op("vector", lambda e: e.tensor_tensor(
                    out=hbuf[:, H_TT:H_TT + 2048], in0=hbuf[:, H_TU:H_TU + 2048],
                    in1=hbuf[:, H_TV:H_TV + 2048], op=ALU.subtract), inc="vec")
                P.wait("tensor", "vec", v_tt)
                cw = 8192 if which == 0 else 2048
                for g in range(ng):
                    kk = k * ng + g
                    nsub = (ntp + 3) // 4
                    for sub in range(nsub):
                        j0 = sub * 4
                        nj = min(4, ntp - j0)
                        half = hp2[0] % 2
                        hp2[0] += 1
                        P.wait("tensor", pf_free[half][0], pf_free[half][1])
                        for jj in range(nj):
                            j = j0 + jj
                            fn = (lambda e, g=g, j=j, jj=jj, half=half, tw=tw:
                                  e.transpose(
                                      P_f[:, half * 512 + jj * 128:
                                          half * 512 + (jj + 1) * 128],
                                      hbuf[:, H_TT + g * tw + j * 128:
                                           H_TT + g * tw + (j + 1) * 128],
                                      ident[:, :]))
                            if jj == nj - 1:
                                v_tp = P.op("tensor", fn, inc="mm")
                            else:
                                P.op("tensor", fn)
                        P.wait("vector", "mm", v_tp)
                        if which == 1 and kk == 0 and sub == 0:
                            P.wait("vector", toks_w1st)
                        o3 = hbuf[:, H_WST:H_WST + ntp * cw].rearrange(
                            "p (j c) -> p j c", c=cw)[
                                :, j0:j0 + nj, kk * 128:kk * 128 + 128]
                        i3 = P_f[:, half * 512: half * 512 + nj * 128].rearrange(
                            "p (j c) -> p j c", c=128)
                        vev = ("vec", P.op(
                            "vector", lambda e, o3=o3, i3=i3: e.tensor_copy(
                                o3, i3), inc="vec"))
                        pf_free[0] = pf_free[1] = vev
                tern_free = ("mm", v_tp)
            nrow = ntp
            cw = 8192 if which == 0 else 2048
            dst = d_w1t_in if which == 0 else d_w2t_in
            toks_std = []
            for j in range(nrow):
                P.wait("scalar", "vec", pf_free[0][1])
                P.wait("scalar", "vec", pf_free[1][1])
                toks_std.append(P.dma(
                    "scalar", dst[j * 128:(j + 1) * 128, :],
                    hbuf[:, H_WST + j * cw:H_WST + (j + 1) * cw]))
            if which == 0:
                toks_w1st = toks_std
                P.wait("gpsimd", toks_w1st)
                v_agw1 = P.collective("cw1", "AllGather", ALU.bypass, GROUP_ALL,
                                      d_w1t_in, d_w1t)
            else:
                toks_w2st = toks_std
                P.wait("gpsimd", toks_w2st)
                v_agw2 = P.collective("cw2", "AllGather", ALU.bypass, GROUP_ALL,
                                      d_w2t_in, d_w2t)

        kvld = {}
        feats_evac = {}
        feats_mm = {}
        exp_ms2 = {}
        bc_copy = {}
        v_feats_evac_last = 0
        v_fb_last0 = C["vec"]

        F1_START = 2 * 64
        f1 = {"emitted": 0, "slot": 0, "v_hmm": {}, "v_hev": {},
              "w1_free": {0: qband_free[0], 1: qband_free[1]}, "toks": {}}

        def f1_load(sb_):
            par = sb_ % 2
            wb = (wbufA if par == 0 else wbufB)[:, 0:4096]
            P.wait("gpsimd", f1["w1_free"][par][0], f1["w1_free"][par][1])
            if sb_ < 2:
                P.wait("gpsimd", "cw1", 1)
            f1["toks"][sb_] = P.dma(
                "gpsimd", wb.rearrange("p (i c) -> p i c", c=256),
                d_w1t[:, sb_ * 256:(sb_ + 1) * 256].rearrange(
                    "(i p) c -> p i c", p=128))

        def f1_emit_unit():
            u = f1["emitted"]
            if u >= 1024:
                return
            f1["emitted"] += 1
            b, i = u // 16, u % 16
            sb_ = b // 2
            l = b % 2
            par = sb_ % 2
            wb = (wbufA if par == 0 else wbufB)[:, 0:4096]
            if i == 0:
                P.wait("tensor", f1["toks"][sb_])
                if b > 0:
                    P.wait("tensor", "vec", f1["v_hev"][b - 1])
            fn = (lambda e, wb=wb, i=i, l=l: e.matmul(
                P_h[:, :], wb[:, i * 256 + l * 128: i * 256 + (l + 1) * 128],
                xnbuf[:, i * T: i * T + 512],
                start=(i == 0), stop=(i == 15)))
            if i < 15:
                P.op("tensor", fn)
            else:
                f1["v_hmm"][b] = P.op("tensor", fn, inc="mm")
                f1["w1_free"][par] = ("mm", f1["v_hmm"][b])
                if l == 1 and sb_ + 2 < 32:
                    f1_load(sb_ + 2)
                P.wait("vector", "mm", f1["v_hmm"][b])
                if b == 0:
                    P.wait("vector", toks_w1st)
                    P.wait("vector", toks_w2st)
                    P.wait("vector", toks_vst)
                    P.wait("vector", toks_kst)
                    P.wait("vector", toks_qst)
                    P.wait("vector", "mm", tern_free[1])
                f1["v_hev"][b] = P.op(
                    "vector", lambda e, b=b: e.tensor_scalar_add(
                        hbuf[:, H_H + b * 512:H_H + (b + 1) * 512],
                        P_h[:, :], s_b1[:, b:b + 1]), inc="vec")

        def f1_slot():
            s = f1["slot"]
            f1["slot"] += 1
            if s < F1_START:
                return
            target = min(1024, ((s - F1_START + 1) * 1024 + 767) // 768)
            while f1["emitted"] < target:
                f1_emit_unit()

        f1_load(0)
        f1_load(1)

        for j in range(PAIRS):
            kb = hbuf[:, H_KB: H_KB + 4096]
            vb = hbuf[:, H_VB: H_VB + 4160]
            qp = j % 2
            qsl = q2[:, qp * 1024:(qp + 1) * 1024]
            if j >= 1:
                P.wait("sync", "mm", feats_mm[(j - 1, 1)])
            if j < 2:
                P.wait("sync", "cck", v_agk)
                P.wait("sync", "ccv", v_agv)
                P.wait("sync", toks_qst)
            toks_kv = [P.dma("sync", qsl, d_q[j * 128:(j + 1) * 128, :])]
            for r in range(4):
                toks_kv.append(P.dma(
                    "sync", kb[:, r * 1024:(r + 1) * 1024],
                    d_k[r * 2048 + j * 128: r * 2048 + (j + 1) * 128, :]))
            for r in range(4):
                toks_kv.append(P.dma(
                    "sync", chunked(vb[:, r * 1040:(r + 1) * 1040], 130),
                    d_v[r * 1024:(r + 1) * 1024,
                        130 * j:130 * (j + 1)].rearrange(
                            "(c p) m -> p c m", p=128)))
            kvld[j] = toks_kv

            for qt in range(2):
                exp_ms = {}
                mmg = {}
                mm2g = {}
                q0 = qt * 512
                expb = [hbuf[:, H_EA + par * 1024: H_EA + (par + 1) * 1024]
                        for par in range(3)]

                def mm2_pair(kc):
                    par = kc % 3
                    for (head, Pfe) in ((0, P_c), (1, P_d)):
                        cbase = kc * 130 + head * 65
                        fn = (lambda e, vb=vb, cbase=cbase, Pfe=Pfe,
                              par=par, head=head, kc=kc: e.matmul(
                                  Pfe[0:65, :],
                                  vb[:, cbase:cbase + 65],
                                  expb[par][:, head * 512:(head + 1) * 512],
                                  start=(kc == 0), stop=(kc == 31)))
                        yield fn

                for kc in range(32):
                    par = kc % 2
                    Psc = P_a if par == 0 else P_b
                    if kc == 0:
                        if qt == 0:
                            P.wait("tensor", kvld[j])
                            if j == 0:
                                P.wait("tensor", "act", v_qev)
                        if (j, qt) != (0, 0):
                            prev = (j, 0) if qt == 1 else (j - 1, 1)
                            P.wait("tensor", "act", exp_ms2.get(prev, 0))
                    else:
                        P.wait("tensor", "act", exp_ms.get(kc - 2, 0))
                    v_g = 0
                    for head, rl, rh in ((0, 0, 64), (1, 64, 128)):
                        fn = (lambda e, kb=kb, kc=kc, Psc=Psc, rl=rl, rh=rh,
                              qsl=qsl, q0=q0, head=head: e.matmul(
                                  Psc[:, head * 512:(head + 1) * 512],
                                  kb[rl:rh, kc * 128:(kc + 1) * 128],
                                  qsl[rl:rh, q0: q0 + 512],
                                  start=True, stop=True,
                                  tile_position=(rl, 0)))
                        if head == 1:
                            v_g = P.op("tensor", fn, inc="mm")
                        else:
                            P.op("tensor", fn)
                    mmg[kc] = v_g
                    if kc >= 2:
                        if kc == 2:
                            prev = (j, 0) if qt == 1 else (j - 1, 1)
                            if (j, qt) != (0, 0):
                                P.wait("tensor", "vec", feats_evac.get(prev, 0))
                        fns = list(mm2_pair(kc - 2))
                        for idx, fn in enumerate(fns):
                            if idx == len(fns) - 1:
                                mm2g[kc - 2] = P.op("tensor", fn, inc="mm")
                            else:
                                P.op("tensor", fn)
                    f1_slot()
                    P.wait("scalar", "mm", mmg[kc])
                    if kc >= 3:
                        P.wait("scalar", "mm", mm2g.get(kc - 3, 0))
                    exp_ms[kc] = P.op("scalar", lambda e, par3=kc % 3, Psc=Psc:
                                      e.activation(out=expb[par3][:, :],
                                                   in_=Psc[:, :], func=AF.Exp,
                                                   scale=SCALE), inc="act")
                exp_ms2[(j, qt)] = exp_ms[31]
                P.wait("tensor", "act", exp_ms[31])
                v_feats = 0
                fns = list(mm2_pair(30)) + list(mm2_pair(31))
                for idx, fn in enumerate(fns):
                    if idx == len(fns) - 1:
                        v_feats = P.op("tensor", fn, inc="mm")
                    else:
                        P.op("tensor", fn)
                P.wait("vector", "mm", v_feats)
                P.op("vector", lambda e: e.reciprocal(
                    recA, P_c[64:65, :]), inc="vec")
                v_rec = P.op("vector", lambda e: e.reciprocal(
                    recB, P_d[64:65, :]), inc="vec")
                P.wait("tensor", "vec", v_rec)
                P.op("tensor", lambda e: e.matmul(
                    P_c[64:128, :], ones64f[0:1, :], recA,
                    start=True, stop=True, tile_position=(0, 64)))
                v_bc = P.op("tensor", lambda e: e.matmul(
                    P_d[64:128, :], ones64f[0:1, :], recB,
                    start=True, stop=True, tile_position=(0, 64)), inc="mm")
                v_bc_last = v_bc
                P.wait("vector", "mm", v_bc)
                if j == 0 and qt == 0:
                    P.wait("vector", toks_w2st)
                fcol = j * T + q0
                P.op("vector", lambda e: e.tensor_copy(
                    bcbufA, P_c[64:128, :]), inc="vec")
                v_bcc = P.op("vector", lambda e: e.tensor_copy(
                    bcbufB, P_d[64:128, :]), inc="vec")
                P.op("vector", lambda e, fcol=fcol: e.tensor_tensor(
                    out=hbuf[0:64, H_FE + fcol:H_FE + fcol + 512],
                    in0=P_c[0:64, :], in1=bcbufA, op=ALU.mult), inc="vec")
                vfe = P.op("vector", lambda e, fcol=fcol: e.tensor_tensor(
                    out=hbuf[64:128, H_FE + fcol:H_FE + fcol + 512],
                    in0=P_d[0:64, :], in1=bcbufB, op=ALU.mult), inc="vec")
                bc_copy[(j, qt)] = v_bcc
                feats_evac[(j, qt)] = vfe
                feats_mm[(j, qt)] = v_feats
                v_feats_evac_last = vfe
        while f1["emitted"] < 1024:
            f1_emit_unit()
        P.wait("scalar", "vec", f1["v_hev"][63])
        v_h_done_th0 = 0
        for cgel in range(4):
            v_h_done_th0 = P.op("scalar", lambda e, cgel=cgel: e.activation(
                out=hbuf[:, cgel * 8192:(cgel + 1) * 8192],
                in_=hbuf[:, cgel * 8192:(cgel + 1) * 8192],
                func=AF.Gelu, scale=1.0), inc="act")

        v_dbg_fe = None
        if DEBUG:
            P.wait("sync", "vec", v_feats_evac_last)
            v_dbg_fe = [P.dma("sync", o_fe[:, :], hbuf[:, H_FE:H_FE + NCH * T])]

        wo_band_free = {0: f1["w1_free"][0], 1: f1["w1_free"][1]}
        wo_psum_free = {0: ("vec", v_feats_evac_last), 1: ("vec", v_feats_evac_last)}
        sst_free = {0: None, 1: None}
        toks_sst = []
        v_woev = {}
        v_womm_last = 0
        for jd in range(16):
            p = jd % 2
            wb = (wbufA if p == 0 else wbufB)[:, 0:2048]
            P.wait("gpsimd", wo_band_free[p][0], wo_band_free[p][1])
            vw = P.dma("gpsimd", chunked(wb, 128), band_ap(wo_d, jd))
            P.wait("tensor", vw)
            P.wait("tensor", wo_psum_free[p][0], wo_psum_free[p][1])
            if jd == 0:
                P.wait("tensor", "vec", v_feats_evac_last)
                if DEBUG:
                    P.wait("tensor", v_dbg_fe)
            P_x = P_a if p == 0 else P_b
            for hh in range(2):
                for i in range(16):
                    fe = H_FE + i * T + hh * 512
                    fn = lambda e, wb=wb, i=i, hh=hh, P_x=P_x, fe=fe: e.matmul(
                        P_x[:, hh * 512:(hh + 1) * 512],
                        wb[:, i * 128:(i + 1) * 128],
                        hbuf[:, fe:fe + 512],
                        start=(i == 0), stop=(i == 15))
                    if hh == 1 and i == 15:
                        v_womm = P.op("tensor", fn, inc="mm")
                    else:
                        P.op("tensor", fn)
            wo_band_free[p] = ("mm", v_womm)
            v_womm_last = v_womm
            P.wait("scalar", "mm", v_womm)
            P.wait("scalar", sst_free[p])
            if jd < 2:
                P.wait("scalar", "mm", feats_mm[(PAIRS - 1, 1)])
            sts = q2[:, p * 1024:(p + 1) * 1024]
            vv = P.op("scalar", lambda e, P_x=P_x, jd=jd, sts=sts: e.activation(
                out=sts, in_=P_x[:, :],
                func=AF.Identity, bias=s_bo[:, jd:jd + 1], scale=1.0), inc="act")
            v_woev[jd] = vv
            wo_psum_free[p] = ("act", vv)
            P.wait("sync", "act", vv)
            v_sst = P.dma("sync", d_s[jd * 128:(jd + 1) * 128, :], sts)
            toks_sst.append(v_sst)
            sst_free[p] = v_sst
        v_woev_last = v_woev[15]

        w1_free = {0: ("mm", 0), 1: ("mm", 0)}
        h_psum_free = {0: ("act", 0), 1: ("act", 0)}
        f2_psum_free = ("act", 0)
        fa_slot_free = {sl: ("vec", v_feats_evac_last) for sl in range(4)}
        s_tile_free = {sl: None for sl in range(4)}
        fb_slot_free = {sl: ("vec", v_feats_evac_last) for sl in range(4)}
        pbf_free = {0: ("mm", 0), 1: ("mm", 0)}
        pf2_free = {0: pf_free[0], 1: pf_free[1]}
        fout_free = {0: None, 1: None}
        w2_free = {0: ("mm", v_bc_last), 1: ("mm", v_bc_last)}
        v_out_dma = 0
        fout = hbuf[:, H_KB:H_KB + 2048].bitcast(F32)
        w1_free = dict(f1["w1_free"])
        v_f2_last_th0 = 0
        for th in range(2):
            t0 = th * 512
            if th == 0:
                v_h_done = v_h_done_th0
            else:
                v_hmm = {}
                v_gelu = {}
                for sb_ in range(32):
                    p = sb_ % 2
                    wb = (wbufA if p == 0 else wbufB)[:, 0:4096]
                    qeng = "sync" if sb_ % 2 == 0 else "gpsimd"
                    P.wait(qeng, w1_free[p][0], w1_free[p][1])
                    if sb_ < 2:
                        P.wait(qeng, "mm", v_womm_last)
                    toks_w1l = [P.dma(
                        qeng,
                        wb.rearrange("p (i c) -> p i c", c=256),
                        d_w1t[:, sb_ * 256:(sb_ + 1) * 256].rearrange(
                            "(i p) c -> p i c", p=128))]
                    for l in range(2):
                        bb = sb_ * 2 + l
                        P.wait("tensor", toks_w1l)
                        if bb == 0:
                            P.wait("tensor", "vec", f1["v_hev"][63])
                            P.wait("tensor", "mm", v_f2_last_th0)
                        else:
                            P.wait("tensor", "act", v_gelu[bb - 1])
                        for i in range(16):
                            fn = (lambda e, wb=wb, i=i, l=l, t0=t0: e.matmul(
                                P_h[:, :],
                                wb[:, i * 256 + l * 128:
                                   i * 256 + (l + 1) * 128],
                                xnbuf[:, i * T + t0: i * T + t0 + 512],
                                start=(i == 0), stop=(i == 15)))
                            if i == 15:
                                v_hmm[bb] = P.op("tensor", fn, inc="mm")
                            else:
                                P.op("tensor", fn)
                        P.wait("scalar", "mm", v_hmm[bb])
                        if DEBUG and bb == 0:
                            P.wait("scalar", v_dbg_h)
                        if bb == 0:
                            P.wait("scalar", "mm", v_f2_last_th0)
                        v_gelu[bb] = P.op(
                            "scalar", lambda e, bb=bb: e.activation(
                                out=hbuf[:, H_H + bb * 512:
                                         H_H + (bb + 1) * 512],
                                in_=P_h[:, :], func=AF.Gelu,
                                bias=s_b1[:, bb:bb + 1], scale=1.0), inc="act")
                    w1_free[p] = ("mm", v_hmm[sb_ * 2 + 1])
                v_h_done = v_gelu[63]
            if DEBUG and th == 0:
                P.wait("sync", "act", v_h_done)
                v_dbg_h = P.dma("sync", o_h[:, :], hbuf[:, H_H:H_H + 32768])
            f2slots = [P_a[:, 0:512], P_a[:, 512:1024],
                       P_b[:, 0:512], P_b[:, 512:1024]]
            for g in range(4):
                v_f2 = {}
                for fg in range(16):
                    p = fg % 2
                    w2b = (wbufA if p == 0 else wbufB)[:, 4096:6144]
                    P.wait("sync" if p == 0 else "gpsimd",
                           w2_free[p][0], w2_free[p][1])
                    if th == 0 and g == 0 and fg < 2:
                        P.wait("sync" if p == 0 else "gpsimd", "cw2", 1)
                    vld = P.dma(
                        "sync" if p == 0 else "gpsimd",
                        w2b.rearrange("p (f c) -> p f c", c=512),
                        d_w2t[fg * 512:(fg + 1) * 512,
                              g * 512:(g + 1) * 512].rearrange(
                                  "(f p) c -> p f c", p=128))
                    P.wait("tensor", vld)
                    if fg == 0:
                        P.wait("tensor", "act", v_h_done)
                        P.wait("tensor", f2_psum_free[0], f2_psum_free[1])
                        if th == 0 and g == 0:
                            P.wait("tensor", "act", v_woev_last)
                    for fl in range(4):
                        fbk = fg * 4 + fl
                        for sl in range(4):
                            fn = (lambda e, w2b=w2b, sl=sl, fl=fl, fbk=fbk:
                                  e.matmul(
                                      f2slots[sl],
                                      w2b[:, fl * 512 + sl * 128:
                                          fl * 512 + (sl + 1) * 128],
                                      hbuf[:, H_H + fbk * 512:
                                           H_H + (fbk + 1) * 512],
                                      start=(fbk == 0), stop=(fbk == 63)))
                            if sl == 3 and fl == 3:
                                v_f2[fg] = P.op("tensor", fn, inc="mm")
                            else:
                                P.op("tensor", fn)
                    w2_free[p] = ("mm", v_f2[fg])
                    if th == 0:
                        v_f2_last_th0 = v_f2[fg]
                v_last_evac = 0
                for sl in range(4):
                    jd = g * 4 + sl
                    ppar = jd % 2
                    P.wait("scalar", "mm", v_f2[15])
                    P.wait("scalar", fa_slot_free[sl][0], fa_slot_free[sl][1])
                    v_silu = P.op("scalar", lambda e, sl=sl, jd=jd: e.activation(
                        out=fa[:, sl * 512:(sl + 1) * 512], in_=f2slots[sl],
                        func=AF.Silu, bias=s_b2[:, jd:jd + 1], scale=1.0),
                        inc="act")
                    v_last_evac = v_silu
                    squ = q2[:, sl * 512:(sl + 1) * 512]
                    P.wait("sync", s_tile_free[sl])
                    if th == 0 and g == 0 and sl < 2:
                        P.wait("sync", toks_sst)
                    v_sld = P.dma("sync", squ,
                                  d_s[jd * 128:(jd + 1) * 128, t0:t0 + 512])
                    P.wait("vector", "act", v_silu)
                    P.wait("vector", v_sld)
                    P.wait("vector", pbf_free[ppar][0], pbf_free[ppar][1])
                    v_p = P.op("vector", lambda e, sl=sl, ppar=ppar, squ=squ:
                               e.tensor_tensor(
                                   out=hbuf[:, H_PBF + ppar * 512:
                                            H_PBF + (ppar + 1) * 512],
                                   in0=fa[:, sl * 512:(sl + 1) * 512],
                                   in1=squ,
                                   op=ALU.mult), inc="vec")
                    fa_slot_free[sl] = ("vec", v_p)
                    s_tile_free[sl] = ("vec", v_p)
                    P.wait("tensor", "vec", v_p)
                    P.wait("tensor", pf2_free[ppar][0], pf2_free[ppar][1])
                    v_tp5 = 0
                    for s4 in range(4):
                        fn = (lambda e, s4=s4, ppar=ppar: e.transpose(
                            P_f[:, ppar * 512 + s4 * 128:
                                ppar * 512 + (s4 + 1) * 128],
                            hbuf[:, H_PBF + ppar * 512 + s4 * 128:
                                 H_PBF + ppar * 512 + (s4 + 1) * 128],
                            ident[:, :]))
                        if s4 == 3:
                            v_tp5 = P.op("tensor", fn, inc="mm")
                        else:
                            P.op("tensor", fn)
                    pbf_free[ppar] = ("mm", v_tp5)
                    P.wait("sync", fb_slot_free[sl][0], fb_slot_free[sl][1])
                    vxl = P.dma("sync",
                                chunked(fb[:, sl * 512:(sl + 1) * 512], 128),
                                x_d[th * 512:(th + 1) * 512,
                                    jd * 128:(jd + 1) * 128].rearrange(
                                        "(s p) d -> p s d", p=128))
                    P.wait("vector", "mm", v_tp5)
                    P.wait("vector", vxl)
                    P.wait("vector", fout_free[ppar])
                    v_ot = P.op("vector", lambda e, sl=sl, ppar=ppar:
                                e.tensor_tensor(
                                    out=fout[:, ppar * 512:(ppar + 1) * 512],
                                    in0=P_f[:, ppar * 512:ppar * 512 + 512],
                                    in1=fb[:, sl * 512:(sl + 1) * 512],
                                    op=ALU.add), inc="vec")
                    pf2_free[0] = pf2_free[1] = ("vec", v_ot)
                    fb_slot_free[sl] = ("vec", v_ot)
                    P.wait("sync", "vec", v_ot)
                    v_out_dma = P.dma(
                        "sync",
                        out_d[th * 512:(th + 1) * 512,
                              jd * 128:(jd + 1) * 128].rearrange(
                                  "(s p) d -> p s d", p=128),
                        chunked(fout[:, ppar * 512:(ppar + 1) * 512], 128))
                    fout_free[ppar] = v_out_dma
                f2_psum_free = ("act", v_last_evac)

        if DEBUG:
            P.wait("sync", "vec", C["vec"])
            P.wait("sync", "act", C["act"])
            P.dma("sync", o_xn[:, :], xnbuf[:, :])
            for r in range(16):
                P.dma("sync", o_st[r * 128:(r + 1) * 128, :],
                      d_s[r * 128:(r + 1) * 128, :])
                P.dma("sync", o_q[r * 128:(r + 1) * 128, :],
                      d_q[r * 128:(r + 1) * 128, :])
            P.dma("sync", o_sum[:, :], ssmall[0:1, 2:4])
            P.dma("sync", o_kg[:, :], d_k[:, :])
            P.dma("sync", o_vg[:, :], d_v[:, :])
            for r in range(16):
                P.dma("sync", o_w1t[r * 128:(r + 1) * 128, :],
                      d_w1t[r * 128:(r + 1) * 128, :])
            for r in range(64):
                P.dma("sync", o_w2t[r * 128:(r + 1) * 128, :],
                      d_w2t[r * 128:(r + 1) * 128, :])

        for s in SQ:
            P.wait("sync", s, C[s])

        with nc.Block() as block:
            @block.sync
            def _(e):
                for fn in P.items["sync"]:
                    fn(e)

            @block.gpsimd
            def _(e):
                for fn in P.items["gpsimd"]:
                    fn(e)

            @block.tensor
            def _(e):
                for fn in P.items["tensor"]:
                    fn(e)

            @block.vector
            def _(e):
                for fn in P.items["vector"]:
                    fn(e)

            @block.scalar
            def _(e):
                for fn in P.items["scalar"]:
                    fn(e)

    return nc


_NC_CACHE = None


def kernel(L0_attn_x, L0_attn_Wq, L0_attn_bq, L0_attn_Wk, L0_attn_bk,
           L0_attn_Wv, L0_attn_bv, L0_attn_Wo, L0_attn_bo,
           L0_ffn_w1, L0_ffn_b1, L0_ffn_w2, L0_ffn_b2):
    global _NC_CACHE
    x = np.ascontiguousarray(np.asarray(L0_attn_x, dtype=np.float32))
    wq = np.ascontiguousarray(np.asarray(L0_attn_Wq, dtype=np.float32))
    wk = np.ascontiguousarray(np.asarray(L0_attn_Wk, dtype=np.float32))
    wv = np.ascontiguousarray(np.asarray(L0_attn_Wv, dtype=np.float32))
    wo = np.ascontiguousarray(np.asarray(L0_attn_Wo, dtype=np.float32))
    w1 = np.ascontiguousarray(np.asarray(L0_ffn_w1, dtype=np.float32))
    w2 = np.ascontiguousarray(np.asarray(L0_ffn_w2, dtype=np.float32))

    def fold(b, n):
        return np.ascontiguousarray(
            np.asarray(b, dtype=np.float32).reshape(n, 128).T)

    bq2 = fold(L0_attn_bq, 16)
    bk2 = fold(L0_attn_bk, 16)
    bo2 = fold(L0_attn_bo, 16)
    b12 = fold(L0_ffn_b1, 64)
    b22 = fold(L0_ffn_b2, 16)
    bvr = np.ascontiguousarray(
        np.asarray(L0_attn_bv, dtype=np.float32).reshape(1, D))

    if _NC_CACHE is None:
        _NC_CACHE = build_program()
    nc = _NC_CACHE

    in_maps = []
    for c in range(8):
        b, sl = c // 4, c % 4
        in_maps.append({
            "x": np.ascontiguousarray(x[b, sl * T:(sl + 1) * T, :]),
            "wq": wq, "wk": wk, "wv": wv, "wo": wo,
            "bq": bq2, "bk": bk2, "bo": bo2, "bv_row": bvr,
            "bo_row": np.ascontiguousarray(
                np.asarray(L0_attn_bo, dtype=np.float32).reshape(1, D)),
            "b1": b12, "b2": b22,
            "w1s": np.ascontiguousarray(w1[:, c * 256:(c + 1) * 256]),
            "w2s": np.ascontiguousarray(w2[:, c * 1024:(c + 1) * 1024]),
        })
    res = run_bass_kernel_spmd(nc, in_maps, core_ids=list(range(8)))
    out = np.empty((2, 4096, D), dtype=np.float32)
    for c in range(8):
        b, sl = c // 4, c % 4
        out[b, sl * T:(sl + 1) * T, :] = res.results[c]["out"]
    return out


# version 14 (final)
# speedup vs baseline: 1.1240x; 1.0163x over previous
import numpy as np
import ml_dtypes
import concourse.bass as bass
import concourse.mybir as mybir
from concourse.bass_utils import run_bass_kernel_spmd

F32 = mybir.dt.float32
BF16 = mybir.dt.bfloat16
AF = mybir.ActivationFunctionType
ALU = mybir.AluOpType
AX = mybir.AxisListType
bf = ml_dtypes.bfloat16

D = 2048
DFF = 8192
T = 1024
NCH = 16
PAIRS = 16
SCALE = float(1.0 / np.sqrt(32.0))
BIG = float(2.0 ** 63)
HALFBIG = float(0.5 * 2.0 ** 63)
NMAT = float(DFF * D)
GROUPS_KV = [[0, 1, 2, 3], [4, 5, 6, 7]]
GROUP_ALL = [[0, 1, 2, 3, 4, 5, 6, 7]]

DEBUG = False

SEM = {}
NSQ = 8
NGQ = 4
SQ = tuple(f"sq{i}" for i in range(NSQ))
GQ = tuple(f"gq{i}" for i in range(NGQ))
SC = tuple(f"sc{i}" for i in range(4))
SEMNAMES = SQ + GQ + SC + ("mm", "act", "vec", "gp", "ccr",
                           "cck0", "cck1", "cck2", "cck3", "ccv",
                           "cw1", "cw2")


class Plan:

    ENGS = ("sync", "gpsimd", "tensor", "vector", "scalar")

    def __init__(self):
        self.items = {e: [] for e in self.ENGS}
        self.ctr = {s: 0 for s in SEMNAMES}
        self.rr = {"sync": 0, "gpsimd": 0, "scalar": 0}

    def add(self, eng, fn):
        self.items[eng].append(fn)

    def wait(self, eng, sem, val=None):
        if val is None:
            tok = sem
            if tok is None:
                return
            if isinstance(tok, list):
                best = {}
                for s, v in tok:
                    if s is not None and v > best.get(s, 0):
                        best[s] = v
                for s, v in best.items():
                    self.add(eng, lambda e, s=s, v=v: e.wait_ge(SEM[s], v))
                return
            sem, val = tok
        if sem is None or val is None or val <= 0:
            return
        self.add(eng, lambda e, s=sem, v=val: e.wait_ge(SEM[s], v))

    def dma(self, eng, out_ap, in_ap, **kw):
        pool = {"sync": SQ, "gpsimd": GQ, "scalar": SC}[eng]
        s = pool[self.rr[eng] % len(pool)]
        self.rr[eng] += 1
        self.ctr[s] += 16
        v = self.ctr[s]
        self.add(eng, lambda e, o=out_ap, i=in_ap, kw=dict(kw), s=s: e.dma_start(
            out=o, in_=i, **kw).then_inc(SEM[s], 16))
        return (s, v)

    def op(self, eng, fn, inc=None):
        if inc is None:
            self.add(eng, fn)
            return None
        self.ctr[inc] += 1
        v = self.ctr[inc]
        self.add(eng, lambda e, f=fn, s=inc: f(e).then_inc(SEM[s], 1))
        return v

    def collective(self, sem, kind, op_, groups, in_ap, out_ap):
        self.ctr[sem] += 1
        v = self.ctr[sem]
        self.add("gpsimd", lambda e, k=kind, o=op_, g=groups, i=in_ap, oo=out_ap,
                 s=sem: e.collective_compute(
                     k, o, replica_groups=g, ins=[i.opt()],
                     outs=[oo.opt()]).then_inc(SEM[s], 1))
        return v


def build_program():
    nc = bass.Bass("TRN2", target_bir_lowering=False)

    x_d = nc.dram_tensor("x", [T, D], F32, kind="ExternalInput").ap()
    wq_d = nc.dram_tensor("wq", [D, D], F32, kind="ExternalInput").ap()
    wk_d = nc.dram_tensor("wk", [D, D], F32, kind="ExternalInput").ap()
    wv_d = nc.dram_tensor("wv", [D, D], F32, kind="ExternalInput").ap()
    wo_d = nc.dram_tensor("wo", [D, D], F32, kind="ExternalInput").ap()
    bq_d = nc.dram_tensor("bq", [128, 16], F32, kind="ExternalInput").ap()
    bk_d = nc.dram_tensor("bk", [128, 16], F32, kind="ExternalInput").ap()
    bo_d = nc.dram_tensor("bo", [128, 16], F32, kind="ExternalInput").ap()
    bv_row_d = nc.dram_tensor("bv_row", [1, D], F32, kind="ExternalInput").ap()
    bo_row_d = nc.dram_tensor("bo_row", [1, D], F32, kind="ExternalInput").ap()
    b1_d = nc.dram_tensor("b1", [128, 64], F32, kind="ExternalInput").ap()
    b2_d = nc.dram_tensor("b2", [128, 16], F32, kind="ExternalInput").ap()
    w1s_d = nc.dram_tensor("w1s", [DFF, 256], F32, kind="ExternalInput").ap()
    w2s_d = nc.dram_tensor("w2s", [D, 1024], F32, kind="ExternalInput").ap()
    out_d = nc.dram_tensor("out", [T, D], F32, kind="ExternalOutput").ap()
    if DEBUG:
        o_xn = nc.dram_tensor("o_xn", [128, NCH * T], BF16,
                              kind="ExternalOutput").ap()
        o_st = nc.dram_tensor("o_st", [D, T], BF16,
                              kind="ExternalOutput").ap()
        o_fe = nc.dram_tensor("o_fe", [128, NCH * T], BF16,
                              kind="ExternalOutput").ap()
        o_h = nc.dram_tensor("o_h", [128, 32768], BF16,
                             kind="ExternalOutput").ap()
        o_kg = nc.dram_tensor("o_kg", [4 * D, T], BF16,
                              kind="ExternalOutput").ap()
        o_vg = nc.dram_tensor("o_vg", [4 * T, 2080], BF16,
                              kind="ExternalOutput").ap()
        o_w1t = nc.dram_tensor("o_w1t", [D, DFF], BF16,
                               kind="ExternalOutput").ap()
        o_w2t = nc.dram_tensor("o_w2t", [DFF, D], BF16,
                               kind="ExternalOutput").ap()
        o_sum = nc.dram_tensor("o_sum", [1, 2], F32, kind="ExternalOutput").ap()
        o_q = nc.dram_tensor("o_q", [D, T], BF16,
                             kind="ExternalOutput").ap()

    d_k_in = nc.dram_tensor("d_k_in", [D, T], BF16).ap()
    d_k_g = [nc.dram_tensor(f"d_k{c}", [4 * 512, T], BF16).ap()
             for c in range(4)]
    d_v_in = nc.dram_tensor("d_v_in", [T, 2080], BF16).ap()
    d_v = nc.dram_tensor("d_v", [4 * T, 2080], BF16).ap()
    d_w1t_in = nc.dram_tensor("d_w1t_in", [256, DFF], BF16).ap()
    d_w1t = nc.dram_tensor("d_w1t", [D, DFF], BF16, addr_space="Shared").ap()
    d_w2t_in = nc.dram_tensor("d_w2t_in", [1024, D], BF16).ap()
    d_w2t = nc.dram_tensor("d_w2t", [DFF, D], BF16, addr_space="Shared").ap()
    d_q = nc.dram_tensor("d_q", [D, T], BF16).ap()
    d_s = nc.dram_tensor("d_s", [D, T], BF16).ap()
    d_sum_in = nc.dram_tensor("d_sum_in", [1, 2], F32).ap()
    d_sum = nc.dram_tensor("d_sum", [1, 2], F32, addr_space="Shared").ap()

    from contextlib import ExitStack
    ctx = ExitStack()
    with ctx:
        def sbt(name, shape, dt):
            return ctx.enter_context(nc.sbuf_tensor(name, shape, dt))

        def pst(name, shape, dt):
            return ctx.enter_context(nc.psum_tensor(name, shape, dt))

        xnbuf = sbt("xnbuf", [128, NCH * T], BF16)
        q2 = sbt("q2", [128, 2048], BF16)
        hbuf = sbt("hbuf", [128, 60480], BF16)
        wbufA = sbt("wbufA", [128, 6144], BF16)
        wbufB = sbt("wbufB", [128, 6144], BF16)
        fa = sbt("fa", [128, 2048], F32)
        fb = sbt("fb", [128, 2048], F32)


        s_bq = sbt("s_bq", [128, 16], F32)
        s_bk = sbt("s_bk", [128, 16], F32)
        s_bo = sbt("s_bo", [128, 16], F32)
        s_b1 = sbt("s_b1", [128, 64], F32)
        s_b2 = sbt("s_b2", [128, 16], F32)
        ident = sbt("ident", [128, 128], BF16)
        bvrow = sbt("bvrow", [1, 2048], BF16)
        ones_t = sbt("ones_t", [1, 128], BF16)
        ones1 = sbt("ones1", [1, 128], F32)
        ones128 = sbt("ones128", [128, 1], F32)
        ones64f = sbt("ones64f", [1, 64], F32)
        stat = sbt("stat", [128, 16], F32)
        epst = sbt("epst", [128, 1], F32)
        acc = sbt("acc", [128, 2], F32)
        acct = sbt("acct", [128, 1], F32)
        ssmall = sbt("ssmall", [1, 8], F32)
        inv_bc = sbt("inv_bc", [128, 4], F32)
        recv = wbufB[:, 4096:6144].bitcast(F32)
        recA = recv[0:1, 0:512]
        recB = recv[0:1, 512:1024]
        bcbufA = fa[0:64, 0:512]
        bcbufB = fa[0:64, 512:1024]

        H_XN = 0
        H_KT = 0
        H_VST = 2048
        H_TU, H_TV, H_TT = 0, 2048, 4096
        H_WST = 6144
        H_H = 0
        H_KB = 32768
        H_VB = 36864
        H_EA = 41024
        H_FE = 44096
        H_PBF = 44096

        P_a = pst("P_a", [128, 1024], F32)
        P_b = pst("P_b", [128, 1024], F32)
        P_c = pst("P_c", [128, 512], F32)
        P_d = pst("P_d", [128, 512], F32)
        P_f = pst("P_f", [128, 1024], BF16)
        P_h = pst("P_h", [128, 512], F32)

        SEM.clear()
        for s in SEMNAMES:
            SEM[s] = ctx.enter_context(nc.semaphore(s))

        P = Plan()
        C = P.ctr

        P.op("gpsimd", lambda e: e.memset(ident[:, :], 0.0))
        v_id = P.op("gpsimd", lambda e: e.affine_select(
            out=ident[:, :], in_=ident[:, :], compare_op=ALU.not_equal,
            fill=1.0, base=0, pattern=[[-1, 128]], channel_multiplier=1),
            inc="gp")
        P.op("gpsimd", lambda e: e.memset(ones1[:, :], 1.0))
        P.op("gpsimd", lambda e: e.memset(ones_t[:, :], 1.0))
        v_bvrow = P.dma("gpsimd", bvrow[:, :], bv_row_d[:, :])
        P.op("gpsimd", lambda e: e.memset(ones128[:, :], 1.0))
        P.op("gpsimd", lambda e: e.memset(ones64f[:, :], 1.0))
        v_ones = P.op("gpsimd", lambda e: e.memset(epst[:, :], 1e-5), inc="gp")

        toks_bias = [P.dma("sync", s_bq[:, :], bq_d[:, :]),
                     P.dma("sync", s_bk[:, :], bk_d[:, :]),
                     P.dma("sync", s_bo[:, :], bo_d[:, :]),
                     P.dma("sync", s_b1[:, :], b1_d[:, :]),
                     P.dma("sync", s_b2[:, :], b2_d[:, :])]

        xnbf = hbuf[:, H_XN:H_XN + 2048]
        xnbf_free = ("mm", 0)
        pf_evac = {0: 0, 1: 0}
        v_tph = 0
        xin_free = {0: ("vec", 0)}
        stat_free = {}
        for i in range(8):
            xin = fa[:, :]
            sb0 = (i % 2) * 8
            P.wait("sync", xin_free[0][0], xin_free[0][1])
            vl = P.dma("sync", xin, x_d[i * 128:(i + 1) * 128, :])
            P.wait("vector", vl)
            P.wait("vector", "act", stat_free.get(i % 2, 0))
            P.op("vector", lambda e, xin=xin, sb0=sb0: e.reduce_sum(
                stat[:, sb0:sb0 + 1], xin, axis=AX.X, negate=True), inc="vec")
            P.wait("vector", "vec", C["vec"])
            v_nm = P.op("vector", lambda e, sb0=sb0: e.tensor_scalar_mul(
                stat[:, sb0 + 1:sb0 + 2], stat[:, sb0:sb0 + 1],
                1.0 / 2048.0), inc="vec")
            P.wait("scalar", "vec", v_nm)
            P.op("scalar", lambda e, xin=xin, sb0=sb0: e.activation(
                out=fb[:, :], in_=xin, func=AF.Identity,
                bias=stat[:, sb0 + 1:sb0 + 2], scale=1.0), inc="act")
            P.wait("scalar", "act", C["act"])
            v_sqr = P.op("scalar", lambda e, xin=xin, sb0=sb0: e.activation(
                out=xin, in_=fb[:, :], func=AF.Square,
                accum_out=stat[:, sb0 + 2:sb0 + 3]), inc="act")
            xin_free[0] = ("act", v_sqr)
            P.wait("scalar", "act", C["act"])
            if i == 0:
                P.wait("scalar", "gp", v_ones)
            v_std = P.op("scalar", lambda e, sb0=sb0: e.activation(
                out=stat[:, sb0 + 3:sb0 + 4], in_=stat[:, sb0 + 2:sb0 + 3],
                func=AF.Sqrt, bias=epst[:, 0:1], scale=1.0 / 2048.0), inc="act")
            stat_free[i % 2] = v_std
            P.wait("vector", "act", v_std)
            v_rstd = P.op("vector", lambda e, sb0=sb0: e.reciprocal(
                stat[:, sb0 + 4:sb0 + 5], stat[:, sb0 + 3:sb0 + 4]), inc="vec")
            P.wait("vector", "vec", C["vec"])
            v_rstd2 = P.op("vector", lambda e, sb0=sb0: e.tensor_copy(
                stat[:, sb0 + 5:sb0 + 6], stat[:, sb0 + 4:sb0 + 5]), inc="vec")
            P.wait("scalar", "vec", v_rstd2)
            P.wait("scalar", xnbf_free[0], xnbf_free[1])
            v_xn = P.op("scalar", lambda e, sb0=sb0: e.activation(
                out=xnbf[:, :], in_=fb[:, :], func=AF.Identity,
                scale=stat[:, sb0 + 5:sb0 + 6]), inc="act")
            v_xn_last = v_xn
            P.wait("tensor", "act", v_xn)
            if i == 0:
                P.wait("tensor", "gp", v_id)
            for b4 in range(4):
                half = b4 % 2
                P.wait("tensor", "vec", pf_evac[half])
                for k in range(4):
                    kk = b4 * 4 + k
                    fn = lambda e, kk=kk, k=k, half=half: e.transpose(
                        P_f[:, half * 512 + k * 128: half * 512 + k * 128 + 128],
                        xnbf[:, kk * 128:kk * 128 + 128], ident[:, :])
                    if k == 3:
                        v_tph = P.op("tensor", fn, inc="mm")
                    else:
                        P.op("tensor", fn)
                P.wait("vector", "mm", v_tph)
                out3 = xnbuf[:, b4 * 4 * T: (b4 * 4 + 4) * T].rearrange(
                    "p (c t) -> p c t", t=T)[:, :, i * 128:i * 128 + 128]
                in3 = P_f[:, half * 512: half * 512 + 512].rearrange(
                    "p (c t) -> p c t", t=128)
                vev = P.op(
                    "vector", lambda e, o=out3, ii=in3: e.tensor_copy(o, ii),
                    inc="vec")
                pf_evac[0] = pf_evac[1] = vev
            xnbf_free = ("mm", v_tph)
        fa_free = xin_free[0]
        v_xnT_done = max(pf_evac.values())

        def band_ap(w_ap, jd):
            return w_ap[:, jd * 128:(jd + 1) * 128].rearrange(
                "(i p) d -> p i d", p=128)

        def chunked(sb_ap, inner):
            return sb_ap.rearrange("p (i d) -> p i d", d=inner)

        v_vst_ms = 0
        for tt in range(8):
            v_vst_ms = P.op("gpsimd", lambda e, tt=tt: e.memset(
                hbuf[:, H_VST + tt * 2080: H_VST + (tt + 1) * 2080], 1.0),
                inc="gp")
        v_vev = {}
        H_WVB = 18688
        toks_wv = []
        for q in range(4):
            toks_wv.append(P.dma(
                "gpsimd",
                chunked(hbuf[:, H_WVB + q * 8192: H_WVB + (q + 1) * 8192], 512),
                wv_d[:, q * 512:(q + 1) * 512].rearrange(
                    "(i p) d -> p i d", p=128)))
        vslots = [P_c[:, :], P_d[:, :], P_a[:, 0:512], P_a[:, 512:1024]]
        vpsum_free = {q: ("vec", 0) for q in range(4)}
        v_vmm = 0
        for tt in range(8):
            P.wait("tensor", toks_wv)
            if tt == 0:
                P.wait("tensor", "vec", v_xnT_done)
                P.wait("tensor", v_bvrow)
                P.wait("tensor", "gp", v_ones)
            for q in range(4):
                P.wait("tensor", vpsum_free[q][0], vpsum_free[q][1])
            for i in range(16):
                for q in range(4):
                    fn = (lambda e, i=i, q=q, tt=tt: e.matmul(
                        vslots[q],
                        xnbuf[:, i * T + tt * 128: i * T + tt * 128 + 128],
                        hbuf[:, H_WVB + q * 8192 + i * 512:
                             H_WVB + q * 8192 + (i + 1) * 512],
                        start=(i == 0), stop=False))
                    P.op("tensor", fn)
            for q in range(4):
                fn = (lambda e, q=q: e.matmul(
                    vslots[q], ones_t[0:1, :],
                    bvrow[0:1, q * 512:(q + 1) * 512],
                    start=False, stop=True))
                if q == 3:
                    v_vmm = P.op("tensor", fn, inc="mm")
                else:
                    P.op("tensor", fn)
            P.wait("vector", "mm", v_vmm)
            if tt == 0:
                P.wait("vector", "gp", v_vst_ms)
            vv = 0
            for q in range(4):
                base = H_VST + tt * 2080 + q * 520
                o3 = hbuf[:, base: base + 520].rearrange(
                    "p (h e) -> p h e", e=65)[:, :, 0:64]
                i3 = vslots[q].rearrange("p (h e) -> p h e", e=64)
                vv = P.op("vector", lambda e, o3=o3, i3=i3: e.tensor_copy(
                    o3, i3), inc="vec")
                vpsum_free[q] = ("vec", vv)
            v_vev[(tt, 3)] = vv
        toks_vst = []
        for tt in range(8):
            P.wait("sync", "vec", v_vev[(tt, 3)])
            toks_vst.append(P.dma(
                "sync", d_v_in[tt * 128:(tt + 1) * 128, :],
                hbuf[:, H_VST + tt * 2080: H_VST + (tt + 1) * 2080]))
        P.wait("gpsimd", toks_vst)
        v_agv = P.collective("ccv", "AllGather", ALU.bypass, GROUPS_KV, d_v_in, d_v)

        v_vevac_last = max(vpsum_free[q][1] for q in range(4))
        band_free = {0: ("gp", 0), 1: ("gp", 0)}
        psum_free = {0: ("vec", v_vevac_last), 1: ("vec", v_vevac_last)}
        kst_free = {0: None, 1: None}
        toks_kst = []
        v_kmm_last = 0
        for jd in range(16):
            p = jd % 2
            wb = (wbufA if p == 0 else wbufB)[:, 0:2048]
            P.wait("gpsimd", band_free[p][0], band_free[p][1])
            vw = P.dma("gpsimd", chunked(wb, 128), band_ap(wk_d, jd))
            P.wait("tensor", vw)
            P.wait("tensor", psum_free[p][0], psum_free[p][1])
            P_x = P_a if p == 0 else P_b
            for hh in range(2):
                for i in range(16):
                    fn = lambda e, wb=wb, i=i, hh=hh, P_x=P_x: e.matmul(
                        P_x[:, hh * 512:(hh + 1) * 512],
                        wb[:, i * 128:(i + 1) * 128],
                        xnbuf[:, i * T + hh * 512: i * T + (hh + 1) * 512],
                        start=(i == 0), stop=(i == 15))
                    if hh == 1 and i == 15:
                        v_kmm = P.op("tensor", fn, inc="mm")
                    else:
                        P.op("tensor", fn)
            band_free[p] = ("mm", v_kmm)
            v_kmm_last = v_kmm
            P.wait("scalar", "mm", v_kmm)
            P.wait("scalar", kst_free[p])
            if jd == 0:
                P.wait("scalar", toks_bias)
            if jd < 2:
                P.wait("scalar", "mm", v_tph)
            kts = hbuf[:, H_KT + p * 1024: H_KT + (p + 1) * 1024]
            v_kev = P.op("scalar", lambda e, P_x=P_x, jd=jd, kts=kts: e.activation(
                out=kts, in_=P_x[:, :], func=AF.Identity,
                bias=s_bk[:, jd:jd + 1], scale=1.0), inc="act")
            psum_free[p] = ("act", v_kev)
            P.wait("sync", "act", v_kev)
            v_kst = P.dma("sync", d_k_in[jd * 128:(jd + 1) * 128, :], kts)
            toks_kst.append(v_kst)
            kst_free[p] = v_kst
        v_agk = {}
        for kg in range(4):
            P.wait("gpsimd", toks_kst[kg * 4:(kg + 1) * 4])
            v_agk[kg] = P.collective(
                f"cck{kg}", "AllGather", ALU.bypass, GROUPS_KV,
                d_k_in[kg * 512:(kg + 1) * 512, :],
                d_k_g[kg])


        qband_free = {0: band_free[0], 1: band_free[1]}
        qpsum_free = {0: psum_free[0], 1: psum_free[1]}
        qst_free = {0: kst_free[0], 1: kst_free[1]}
        toks_qst = []
        v_qev = 0
        for jd in range(16):
            p = jd % 2
            wb = (wbufA if p == 0 else wbufB)[:, 0:2048]
            P.wait("gpsimd", qband_free[p][0], qband_free[p][1])
            vw = P.dma("gpsimd", chunked(wb, 128), band_ap(wq_d, jd))
            P.wait("tensor", vw)
            P.wait("tensor", qpsum_free[p][0], qpsum_free[p][1])
            P_x = P_a if p == 0 else P_b
            for hh in range(2):
                for i in range(16):
                    fn = lambda e, wb=wb, i=i, hh=hh, P_x=P_x: e.matmul(
                        P_x[:, hh * 512:(hh + 1) * 512],
                        wb[:, i * 128:(i + 1) * 128],
                        xnbuf[:, i * T + hh * 512: i * T + (hh + 1) * 512],
                        start=(i == 0), stop=(i == 15))
                    if hh == 1 and i == 15:
                        v_qmm = P.op("tensor", fn, inc="mm")
                    else:
                        P.op("tensor", fn)
            qband_free[p] = ("mm", v_qmm)
            P.wait("scalar", "mm", v_qmm)
            P.wait("scalar", qst_free[p])
            qts = hbuf[:, H_KT + p * 1024: H_KT + (p + 1) * 1024]
            v_qev = P.op("scalar", lambda e, P_x=P_x, jd=jd, qts=qts: e.activation(
                out=qts, in_=P_x[:, :],
                func=AF.Identity, bias=s_bq[:, jd:jd + 1], scale=1.0), inc="act")
            qpsum_free[p] = ("act", v_qev)
            P.wait("sync", "act", v_qev)
            v_qst = P.dma("sync", d_q[jd * 128:(jd + 1) * 128, :], qts)
            toks_qst.append(v_qst)
            qst_free[p] = v_qst

        v_acc0 = P.op("vector", lambda e: e.memset(acc[:, :], 0.0), inc="vec")
        fa_free = xin_free[0]
        fb_free = ("act", v_xn_last)
        for which, src_ap, ntile, ng, tw in ((0, w1s_d, 8, 8, 256),
                                             (1, w2s_d, 8, 2, 1024)):
            rows = ng * 128
            for k in range(ntile):
                buf, free = (fa, fa_free) if k % 2 == 0 else (fb, fb_free)
                P.wait("scalar", free[0], free[1])
                vl = P.dma("scalar", buf[:, :].rearrange("p (g c) -> p g c", c=tw),
                           src_ap[k * rows:(k + 1) * rows, :].rearrange(
                               "(g p) c -> p g c", p=128))
                P.wait("vector", vl)
                P.op("vector", lambda e, buf=buf: e.reduce_sum(
                    acct[:, 0:1], buf[:, :], axis=AX.X,
                    apply_absolute_value=True), inc="vec")
                if k % 2 == 0:
                    fa_free = ("vec", C["vec"])
                else:
                    fb_free = ("vec", C["vec"])
                P.wait("vector", "vec", C["vec"])
                P.op("vector", lambda e, w=which: e.tensor_add(
                    acc[:, w:w + 1], acc[:, w:w + 1], acct[:, 0:1]), inc="vec")
        v_abs_done = C["vec"]
        P.wait("tensor", "vec", v_abs_done)
        P.wait("tensor", "vec", vpsum_free[0][1])
        v_sum_mm = P.op("tensor", lambda e: e.matmul(
            P_c[0:1, 0:2], ones128[:, 0:1], acc[:, :], start=True, stop=True),
            inc="mm")
        P.wait("vector", "mm", v_sum_mm)
        v_sum_cp = P.op("vector", lambda e: e.tensor_copy(
            ssmall[0:1, 0:2], P_c[0:1, 0:2]), inc="vec")
        P.wait("sync", "vec", v_sum_cp)
        v_sum_st = P.dma("sync", d_sum_in[:, :], ssmall[0:1, 0:2])
        P.wait("gpsimd", v_sum_st)
        v_ar = P.collective("ccr", "AllReduce", ALU.add, GROUP_ALL, d_sum_in, d_sum)

        P.wait("sync", "ccr", v_ar)
        v_sumld = P.dma("sync", ssmall[0:1, 2:4], d_sum[:, :])
        P.wait("vector", v_sumld)
        P.op("vector", lambda e: e.tensor_scalar(
            out=ssmall[0:1, 4:6], in0=ssmall[0:1, 2:4], scalar1=1.0 / NMAT,
            scalar2=1e-8, op0=ALU.mult, op1=ALU.max), inc="vec")
        P.wait("vector", "vec", C["vec"])
        v_inv = P.op("vector", lambda e: e.reciprocal(
            ssmall[0:1, 6:8], ssmall[0:1, 4:6]), inc="vec")
        P.wait("vector", "vec", C["vec"])
        P.wait("tensor", "vec", v_inv)
        P.wait("tensor", vpsum_free[1][0], vpsum_free[1][1])
        v_invb = P.op("tensor", lambda e: e.matmul(
            P_d[:, 0:2], ones1[0:1, :], ssmall[0:1, 6:8],
            start=True, stop=True), inc="mm")
        P.wait("vector", "mm", v_invb)
        P.op("vector", lambda e: e.tensor_scalar_mul(
            inv_bc[:, 0:2], P_d[:, 0:2], BIG), inc="vec")
        v_invbc = P.op("vector", lambda e: e.tensor_scalar_mul(
            inv_bc[:, 2:4], P_d[:, 0:2], -BIG), inc="vec")

        v_wst = {}
        hp2 = [0]
        pf_free = {0: ("vec", pf_evac[0]), 1: ("vec", pf_evac[1])}
        tern_free = ("mm", 0)
        v_w1std = v_w2std = 0
        for which, src_ap, ntile, ng, tw in ((0, w1s_d, 8, 8, 256),
                                             (1, w2s_d, 8, 2, 1024)):
            rows = ng * 128
            wide = ng * tw
            ntp = tw // 128
            v_tp = 0
            for k in range(ntile):
                P.wait("scalar", fa_free[0], fa_free[1])
                vl = P.dma("scalar", fa[:, :].rearrange("p (g c) -> p g c", c=tw),
                           src_ap[k * rows:(k + 1) * rows, :].rearrange(
                               "(g p) c -> p g c", p=128))
                P.wait("vector", vl)
                P.wait("vector", "vec", v_invbc)
                if which == 0 and k == 0:
                    P.wait("vector", toks_vst)
                    P.wait("vector", toks_kst)
                    P.wait("vector", "act", v_xn_last)
                P.wait("vector", tern_free[0], tern_free[1])
                m = which
                P.op("vector", lambda e, m=m: e.tensor_scalar(
                    out=fb[:, :], in0=fa[:, :], scalar1=inv_bc[:, m:m + 1],
                    scalar2=-HALFBIG, op0=ALU.mult, op1=ALU.add), inc="vec")
                P.wait("vector", "vec", C["vec"])
                P.op("vector", lambda e: e.tensor_scalar(
                    out=hbuf[:, H_TU:H_TU + 2048], in0=fb[:, :], scalar1=0.0,
                    scalar2=1.0, op0=ALU.max, op1=ALU.min), inc="vec")
                P.wait("vector", "vec", C["vec"])
                v_u2 = P.op("vector", lambda e, m=m: e.tensor_scalar(
                    out=fb[:, :], in0=fa[:, :],
                    scalar1=inv_bc[:, 2 + m:3 + m],
                    scalar2=-HALFBIG, op0=ALU.mult, op1=ALU.add), inc="vec")
                fa_free = ("vec", v_u2)
                P.wait("vector", "vec", C["vec"])
                P.op("vector", lambda e: e.tensor_scalar(
                    out=hbuf[:, H_TV:H_TV + 2048], in0=fb[:, :], scalar1=0.0,
                    scalar2=1.0, op0=ALU.max, op1=ALU.min), inc="vec")
                P.wait("vector", "vec", C["vec"])
                v_tt = P.op("vector", lambda e: e.tensor_tensor(
                    out=hbuf[:, H_TT:H_TT + 2048], in0=hbuf[:, H_TU:H_TU + 2048],
                    in1=hbuf[:, H_TV:H_TV + 2048], op=ALU.subtract), inc="vec")
                P.wait("tensor", "vec", v_tt)
                cw = 8192 if which == 0 else 2048
                for g in range(ng):
                    kk = k * ng + g
                    nsub = (ntp + 3) // 4
                    for sub in range(nsub):
                        j0 = sub * 4
                        nj = min(4, ntp - j0)
                        half = hp2[0] % 2
                        hp2[0] += 1
                        P.wait("tensor", pf_free[half][0], pf_free[half][1])
                        for jj in range(nj):
                            j = j0 + jj
                            fn = (lambda e, g=g, j=j, jj=jj, half=half, tw=tw:
                                  e.transpose(
                                      P_f[:, half * 512 + jj * 128:
                                          half * 512 + (jj + 1) * 128],
                                      hbuf[:, H_TT + g * tw + j * 128:
                                           H_TT + g * tw + (j + 1) * 128],
                                      ident[:, :]))
                            if jj == nj - 1:
                                v_tp = P.op("tensor", fn, inc="mm")
                            else:
                                P.op("tensor", fn)
                        P.wait("vector", "mm", v_tp)
                        if which == 1 and kk == 0 and sub == 0:
                            P.wait("vector", toks_w1st)
                        o3 = hbuf[:, H_WST:H_WST + ntp * cw].rearrange(
                            "p (j c) -> p j c", c=cw)[
                                :, j0:j0 + nj, kk * 128:kk * 128 + 128]
                        i3 = P_f[:, half * 512: half * 512 + nj * 128].rearrange(
                            "p (j c) -> p j c", c=128)
                        vev = ("vec", P.op(
                            "vector", lambda e, o3=o3, i3=i3: e.tensor_copy(
                                o3, i3), inc="vec"))
                        pf_free[0] = pf_free[1] = vev
                tern_free = ("mm", v_tp)
            nrow = ntp
            cw = 8192 if which == 0 else 2048
            dst = d_w1t_in if which == 0 else d_w2t_in
            toks_std = []
            for j in range(nrow):
                P.wait("scalar", "vec", pf_free[0][1])
                P.wait("scalar", "vec", pf_free[1][1])
                toks_std.append(P.dma(
                    "scalar", dst[j * 128:(j + 1) * 128, :],
                    hbuf[:, H_WST + j * cw:H_WST + (j + 1) * cw]))
            if which == 0:
                toks_w1st = toks_std
                P.wait("gpsimd", toks_w1st)
                v_agw1 = P.collective("cw1", "AllGather", ALU.bypass, GROUP_ALL,
                                      d_w1t_in, d_w1t)
            else:
                toks_w2st = toks_std
                P.wait("gpsimd", toks_w2st)
                v_agw2 = P.collective("cw2", "AllGather", ALU.bypass, GROUP_ALL,
                                      d_w2t_in, d_w2t)

        kvld = {}
        feats_evac = {}
        feats_mm = {}
        exp_ms2 = {}
        bc_copy = {}
        v_feats_evac_last = 0
        v_fb_last0 = C["vec"]

        F1_START = 2 * 64
        f1 = {"emitted": 0, "slot": 0, "v_hmm": {}, "v_hev": {},
              "w1_free": {0: qband_free[0], 1: qband_free[1]}, "toks": {}}

        def f1_load(sb_):
            par = sb_ % 2
            wb = (wbufA if par == 0 else wbufB)[:, 0:4096]
            P.wait("gpsimd", f1["w1_free"][par][0], f1["w1_free"][par][1])
            if sb_ < 2:
                P.wait("gpsimd", "cw1", 1)
            f1["toks"][sb_] = P.dma(
                "gpsimd", wb.rearrange("p (i c) -> p i c", c=256),
                d_w1t[:, sb_ * 256:(sb_ + 1) * 256].rearrange(
                    "(i p) c -> p i c", p=128))

        def f1_emit_unit():
            u = f1["emitted"]
            if u >= 1024:
                return
            f1["emitted"] += 1
            b, i = u // 16, u % 16
            sb_ = b // 2
            l = b % 2
            par = sb_ % 2
            wb = (wbufA if par == 0 else wbufB)[:, 0:4096]
            if i == 0:
                P.wait("tensor", f1["toks"][sb_])
                if b > 0:
                    P.wait("tensor", "vec", f1["v_hev"][b - 1])
            fn = (lambda e, wb=wb, i=i, l=l: e.matmul(
                P_h[:, :], wb[:, i * 256 + l * 128: i * 256 + (l + 1) * 128],
                xnbuf[:, i * T: i * T + 512],
                start=(i == 0), stop=(i == 15)))
            if i < 15:
                P.op("tensor", fn)
            else:
                f1["v_hmm"][b] = P.op("tensor", fn, inc="mm")
                f1["w1_free"][par] = ("mm", f1["v_hmm"][b])
                if l == 1 and sb_ + 2 < 32:
                    f1_load(sb_ + 2)
                P.wait("vector", "mm", f1["v_hmm"][b])
                if b == 0:
                    P.wait("vector", toks_w1st)
                    P.wait("vector", toks_w2st)
                    P.wait("vector", toks_vst)
                    P.wait("vector", toks_kst)
                    P.wait("vector", toks_qst)
                    P.wait("vector", "mm", tern_free[1])
                f1["v_hev"][b] = P.op(
                    "vector", lambda e, b=b: e.tensor_scalar_add(
                        hbuf[:, H_H + b * 512:H_H + (b + 1) * 512],
                        P_h[:, :], s_b1[:, b:b + 1]), inc="vec")

        def f1_slot():
            s = f1["slot"]
            f1["slot"] += 1
            if s < F1_START:
                return
            target = min(1024, ((s - F1_START + 1) * 1024 + 767) // 768)
            while f1["emitted"] < target:
                f1_emit_unit()

        f1_load(0)
        f1_load(1)

        for j in range(PAIRS):
            kb = hbuf[:, H_KB: H_KB + 4096]
            vb = hbuf[:, H_VB: H_VB + 4160]
            qp = j % 2
            qsl = q2[:, qp * 1024:(qp + 1) * 1024]
            if j >= 1:
                P.wait("sync", "mm", feats_mm[(j - 1, 1)])
            if j < 2:
                P.wait("sync", "ccv", v_agv)
                P.wait("sync", toks_qst)
            if j % 4 == 0:
                P.wait("sync", f"cck{j // 4}", v_agk[j // 4])
            toks_kv = [P.dma("sync", qsl, d_q[j * 128:(j + 1) * 128, :])]
            jj = j % 4
            for r in range(4):
                toks_kv.append(P.dma(
                    "sync", kb[:, r * 1024:(r + 1) * 1024],
                    d_k_g[j // 4][r * 512 + jj * 128: r * 512 + (jj + 1) * 128,
                                  :]))
            for r in range(4):
                toks_kv.append(P.dma(
                    "sync", chunked(vb[:, r * 1040:(r + 1) * 1040], 130),
                    d_v[r * 1024:(r + 1) * 1024,
                        130 * j:130 * (j + 1)].rearrange(
                            "(c p) m -> p c m", p=128)))
            kvld[j] = toks_kv

            for qt in range(2):
                exp_ms = {}
                mmg = {}
                mm2g = {}
                q0 = qt * 512
                expb = [hbuf[:, H_EA + par * 1024: H_EA + (par + 1) * 1024]
                        for par in range(3)]

                def mm2_pair(kc):
                    par = kc % 3
                    for (head, Pfe) in ((0, P_c), (1, P_d)):
                        cbase = kc * 130 + head * 65
                        fn = (lambda e, vb=vb, cbase=cbase, Pfe=Pfe,
                              par=par, head=head, kc=kc: e.matmul(
                                  Pfe[0:65, :],
                                  vb[:, cbase:cbase + 65],
                                  expb[par][:, head * 512:(head + 1) * 512],
                                  start=(kc == 0), stop=(kc == 31)))
                        yield fn

                for kc in range(32):
                    par = kc % 2
                    Psc = P_a if par == 0 else P_b
                    if kc == 0:
                        if qt == 0:
                            P.wait("tensor", kvld[j])
                            if j == 0:
                                P.wait("tensor", "act", v_qev)
                        if (j, qt) != (0, 0):
                            prev = (j, 0) if qt == 1 else (j - 1, 1)
                            P.wait("tensor", "act", exp_ms2.get(prev, 0))
                    else:
                        P.wait("tensor", "act", exp_ms.get(kc - 2, 0))
                    v_g = 0
                    for head, rl, rh in ((0, 0, 64), (1, 64, 128)):
                        fn = (lambda e, kb=kb, kc=kc, Psc=Psc, rl=rl, rh=rh,
                              qsl=qsl, q0=q0, head=head: e.matmul(
                                  Psc[:, head * 512:(head + 1) * 512],
                                  kb[rl:rh, kc * 128:(kc + 1) * 128],
                                  qsl[rl:rh, q0: q0 + 512],
                                  start=True, stop=True,
                                  tile_position=(rl, 0)))
                        if head == 1:
                            v_g = P.op("tensor", fn, inc="mm")
                        else:
                            P.op("tensor", fn)
                    mmg[kc] = v_g
                    if kc >= 2:
                        if kc == 2:
                            prev = (j, 0) if qt == 1 else (j - 1, 1)
                            if (j, qt) != (0, 0):
                                P.wait("tensor", "vec", feats_evac.get(prev, 0))
                        fns = list(mm2_pair(kc - 2))
                        for idx, fn in enumerate(fns):
                            if idx == len(fns) - 1:
                                mm2g[kc - 2] = P.op("tensor", fn, inc="mm")
                            else:
                                P.op("tensor", fn)
                    f1_slot()
                    P.wait("scalar", "mm", mmg[kc])
                    if kc >= 3:
                        P.wait("scalar", "mm", mm2g.get(kc - 3, 0))
                    exp_ms[kc] = P.op("scalar", lambda e, par3=kc % 3, Psc=Psc:
                                      e.activation(out=expb[par3][:, :],
                                                   in_=Psc[:, :], func=AF.Exp,
                                                   scale=SCALE), inc="act")
                exp_ms2[(j, qt)] = exp_ms[31]
                P.wait("tensor", "act", exp_ms[31])
                v_feats = 0
                fns = list(mm2_pair(30)) + list(mm2_pair(31))
                for idx, fn in enumerate(fns):
                    if idx == len(fns) - 1:
                        v_feats = P.op("tensor", fn, inc="mm")
                    else:
                        P.op("tensor", fn)
                P.wait("vector", "mm", v_feats)
                P.op("vector", lambda e: e.reciprocal(
                    recA, P_c[64:65, :]), inc="vec")
                v_rec = P.op("vector", lambda e: e.reciprocal(
                    recB, P_d[64:65, :]), inc="vec")
                P.wait("tensor", "vec", v_rec)
                P.op("tensor", lambda e: e.matmul(
                    P_c[64:128, :], ones64f[0:1, :], recA,
                    start=True, stop=True, tile_position=(0, 64)))
                v_bc = P.op("tensor", lambda e: e.matmul(
                    P_d[64:128, :], ones64f[0:1, :], recB,
                    start=True, stop=True, tile_position=(0, 64)), inc="mm")
                v_bc_last = v_bc
                P.wait("vector", "mm", v_bc)
                if j == 0 and qt == 0:
                    P.wait("vector", toks_w2st)
                fcol = j * T + q0
                P.op("vector", lambda e: e.tensor_copy(
                    bcbufA, P_c[64:128, :]), inc="vec")
                v_bcc = P.op("vector", lambda e: e.tensor_copy(
                    bcbufB, P_d[64:128, :]), inc="vec")
                P.op("vector", lambda e, fcol=fcol: e.tensor_tensor(
                    out=hbuf[0:64, H_FE + fcol:H_FE + fcol + 512],
                    in0=P_c[0:64, :], in1=bcbufA, op=ALU.mult), inc="vec")
                vfe = P.op("vector", lambda e, fcol=fcol: e.tensor_tensor(
                    out=hbuf[64:128, H_FE + fcol:H_FE + fcol + 512],
                    in0=P_d[0:64, :], in1=bcbufB, op=ALU.mult), inc="vec")
                bc_copy[(j, qt)] = v_bcc
                feats_evac[(j, qt)] = vfe
                feats_mm[(j, qt)] = v_feats
                v_feats_evac_last = vfe
        while f1["emitted"] < 1024:
            f1_emit_unit()
        P.wait("scalar", "vec", f1["v_hev"][63])
        v_h_done_th0 = 0
        for cgel in range(4):
            v_h_done_th0 = P.op("scalar", lambda e, cgel=cgel: e.activation(
                out=hbuf[:, cgel * 8192:(cgel + 1) * 8192],
                in_=hbuf[:, cgel * 8192:(cgel + 1) * 8192],
                func=AF.Gelu, scale=1.0), inc="act")

        v_dbg_fe = None
        if DEBUG:
            P.wait("sync", "vec", v_feats_evac_last)
            v_dbg_fe = [P.dma("sync", o_fe[:, :], hbuf[:, H_FE:H_FE + NCH * T])]

        wo_band_free = {0: f1["w1_free"][0], 1: f1["w1_free"][1]}
        wo_psum_free = {0: ("vec", v_feats_evac_last), 1: ("vec", v_feats_evac_last)}
        sst_free = {0: None, 1: None}
        toks_sst = []
        v_woev = {}
        v_womm_last = 0
        for jd in range(16):
            p = jd % 2
            wb = (wbufA if p == 0 else wbufB)[:, 0:2048]
            P.wait("gpsimd", wo_band_free[p][0], wo_band_free[p][1])
            vw = P.dma("gpsimd", chunked(wb, 128), band_ap(wo_d, jd))
            P.wait("tensor", vw)
            P.wait("tensor", wo_psum_free[p][0], wo_psum_free[p][1])
            if jd == 0:
                P.wait("tensor", "vec", v_feats_evac_last)
                if DEBUG:
                    P.wait("tensor", v_dbg_fe)
            P_x = P_a if p == 0 else P_b
            for hh in range(2):
                for i in range(16):
                    fe = H_FE + i * T + hh * 512
                    fn = lambda e, wb=wb, i=i, hh=hh, P_x=P_x, fe=fe: e.matmul(
                        P_x[:, hh * 512:(hh + 1) * 512],
                        wb[:, i * 128:(i + 1) * 128],
                        hbuf[:, fe:fe + 512],
                        start=(i == 0), stop=(i == 15))
                    if hh == 1 and i == 15:
                        v_womm = P.op("tensor", fn, inc="mm")
                    else:
                        P.op("tensor", fn)
            wo_band_free[p] = ("mm", v_womm)
            v_womm_last = v_womm
            P.wait("scalar", "mm", v_womm)
            P.wait("scalar", sst_free[p])
            if jd < 2:
                P.wait("scalar", "mm", feats_mm[(PAIRS - 1, 1)])
            sts = q2[:, p * 1024:(p + 1) * 1024]
            vv = P.op("scalar", lambda e, P_x=P_x, jd=jd, sts=sts: e.activation(
                out=sts, in_=P_x[:, :],
                func=AF.Identity, bias=s_bo[:, jd:jd + 1], scale=1.0), inc="act")
            v_woev[jd] = vv
            wo_psum_free[p] = ("act", vv)
            P.wait("sync", "act", vv)
            v_sst = P.dma("sync", d_s[jd * 128:(jd + 1) * 128, :], sts)
            toks_sst.append(v_sst)
            sst_free[p] = v_sst
        v_woev_last = v_woev[15]

        w1_free = {0: ("mm", 0), 1: ("mm", 0)}
        h_psum_free = {0: ("act", 0), 1: ("act", 0)}
        f2_psum_free = ("act", 0)
        fa_slot_free = {sl: ("vec", v_feats_evac_last) for sl in range(4)}
        s_tile_free = {sl: None for sl in range(4)}
        fb_slot_free = {sl: ("vec", v_feats_evac_last) for sl in range(4)}
        pbf_free = {0: ("mm", 0), 1: ("mm", 0)}
        pf2_free = {0: pf_free[0], 1: pf_free[1]}
        fout_free = {0: None, 1: None}
        w2_free = {0: ("mm", v_bc_last), 1: ("mm", v_bc_last)}
        v_out_dma = 0
        fout = hbuf[:, H_KB:H_KB + 2048].bitcast(F32)
        w1_free = dict(f1["w1_free"])
        v_f2_last_th0 = 0
        for th in range(2):
            t0 = th * 512
            if th == 0:
                v_h_done = v_h_done_th0
            else:
                v_hmm = {}
                v_gelu = {}
                for sb_ in range(32):
                    p = sb_ % 2
                    wb = (wbufA if p == 0 else wbufB)[:, 0:4096]
                    qeng = "sync" if sb_ % 2 == 0 else "gpsimd"
                    P.wait(qeng, w1_free[p][0], w1_free[p][1])
                    if sb_ < 2:
                        P.wait(qeng, "mm", v_womm_last)
                    toks_w1l = [P.dma(
                        qeng,
                        wb.rearrange("p (i c) -> p i c", c=256),
                        d_w1t[:, sb_ * 256:(sb_ + 1) * 256].rearrange(
                            "(i p) c -> p i c", p=128))]
                    for l in range(2):
                        bb = sb_ * 2 + l
                        P.wait("tensor", toks_w1l)
                        if bb == 0:
                            P.wait("tensor", "vec", f1["v_hev"][63])
                            P.wait("tensor", "mm", v_f2_last_th0)
                        else:
                            P.wait("tensor", "act", v_gelu[bb - 1])
                        for i in range(16):
                            fn = (lambda e, wb=wb, i=i, l=l, t0=t0: e.matmul(
                                P_h[:, :],
                                wb[:, i * 256 + l * 128:
                                   i * 256 + (l + 1) * 128],
                                xnbuf[:, i * T + t0: i * T + t0 + 512],
                                start=(i == 0), stop=(i == 15)))
                            if i == 15:
                                v_hmm[bb] = P.op("tensor", fn, inc="mm")
                            else:
                                P.op("tensor", fn)
                        P.wait("scalar", "mm", v_hmm[bb])
                        if DEBUG and bb == 0:
                            P.wait("scalar", v_dbg_h)
                        if bb == 0:
                            P.wait("scalar", "mm", v_f2_last_th0)
                        v_gelu[bb] = P.op(
                            "scalar", lambda e, bb=bb: e.activation(
                                out=hbuf[:, H_H + bb * 512:
                                         H_H + (bb + 1) * 512],
                                in_=P_h[:, :], func=AF.Gelu,
                                bias=s_b1[:, bb:bb + 1], scale=1.0), inc="act")
                    w1_free[p] = ("mm", v_hmm[sb_ * 2 + 1])
                v_h_done = v_gelu[63]
            if DEBUG and th == 0:
                P.wait("sync", "act", v_h_done)
                v_dbg_h = P.dma("sync", o_h[:, :], hbuf[:, H_H:H_H + 32768])
            f2slots = [P_a[:, 0:512], P_a[:, 512:1024],
                       P_b[:, 0:512], P_b[:, 512:1024]]
            for g in range(4):
                v_f2 = {}
                for fg in range(16):
                    p = fg % 2
                    w2b = (wbufA if p == 0 else wbufB)[:, 4096:6144]
                    P.wait("sync" if p == 0 else "gpsimd",
                           w2_free[p][0], w2_free[p][1])
                    if th == 0 and g == 0 and fg < 2:
                        P.wait("sync" if p == 0 else "gpsimd", "cw2", 1)
                    vld = P.dma(
                        "sync" if p == 0 else "gpsimd",
                        w2b.rearrange("p (f c) -> p f c", c=512),
                        d_w2t[fg * 512:(fg + 1) * 512,
                              g * 512:(g + 1) * 512].rearrange(
                                  "(f p) c -> p f c", p=128))
                    P.wait("tensor", vld)
                    if fg == 0:
                        P.wait("tensor", "act", v_h_done)
                        P.wait("tensor", f2_psum_free[0], f2_psum_free[1])
                        if th == 0 and g == 0:
                            P.wait("tensor", "act", v_woev_last)
                    for fl in range(4):
                        fbk = fg * 4 + fl
                        for sl in range(4):
                            fn = (lambda e, w2b=w2b, sl=sl, fl=fl, fbk=fbk:
                                  e.matmul(
                                      f2slots[sl],
                                      w2b[:, fl * 512 + sl * 128:
                                          fl * 512 + (sl + 1) * 128],
                                      hbuf[:, H_H + fbk * 512:
                                           H_H + (fbk + 1) * 512],
                                      start=(fbk == 0), stop=(fbk == 63)))
                            if sl == 3 and fl == 3:
                                v_f2[fg] = P.op("tensor", fn, inc="mm")
                            else:
                                P.op("tensor", fn)
                    w2_free[p] = ("mm", v_f2[fg])
                    if th == 0:
                        v_f2_last_th0 = v_f2[fg]
                v_last_evac = 0
                for sl in range(4):
                    jd = g * 4 + sl
                    ppar = jd % 2
                    P.wait("scalar", "mm", v_f2[15])
                    P.wait("scalar", fa_slot_free[sl][0], fa_slot_free[sl][1])
                    v_silu = P.op("scalar", lambda e, sl=sl, jd=jd: e.activation(
                        out=fa[:, sl * 512:(sl + 1) * 512], in_=f2slots[sl],
                        func=AF.Silu, bias=s_b2[:, jd:jd + 1], scale=1.0),
                        inc="act")
                    v_last_evac = v_silu
                    squ = q2[:, sl * 512:(sl + 1) * 512]
                    P.wait("sync", s_tile_free[sl])
                    if th == 0 and g == 0 and sl < 2:
                        P.wait("sync", toks_sst)
                    v_sld = P.dma("sync", squ,
                                  d_s[jd * 128:(jd + 1) * 128, t0:t0 + 512])
                    P.wait("vector", "act", v_silu)
                    P.wait("vector", v_sld)
                    P.wait("vector", pbf_free[ppar][0], pbf_free[ppar][1])
                    v_p = P.op("vector", lambda e, sl=sl, ppar=ppar, squ=squ:
                               e.tensor_tensor(
                                   out=hbuf[:, H_PBF + ppar * 512:
                                            H_PBF + (ppar + 1) * 512],
                                   in0=fa[:, sl * 512:(sl + 1) * 512],
                                   in1=squ,
                                   op=ALU.mult), inc="vec")
                    fa_slot_free[sl] = ("vec", v_p)
                    s_tile_free[sl] = ("vec", v_p)
                    P.wait("tensor", "vec", v_p)
                    P.wait("tensor", pf2_free[ppar][0], pf2_free[ppar][1])
                    v_tp5 = 0
                    for s4 in range(4):
                        fn = (lambda e, s4=s4, ppar=ppar: e.transpose(
                            P_f[:, ppar * 512 + s4 * 128:
                                ppar * 512 + (s4 + 1) * 128],
                            hbuf[:, H_PBF + ppar * 512 + s4 * 128:
                                 H_PBF + ppar * 512 + (s4 + 1) * 128],
                            ident[:, :]))
                        if s4 == 3:
                            v_tp5 = P.op("tensor", fn, inc="mm")
                        else:
                            P.op("tensor", fn)
                    pbf_free[ppar] = ("mm", v_tp5)
                    P.wait("sync", fb_slot_free[sl][0], fb_slot_free[sl][1])
                    vxl = P.dma("sync",
                                chunked(fb[:, sl * 512:(sl + 1) * 512], 128),
                                x_d[th * 512:(th + 1) * 512,
                                    jd * 128:(jd + 1) * 128].rearrange(
                                        "(s p) d -> p s d", p=128))
                    P.wait("vector", "mm", v_tp5)
                    P.wait("vector", vxl)
                    P.wait("vector", fout_free[ppar])
                    v_ot = P.op("vector", lambda e, sl=sl, ppar=ppar:
                                e.tensor_tensor(
                                    out=fout[:, ppar * 512:(ppar + 1) * 512],
                                    in0=P_f[:, ppar * 512:ppar * 512 + 512],
                                    in1=fb[:, sl * 512:(sl + 1) * 512],
                                    op=ALU.add), inc="vec")
                    pf2_free[0] = pf2_free[1] = ("vec", v_ot)
                    fb_slot_free[sl] = ("vec", v_ot)
                    P.wait("sync", "vec", v_ot)
                    v_out_dma = P.dma(
                        "sync",
                        out_d[th * 512:(th + 1) * 512,
                              jd * 128:(jd + 1) * 128].rearrange(
                                  "(s p) d -> p s d", p=128),
                        chunked(fout[:, ppar * 512:(ppar + 1) * 512], 128))
                    fout_free[ppar] = v_out_dma
                f2_psum_free = ("act", v_last_evac)

        if DEBUG:
            P.wait("sync", "vec", C["vec"])
            P.wait("sync", "act", C["act"])
            P.dma("sync", o_xn[:, :], xnbuf[:, :])
            for r in range(16):
                P.dma("sync", o_st[r * 128:(r + 1) * 128, :],
                      d_s[r * 128:(r + 1) * 128, :])
                P.dma("sync", o_q[r * 128:(r + 1) * 128, :],
                      d_q[r * 128:(r + 1) * 128, :])
            P.dma("sync", o_sum[:, :], ssmall[0:1, 2:4])
            P.dma("sync", o_kg[:, :], d_k[:, :])
            P.dma("sync", o_vg[:, :], d_v[:, :])
            for r in range(16):
                P.dma("sync", o_w1t[r * 128:(r + 1) * 128, :],
                      d_w1t[r * 128:(r + 1) * 128, :])
            for r in range(64):
                P.dma("sync", o_w2t[r * 128:(r + 1) * 128, :],
                      d_w2t[r * 128:(r + 1) * 128, :])

        for s in SQ:
            P.wait("sync", s, C[s])

        with nc.Block() as block:
            @block.sync
            def _(e):
                for fn in P.items["sync"]:
                    fn(e)

            @block.gpsimd
            def _(e):
                for fn in P.items["gpsimd"]:
                    fn(e)

            @block.tensor
            def _(e):
                for fn in P.items["tensor"]:
                    fn(e)

            @block.vector
            def _(e):
                for fn in P.items["vector"]:
                    fn(e)

            @block.scalar
            def _(e):
                for fn in P.items["scalar"]:
                    fn(e)

    return nc


_NC_CACHE = None


def kernel(L0_attn_x, L0_attn_Wq, L0_attn_bq, L0_attn_Wk, L0_attn_bk,
           L0_attn_Wv, L0_attn_bv, L0_attn_Wo, L0_attn_bo,
           L0_ffn_w1, L0_ffn_b1, L0_ffn_w2, L0_ffn_b2):
    global _NC_CACHE
    x = np.ascontiguousarray(np.asarray(L0_attn_x, dtype=np.float32))
    wq = np.ascontiguousarray(np.asarray(L0_attn_Wq, dtype=np.float32))
    wk = np.ascontiguousarray(np.asarray(L0_attn_Wk, dtype=np.float32))
    wv = np.ascontiguousarray(np.asarray(L0_attn_Wv, dtype=np.float32))
    wo = np.ascontiguousarray(np.asarray(L0_attn_Wo, dtype=np.float32))
    w1 = np.ascontiguousarray(np.asarray(L0_ffn_w1, dtype=np.float32))
    w2 = np.ascontiguousarray(np.asarray(L0_ffn_w2, dtype=np.float32))

    def fold(b, n):
        return np.ascontiguousarray(
            np.asarray(b, dtype=np.float32).reshape(n, 128).T)

    bq2 = fold(L0_attn_bq, 16)
    bk2 = fold(L0_attn_bk, 16)
    bo2 = fold(L0_attn_bo, 16)
    b12 = fold(L0_ffn_b1, 64)
    b22 = fold(L0_ffn_b2, 16)
    bvr = np.ascontiguousarray(
        np.asarray(L0_attn_bv, dtype=np.float32).reshape(1, D))

    if _NC_CACHE is None:
        _NC_CACHE = build_program()
    nc = _NC_CACHE

    in_maps = []
    for c in range(8):
        b, sl = c // 4, c % 4
        in_maps.append({
            "x": np.ascontiguousarray(x[b, sl * T:(sl + 1) * T, :]),
            "wq": wq, "wk": wk, "wv": wv, "wo": wo,
            "bq": bq2, "bk": bk2, "bo": bo2, "bv_row": bvr,
            "bo_row": np.ascontiguousarray(
                np.asarray(L0_attn_bo, dtype=np.float32).reshape(1, D)),
            "b1": b12, "b2": b22,
            "w1s": np.ascontiguousarray(w1[:, c * 256:(c + 1) * 256]),
            "w2s": np.ascontiguousarray(w2[:, c * 1024:(c + 1) * 1024]),
        })
    res = run_bass_kernel_spmd(nc, in_maps, core_ids=list(range(8)))
    out = np.empty((2, 4096, D), dtype=np.float32)
    for c in range(8):
        b, sl = c // 4, c % 4
        out[b, sl * T:(sl + 1) * T, :] = res.results[c]["out"]
    return out
